# version 13
# baseline (speedup 1.0000x reference)
import contextlib
import numpy as np
import concourse.bass as bass
import concourse.mybir as mybir
from concourse.bass_utils import run_bass_kernel_spmd

F32 = mybir.dt.float32
BF16 = mybir.dt.bfloat16
AF = mybir.ActivationFunctionType
ALU = mybir.AluOpType

S, D, BSZ, H, HD, NF, KR, VR, DH = 4096, 1024, 4, 16, 64, 32, 3, 192, 4096
TQ = 2048
CNT = (8, 16, 24, 32)
NMSK = 8
EXPB = 12.0
LNEPS = 1e-5
ROPE_BASE = 10000.0
NHG = 4
NTG = 4

ENGS = ("tensor", "vector", "scalar", "gpsimd", "sync")


class Prog:

    def __init__(self):
        self.ops = []
        self.per_eng = {e: [] for e in ENGS}

    def add(self, eng, fn, deps=(), dma=False):
        i = len(self.ops)
        self.ops.append((eng, fn, tuple(d for d in deps if d is not None), dma))
        self.per_eng[eng].append(i)
        return i

    def emit(self, nc, block, es):
        need = set()
        for i, (eng, fn, deps, dma) in enumerate(self.ops):
            if dma:
                need.add(i)
            if eng in ("vector", "scalar"):
                need.add(i)
            for d in deps:
                pe, _, _, pdma = self.ops[d]
                if pdma or pe != eng:
                    need.add(d)
        sems = {e: es.enter_context(nc.semaphore("S" + e)) for e in ENGS}
        for e in ENGS:
            for s in range(3):
                sems[(e, s)] = es.enter_context(nc.semaphore(f"S{e}{s}"))
        val, cnt = {}, {e: 0 for e in ENGS}
        for e in ENGS:
            for i in self.per_eng[e]:
                if i in need and not self.ops[i][3]:
                    cnt[e] += 1
                    val[i] = cnt[e]
        waits = []
        for i, (eng, fn, deps, dma) in enumerate(self.ops):
            w = {}
            for d in deps:
                if d in val and not self.ops[d][3]:
                    pe = self.ops[d][0]
                    if pe != eng:
                        w[pe] = max(w.get(pe, 0), val[d])
            if eng in ("vector", "scalar") and val.get(i, 1) > 1:
                w[eng] = max(w.get(eng, 0), val[i] - 1)
            waits.append(w)
        B, R = 2, 3
        self.stripe_of = {}
        self.stripe_val = {}
        stripe_gate = {}
        for e in ENGS:
            dmas = [i for i in self.per_eng[e] if self.ops[i][3]]
            epochs, cur = [], []
            for i in dmas:
                boundary = len(cur) >= B
                if not boundary and cur:
                    start = cur[0]
                    if any(d >= start for d in self.ops[i][2]):
                        boundary = True
                if boundary and cur:
                    epochs.append(cur)
                    cur = []
                cur.append(i)
            if cur:
                epochs.append(cur)
            stot = [0] * R
            for k, ep in enumerate(epochs):
                s = k % R
                if stot[s] > 0:
                    stripe_gate[ep[0]] = (s, stot[s])
                stot[s] += 16 * len(ep)
                for i in ep:
                    self.stripe_of[i] = (e, s)
                    self.stripe_val[i] = stot[s]
        self._stripe_gate = stripe_gate
        waits2 = []
        for i, (eng, fn, deps, dma) in enumerate(self.ops):
            w = {}
            for pe, v in waits[i].items():
                w[pe] = max(w.get(pe, 0), v)
            for d in deps:
                if d in self.stripe_of:
                    pe, s = self.stripe_of[d]
                    key = (pe, s)
                    w[key] = max(w.get(key, 0), self.stripe_val[d])
            if i in self._stripe_gate:
                s, v = self._stripe_gate[i]
                key = (eng, s)
                w[key] = max(w.get(key, 0), v)
            waits2.append(w)
        waits = waits2
        seen = {e: {} for e in ENGS}
        for e in ENGS:
            for i in self.per_eng[e]:
                w = waits[i]
                for pe in list(w):
                    if seen[e].get(pe, 0) >= w[pe]:
                        del w[pe]
                    else:
                        seen[e][pe] = w[pe]
        for e in ENGS:
            def body(eng, _e=e):
                for i in self.per_eng[_e]:
                    _, fn, _, dma = self.ops[i]
                    for pe, v in sorted(waits[i].items(), key=str):
                        eng.wait_ge(sems[pe], v)
                    ins = fn(eng)
                    if dma:
                        ins.then_inc(sems[self.stripe_of[i]], 16)
                    elif i in val:
                        ins.then_inc(sems[_e], 1)
            getattr(block, e)(body)


class Carver:
    def __init__(self, pool_ap, limit, off=0):
        self.pool = pool_ap
        self.off = off
        self.limit = limit

    def take(self, shape, dt):
        p = shape[0]
        elems = int(np.prod(shape[1:]))
        width = elems * (2 if dt == F32 else 1)
        self.off = (self.off + 1) & ~1
        a = self.pool[0:p, self.off:self.off + width]
        self.off += width
        assert self.off <= self.limit, f"sbuf pool overflow {self.off} > {self.limit}"
        if dt == F32:
            a = a.bitcast(F32)
        if len(shape) == 3:
            a = a.rearrange("p (a b) -> p a b", a=shape[1])
        elif len(shape) == 4:
            a = a.rearrange("p (a b c) -> p a b c", a=shape[1], b=shape[2])
        elif len(shape) == 5:
            a = a.rearrange("p (a b c d) -> p a b c d", a=shape[1], b=shape[2],
                            c=shape[3])
        else:
            assert len(shape) == 2
        return a


def build_nc(dbg=False):
    nc = bass.Bass("TRN2", target_bir_lowering=False,
                   detect_race_conditions=False)

    def din(name, shape):
        return nc.dram_tensor(name, shape, F32, kind="ExternalInput").ap()

    x_own = din("x_own", [TQ, D])
    x_full = din("x_full", [S, D])
    cqd, sqd = din("cq", [128, 16, 32]), din("sq", [128, 16, 32])
    ckd, skd = din("ck", [128, 32, 32]), din("sk", [128, 32, 32])
    masksd = din("masks", [4, NMSK, 128, 512])
    wqd = din("wq", [128, 8, D])
    wkd = din("wk", [128, 8, VR])
    wvd = din("wv", [128, 8, VR])
    d0d, d1d = din("d0", [128, D]), din("d1", [64, D])
    vd0d, vd1d = din("vd0", [128, D]), din("vd1", [64, D])
    wprojd = din("wproj", [128, 8, D])
    w12d = din("w12", [32, 128, 2, 8, 128])
    h1wd = din("h1w", [32, 128, 32, 128])
    h2wd = din("h2w", [32, 128, 32, 128])
    w3wd = din("w3w", [8, 128, 32, 128])
    s1d, s2d = din("s1s", [128, 32]), din("s2s", [128, 32])
    h1sd, h2sd = din("h1s", [128, 32]), din("h2s", [128, 32])
    w3sd = din("w3s", [128, 8])
    identd = din("ident", [128, 128])
    sel33d = din("sel33", [33, 128])
    onesd = din("onesb", [128, 1])

    out = nc.dram_tensor("out", [TQ, D], F32, kind="ExternalOutput").ap()
    x1sp = nc.dram_tensor("x1sp", [TQ, D], F32, kind="ExternalOutput").ap()
    dbg_o = {}
    if dbg:
        for nm, shp in (("dbg_q", [128, 8, TQ]), ("dbg_k", [128, 2, S]),
                        ("dbg_v", [128, 32, 256]), ("dbg_y", [128, 8, TQ]),
                        ("dbg_x1", [TQ, D]), ("dbg_z", [128, S]),
                        ("dbg_a", [128, 32, 1024]), ("dbg_g", [128, 32, 1024]),
                        ("dbg_af", [128, 32, 1024])):
            dbg_o[nm] = nc.dram_tensor(nm, shp, F32, kind="ExternalOutput").ap()

    P = Prog()

    def G(fn, deps=()):
        return P.add("gpsimd", fn, deps, dma=True)

    def SY(fn, deps=()):
        return P.add("sync", fn, deps, dma=True)

    def T(fn, deps=()):
        return P.add("tensor", fn, deps)

    def V_(fn, deps=()):
        return P.add("vector", fn, deps)

    def SC(fn, deps=()):
        return P.add("scalar", fn, deps)

    def ldg(dst, src, deps=()):
        return G(lambda e, a=dst, b=src: e.dma_start(out=a, in_=b), deps)

    def lds(dst, src, deps=()):
        return SY(lambda e, a=dst, b=src: e.dma_start(out=a, in_=b), deps)

    es = contextlib.ExitStack()
    with es:
        POOLW = 106416
        pool = es.enter_context(nc.sbuf_tensor("pool", [128, POOLW], BF16)).ap()
        psum = es.enter_context(nc.psum_tensor("psum", [128, 8, 512], F32)).ap()

        def bank(b):
            return psum[:, b, :]

        def bank2(b):
            return psum[:, b:b + 2, :].rearrange("p a b -> p (a b)")

        def bankb(b):
            return psum[:, b, :].bitcast(BF16)

        C = Carver(pool, POOLW)
        ident_sb = C.take([128, 128], BF16)
        identf_sb = C.take([128, 128], F32)
        eps_sb = C.take([128, 1], F32)
        s1_sb = C.take([128, 32], F32)
        s2_sb = C.take([128, 32], F32)
        h1s_sb = C.take([128, 32], F32)
        h2s_sb = C.take([128, 32], F32)
        w3s_sb = C.take([128, 8], F32)
        col_sb = C.take([128, 8], F32)
        a_live_end = C.off
        ck_sb = C.take([128, 32, 32], F32)
        sk_sb = C.take([128, 32, 32], F32)
        cq_sb = C.take([128, 16, 32], F32)
        sq_sb = C.take([128, 16, 32], F32)
        mask_sb = C.take([128, NMSK, 512], BF16)
        sel_sb = C.take([33, 128], F32)
        ones_sb = C.take([128, 1], BF16)
        bias_sb = C.take([128, 1], F32)
        pt_sb = C.take([128, 4, 1024], BF16)
        sstage_sb = C.take([33, 512], F32)
        lnr_sb = C.take([128, 512], F32)
        rs_sb = C.take([128, 512], F32)
        wk_sb = C.take([128, 8, VR], BF16)
        wv_sb = C.take([128, 8, VR], BF16)
        d0_sb = C.take([128, D], BF16)
        d1_sb = C.take([64, D], BF16)
        vd0_sb = C.take([128, D], BF16)
        vd1_sb = C.take([64, D], BF16)
        a_end = C.off
        qd_sb = C.take([128, 8, TQ], BF16)
        y_sb = C.take([128, 8, TQ], BF16)
        bc_end = C.off
        z0_sb = C.take([128, S], BF16)
        z1_sb = C.take([64, S], BF16)
        xv0_sb = C.take([128, S], BF16)
        xv1_sb = C.take([64, S], BF16)
        kd_off = C.off
        kd_sb = C.take([128, 2, S], BF16)
        v_sb = C.take([128, 32, 256], BF16)
        d_stream = C.off
        wq_sb = Carver(pool, d_stream, off=kd_off).take([128, 8, D], BF16)
        xtm_sb = C.take([128, 2, D], BF16)
        xfm_sb = C.take([128, 2, 8, 128], BF16)
        qtm_sb = C.take([128, 2, D], BF16)
        ktm_sb = C.take([128, 2, 512], BF16)
        rsc1_sb = C.take([128, 512], F32)
        rsc2_sb = C.take([128, 512], F32)
        attn_peak = C.off
        CP = Carver(pool, POOLW, off=kd_off)
        wproj_sb = CP.take([128, 8, D], BF16)
        xo_sb = CP.take([128, 2, D], F32)
        x1p_sb = CP.take([128, 2, D], F32)
        x1_sb = CP.take([128, 2, D], F32)
        CF1 = Carver(pool, a_end, off=a_live_end)
        stat_sb = CF1.take([128, D], F32)
        wbuf_off = CF1.off
        wbuf_sb = CF1.take([128, 3, 32, 128], BF16)
        w12b_off = CF1.off
        w12b_sb = CF1.take([128, 2, 2, 8, 128], BF16)
        sw3_off = CF1.off
        silu_sb = CF1.take([128, 4, 512], F32)
        x1b_sb = Carver(pool, a_end, off=wbuf_off).take([128, 2, D], BF16)
        CF2 = Carver(pool, bc_end, off=a_end)
        g_sb = CF2.take([128, 32, 1024], BF16)
        w3b_sb = Carver(pool, a_end, off=sw3_off).take([128, 32, 128], BF16)
        CF3 = Carver(pool, POOLW, off=bc_end)
        af_sb = CF3.take([128, 32, 1024], BF16)
        x1fm_off = CF3.off
        x1fm_sb = CF3.take([128, 8, 1024], BF16)
        x1f_sb = CF3.take([128, 2, D], F32)
        ffn_peak = CF3.off
        fsub_sb = Carver(pool, POOLW, off=x1fm_off).take([128, 8, 512], F32)
        x2_sb = Carver(pool, a_end,
                       off=wbuf_off + 2 * 4096).take([128, 2, D], F32)
        w3b2_sb = Carver(pool, a_end, off=w12b_off).take([128, 32, 128], BF16)

        ld_wk = ldg(wk_sb, wkd)
        ld_wv = ldg(wv_sb, wvd)
        ld_d0 = ldg(d0_sb, d0d)
        ld_d1 = ldg(d1_sb, d1d)
        ld_vd0 = ldg(vd0_sb, vd0d)
        ld_vd1 = ldg(vd1_sb, vd1d)
        ld_ident = ldg(ident_sb, identd)
        ld_ones = ldg(ones_sb, onesd)
        ld_wq = ldg(wq_sb, wqd)
        ld_ck = lds(ck_sb, ckd)
        ld_sk = lds(sk_sb, skd)
        ld_cq = lds(cq_sb, cqd)
        ld_sq = lds(sq_sb, sqd)
        ld_identf = lds(identf_sb, identd)
        ld_sel = lds(sel_sb, sel33d)
        ld_s1 = lds(s1_sb, s1d)
        ld_s2 = lds(s2_sb, s2d)
        ld_h1s = lds(h1s_sb, h1sd)
        ld_h2s = lds(h2s_sb, h2sd)
        ld_w3s = lds(w3s_sb, w3sd)
        ms_bias = V_(lambda e: e.memset(bias_sb, -EXPB))
        ms_sst = V_(lambda e: e.memset(sstage_sb, 0.0))
        ms_eps = V_(lambda e: e.memset(eps_sb, LNEPS))

        def reim(ap2, nh):
            return ap2.rearrange("p (h c f) -> p h c f", h=nh, c=2)

        neg0 = V_(lambda e: e.tensor_scalar_mul(
            reim(d0_sb[96:128, :], 16)[:, :, 0, :],
            reim(d0_sb[96:128, :], 16)[:, :, 0, :], -1.0), deps=[ld_d0])
        neg1 = V_(lambda e: e.tensor_scalar_mul(
            reim(d1_sb, 16)[:, :, 0, :],
            reim(d1_sb, 16)[:, :, 0, :], -1.0), deps=[ld_d1])

        def transpose_op(dst, src, idn):
            return lambda e, a=dst, b=src, c=idn: e.transpose(a, b, c)

        ldx = [None, None]
        tr_last = [None, None]
        ev_fm = [None, None]
        zx_ev = [None, None]
        mmz = None
        for i1 in range(33):
            if i1 < 32:
                kt = i1
                sl = kt % 2
                ldx[sl] = ldg(xtm_sb[:, sl, :],
                              x_full[kt * 128:(kt + 1) * 128, :],
                              deps=[tr_last[sl]])
                for ic in range(8):
                    o = T(transpose_op(
                        bankb(6 + sl)[:, ic * 128:(ic + 1) * 128],
                        xtm_sb[:, sl, ic * 128:(ic + 1) * 128], ident_sb),
                        deps=([ldx[sl], ld_ident, ev_fm[sl]] if ic == 0 else ()))
                tr_last[sl] = o
                ev_fm[sl] = V_(lambda e, sl=sl: e.tensor_copy(
                    xfm_sb[:, sl, :, :],
                    bankb(6 + sl).rearrange("p (a b) -> p a b", a=8)),
                    deps=[tr_last[sl]])
            if i1 >= 1:
                kt = i1 - 1
                sl = kt % 2
                zx = bank(sl).rearrange("p (a b) -> p a b", a=4)
                for ic in range(8):
                    st_, sp = (ic == 0), (ic == 7)
                    xf = xfm_sb[:, sl, ic, :]
                    dps = [ev_fm[sl], ld_wk, ld_wv, zx_ev[sl]] if ic == 0 else ()
                    T(lambda e, w=wk_sb[:, ic, 0:128], x=xf, s=st_, zx=zx:
                      e.matmul(zx[:, 0, :], w, x, start=s, stop=False), deps=dps)
                    T(lambda e, w=wk_sb[:, ic, 128:192], x=xf, zx=zx:
                      e.matmul(zx[0:64, 1, :], w, x, start=False, stop=False))
                    T(lambda e, w=wv_sb[:, ic, 128:192], x=xf, zx=zx:
                      e.matmul(zx[0:64, 3, :], w, x, start=False, stop=False))
                    mmz = T(lambda e, w=wv_sb[:, ic, 0:128], x=xf, p=sp, zx=zx:
                            e.matmul(zx[:, 2, :], w, x, start=False, stop=p))
                V_(lambda e, kt=kt, zx=zx: e.tensor_copy(
                    z0_sb[:, kt * 128:(kt + 1) * 128], zx[:, 0, :]), deps=[mmz])
                V_(lambda e, kt=kt, zx=zx: e.tensor_copy(
                    z1_sb[:, kt * 128:(kt + 1) * 128], zx[0:64, 1, :]))
                V_(lambda e, kt=kt, zx=zx: e.tensor_copy(
                    xv0_sb[:, kt * 128:(kt + 1) * 128], zx[:, 2, :]))
                zx_ev[sl] = V_(lambda e, kt=kt, zx=zx: e.tensor_copy(
                    xv1_sb[:, kt * 128:(kt + 1) * 128], zx[0:64, 3, :]))
        z_done = zx_ev[31 % 2]


        def rope_ops(ps_tm, nh, ctab, stab, out_tm, deps):
            src, dst = reim(ps_tm, nh), reim(out_tm, nh)
            cb = ctab.unsqueeze(1).broadcast_to([128, nh, 32])
            st = stab.unsqueeze(1).broadcast_to([128, nh, 32])
            r1 = rsc1_sb[:, 0:nh * 32].rearrange("p (h f) -> p h f", h=nh)
            r2 = rsc2_sb[:, 0:nh * 32].rearrange("p (h f) -> p h f", h=nh)
            V_(lambda e: e.tensor_tensor(r1, src[:, :, 0, :], cb, ALU.mult),
               deps=deps)
            V_(lambda e: e.tensor_tensor(r2, src[:, :, 1, :], st, ALU.mult))
            V_(lambda e: e.tensor_sub(dst[:, :, 0, :], r1, r2))
            V_(lambda e: e.tensor_tensor(r1, src[:, :, 0, :], st, ALU.mult))
            V_(lambda e: e.tensor_tensor(r2, src[:, :, 1, :], cb, ALU.mult))
            return V_(lambda e: e.tensor_add(dst[:, :, 1, :], r1, r2))

        ldq = [None, None]
        ev1 = [None, None]
        qd_ev = [None, None]
        rope_done = [None, None]
        qtr = [None, None]
        qmm = [None, None]
        for i2 in range(18):
            if i2 < 16:
                tt = i2
                sl = tt % 2
                ldq[sl] = ldg(xtm_sb[:, sl, :],
                              x_own[tt * 128:(tt + 1) * 128, :],
                              deps=[tr_last[sl]])
                for ic in range(8):
                    o = T(transpose_op(
                        bankb(6 + sl)[:, ic * 128:(ic + 1) * 128],
                        xtm_sb[:, sl, ic * 128:(ic + 1) * 128], ident_sb),
                        deps=([ldq[sl], ev_fm[sl] if tt < 2 else ev1[sl]]
                              if ic == 0 else ()))
                tr_last[sl] = o
                ev1[sl] = V_(lambda e, sl=sl: e.tensor_copy(
                    xfm_sb[:, sl, :, :],
                    bankb(6 + sl).rearrange("p (a b) -> p a b", a=8)),
                    deps=[tr_last[sl]])
            if 1 <= i2 <= 16:
                tt = i2 - 1
                sl = tt % 2
                qb = sl * 2
                for og in range(2):
                    for ic in range(8):
                        dps = ([ev1[sl], ld_wq, rope_done[sl], zx_ev[sl]]
                               if (og == 0 and ic == 0) else ())
                        o = T(lambda e, og=og, ic=ic, sl=sl, qb=qb:
                              e.matmul(bank(qb + og), xfm_sb[:, sl, ic, :],
                                       wq_sb[:, ic, og * 512:(og + 1) * 512],
                                       start=(ic == 0), stop=(ic == 7)), deps=dps)
                qmm[sl] = o
                rope_done[sl] = rope_ops(
                    bank2(qb), 16, cq_sb[:, tt, :], sq_sb[:, tt, :],
                    qtm_sb[:, sl, :],
                    deps=[qmm[sl], ld_cq, ld_sq, qtr[sl]])
            if i2 >= 2:
                tt = i2 - 2
                sl = tt % 2
                for p in range(8):
                    o = T(transpose_op(
                        bankb(4 + sl)[:, p * 128:(p + 1) * 128],
                        qtm_sb[:, sl, p * 128:(p + 1) * 128], ident_sb),
                        deps=([rope_done[sl], qd_ev[sl]] if p == 0 else ()))
                qtr[sl] = o
                qd_ev[sl] = V_(lambda e, sl=sl, tt=tt: e.tensor_copy(
                    qd_sb[:, :, tt * 128:(tt + 1) * 128],
                    bankb(4 + sl).rearrange("p (a b) -> p a b", a=8)),
                    deps=[qtr[sl]])
        q_done = qd_ev[15 % 2]
        q_done2 = qd_ev[14 % 2]

        fs = dict(it=0, exp=[], avs=[], y_ev=[None, None],
                  sstage_rd=None, bc_rd=[None, None], rs_rd=None, unit=0)
        y_final = None
        prev_flash_tail = [q_done, q_done2, rope_done[0], rope_done[1]]

        mask_rd = None
        for hg in range(NHG):
            dcol = slice(hg * 256, hg * 256 + 256)
            ropek = [None, None]
            trk = [None, None]
            evk = [None, None]
            vev = [None, None]
            kmm = [None, None]
            vmm = [None, None]

            def rope_pair(ps2, pi, out2, deps):
                src4 = ps2.rearrange("p t (h c f) -> p t h c f", h=4, c=2)
                dst4 = out2.rearrange("p t (h c f) -> p t h c f", h=4, c=2)
                cb = ck_sb[:, 2 * pi:2 * pi + 2, :].unsqueeze(2) \
                    .broadcast_to([128, 2, 4, 32])
                st = sk_sb[:, 2 * pi:2 * pi + 2, :].unsqueeze(2) \
                    .broadcast_to([128, 2, 4, 32])
                r1 = rsc1_sb[:, 0:256].rearrange("p (t h f) -> p t h f", t=2, h=4)
                r2 = rsc2_sb[:, 0:256].rearrange("p (t h f) -> p t h f", t=2, h=4)
                V_(lambda e: e.tensor_tensor(r1, src4[:, :, :, 0, :], cb,
                                             ALU.mult), deps=deps)
                V_(lambda e: e.tensor_tensor(r2, src4[:, :, :, 1, :], st,
                                             ALU.mult))
                V_(lambda e: e.tensor_sub(dst4[:, :, :, 0, :], r1, r2))
                V_(lambda e: e.tensor_tensor(r1, src4[:, :, :, 0, :], st,
                                             ALU.mult))
                V_(lambda e: e.tensor_tensor(r2, src4[:, :, :, 1, :], cb,
                                             ALU.mult))
                return V_(lambda e: e.tensor_add(dst4[:, :, :, 1, :], r1, r2))

            for pi in range(17):
                if pi < 16:
                    sl = pi % 2
                    b0 = sl * 2
                    for tl in range(2):
                        kt = 2 * pi + tl
                        dps = ([z_done, neg0, neg1, ld_ck, ld_sk, ropek[sl]] +
                               prev_flash_tail if pi == 0 else
                               ([ropek[sl]] if tl == 0 else []))
                        T(lambda e, kt=kt, b=b0 + tl, dc=dcol:
                          e.matmul(bank(b)[:, 0:256],
                                   z0_sb[:, kt * 128:(kt + 1) * 128],
                                   d0_sb[:, dc], start=True, stop=False),
                          deps=dps)
                        kmm[sl] = T(lambda e, kt=kt, b=b0 + tl, dc=dcol:
                                    e.matmul(bank(b)[:, 0:256],
                                             z1_sb[:, kt * 128:(kt + 1) * 128],
                                             d1_sb[:, dc], start=False,
                                             stop=True))
                        vdps = ([z_done, ld_vd0, ld_vd1, vev[kt % 2]] +
                                (prev_flash_tail if pi == 0 else []))
                        vmm[kt % 2] = T(lambda e, kt=kt, dc=dcol:
                                        e.matmul(bank(4 + kt % 2)[:, 0:256],
                                                 xv0_sb[:, kt * 128:(kt + 1) * 128],
                                                 vd0_sb[:, dc], start=True,
                                                 stop=False), deps=vdps)
                        vmm[kt % 2] = T(lambda e, kt=kt, dc=dcol:
                                        e.matmul(bank(4 + kt % 2)[:, 0:256],
                                                 xv1_sb[:, kt * 128:(kt + 1) * 128],
                                                 vd1_sb[:, dc], start=False,
                                                 stop=True))
                    ropek[sl] = rope_pair(
                        psum[:, b0:b0 + 2, 0:256], pi, ktm_sb[:, sl, :]
                        .rearrange("p (t c) -> p t c", t=2),
                        deps=[kmm[sl], trk[sl]])
                    for tl in range(2):
                        kt = 2 * pi + tl
                        vev[kt % 2] = V_(lambda e, kt=kt: e.tensor_copy(
                            v_sb[:, kt, :], bank(4 + kt % 2)[:, 0:256]),
                            deps=[vmm[kt % 2]])
                if pi >= 1:
                    pj = pi - 1
                    sl = pj % 2
                    for p in range(4):
                        o = T(transpose_op(
                            bankb(6 + sl)[:, p * 128:(p + 1) * 128],
                            ktm_sb[:, sl, p * 128:(p + 1) * 128], ident_sb),
                            deps=([ropek[sl], evk[sl]] +
                                  (prev_flash_tail if pj == 0 else [])
                                  if p == 0 else ()))
                    trk[sl] = o
                    evk[sl] = V_(lambda e, sl=sl, pj=pj: e.tensor_copy(
                        kd_sb[:, :, pj * 256:(pj + 1) * 256]
                        .rearrange("p r (t c) -> p r t c", t=2),
                        bankb(6 + sl)[:, 0:512]
                        .rearrange("p (t r c) -> p t r c", t=2, r=2)
                        .rearrange("p t r c -> p r t c")),
                        deps=[trk[sl]])
            kv_dep = [evk[0], evk[1], vev[0], vev[1]]

            mask_ld = None
            chunks = []
            for j in range(4):
                for pr in range(2):
                    for ci in range(CNT[j]):
                        chunks.append((j, pr, ci))
            stage = {}
            mask_lds = {}
            for idx in range(len(chunks) + 1):
                if idx < len(chunks):
                    j, pr, ci = chunks[idx]
                    cntj = CNT[j]
                    q0 = j * 512
                    pg = hg * 2 + pr
                    if pr == 0 and ci == 0:
                        mask_lds[j] = ldg(mask_sb,
                                          masksd[j].rearrange("s p q -> p s q"),
                                          deps=[mask_rd])
                    it = fs["it"]
                    scs, pts = it % 2, it % 4
                    sc_ap = bank2(scs * 2)
                    expdep = fs["exp"][it - 2] if it >= 2 else None
                    T(lambda e, sc=sc_ap, pr=pr, pg=pg, q0=q0, ci=ci:
                      e.matmul(sc[:, 0:512],
                               kd_sb[0:64, pr, ci * 128:(ci + 1) * 128],
                               qd_sb[0:64, pg, q0:q0 + 512],
                               start=True, stop=True),
                      deps=[expdep] + (kv_dep + [q_done] if idx == 0 else []))
                    mm_sc = T(lambda e, sc=sc_ap, pr=pr, pg=pg, q0=q0, ci=ci:
                              e.matmul(sc[:, 512:1024],
                                       kd_sb[64:128, pr, ci * 128:(ci + 1) * 128],
                                       qd_sb[64:128, pg, q0:q0 + 512],
                                       start=True, stop=True,
                                       tile_position=(64, 0)))
                    ptdep = fs["avs"][it - 4] if it >= 4 else None
                    exp_o = SC(lambda e, sc=sc_ap, pts=pts:
                               e.activation(pt_sb[:, pts, :], sc, AF.Exp,
                                            scale=0.125, bias=bias_sb),
                               deps=[mm_sc, ms_bias, ptdep])
                    fs["exp"].append(exp_o)
                    av_dep = exp_o
                    if ci >= cntj - NMSK:
                        s_ = ci - (cntj - NMSK)
                        mk = V_(lambda e, pts=pts, s_=s_: e.tensor_tensor(
                            pt_sb[:, pts, :].rearrange("p (a b) -> p a b", a=2),
                            pt_sb[:, pts, :].rearrange("p (a b) -> p a b", a=2),
                            mask_sb[:, s_, :].unsqueeze(1).broadcast_to(
                                [128, 2, 512]), ALU.mult),
                            deps=[exp_o, mask_lds[j]])
                        av_dep = mk
                        mask_rd = mk
                    stage[idx] = dict(av_dep=av_dep, pts=pts)
                    fs["it"] += 1
                if idx >= 1:
                    j, pr, ci = chunks[idx - 1]
                    cntj = CNT[j]
                    q0 = j * 512
                    pg = hg * 2 + pr
                    st_, sp = (ci == 0), (ci == cntj - 1)
                    up = fs["unit"] % 2
                    yb = 4 + 2 * up
                    sb = 5 + 2 * up
                    sg = stage.pop(idx - 1)
                    pts = sg["pts"]
                    ydeps = [sg["av_dep"]] + ([fs["y_ev"][up]] if st_ else [])
                    T(lambda e, pts=pts, pr=pr, yb=yb, s=st_, p=sp, ci=ci:
                      e.matmul(bank(yb)[0:64, :],
                               v_sb[:, ci, pr * 128:pr * 128 + 64],
                               pt_sb[:, pts, 0:512], start=s, stop=p),
                      deps=ydeps)
                    T(lambda e, pts=pts, pr=pr, yb=yb, s=st_, p=sp, ci=ci:
                      e.matmul(bank(yb)[64:128, :],
                               v_sb[:, ci, pr * 128 + 64:pr * 128 + 128],
                               pt_sb[:, pts, 512:1024], start=s, stop=p,
                               tile_position=(0, 64)))
                    T(lambda e, pts=pts, sb=sb, s=st_, p=sp:
                      e.matmul(bank(sb)[0:1, :], ones_sb, pt_sb[:, pts, 0:512],
                               start=s, stop=p),
                      deps=[ld_ones] + ([fs["bc_rd"][up]] if st_ else []))
                    mm_last = T(lambda e, pts=pts, sb=sb, s=st_, p=sp:
                                e.matmul(bank(sb)[32:33, :], ones_sb,
                                         pt_sb[:, pts, 512:1024],
                                         start=s, stop=p,
                                         tile_position=(0, 32)))
                    fs["avs"].append(mm_last)
                    if sp:
                        if fs["unit"] < 2:
                            V_(lambda e, sb=sb: e.tensor_copy(
                                sstage_sb[0:1, :], bank(sb)[0:1, :]),
                               deps=[mm_last, fs["sstage_rd"]])
                            cpB = V_(lambda e, sb=sb: e.tensor_copy(
                                sstage_sb[32:33, :], bank(sb)[32:33, :]))
                        else:
                            cpB = V_(lambda e, sb=sb: e.tensor_copy(
                                sstage_sb[0:33, :], bank(sb)[0:33, :]),
                                deps=[mm_last, fs["sstage_rd"]])
                        bc = T(lambda e, sb=sb: e.matmul(bank(sb), sel_sb,
                                                         sstage_sb,
                                                         start=True, stop=True),
                               deps=[cpB, ld_sel, ms_sst, fs["bc_rd"][up]])
                        fs["sstage_rd"] = bc
                        rc = V_(lambda e, sb=sb: e.reciprocal(rs_sb, bank(sb)),
                                deps=[bc, fs["rs_rd"]])
                        fs["bc_rd"][up] = rc
                        yev = V_(lambda e, pg=pg, q0=q0, yb=yb: e.tensor_tensor(
                            y_sb[:, pg, q0:q0 + 512], bank(yb), rs_sb, ALU.mult),
                            deps=[rc])
                        fs["rs_rd"] = yev
                        fs["y_ev"][up] = yev
                        y_final = yev
                        fs["unit"] += 1
            prev_flash_tail = [fs["exp"][-1], fs["avs"][-1], fs["bc_rd"][0],
                               fs["bc_rd"][1], fs["y_ev"][0], fs["y_ev"][1]]
        attn_done = y_final

        dbg_ops = []
        if dbg:
            dbg_ops.append(G(lambda e: e.dma_start(out=dbg_o["dbg_q"], in_=qd_sb),
               deps=[q_done]))
            dbg_ops.append(G(lambda e: e.dma_start(out=dbg_o["dbg_k"], in_=kd_sb),
               deps=[attn_done]))
            dbg_ops.append(G(lambda e: e.dma_start(out=dbg_o["dbg_v"], in_=v_sb),
               deps=[attn_done]))
            dbg_ops.append(G(lambda e: e.dma_start(out=dbg_o["dbg_y"], in_=y_sb),
               deps=[attn_done]))
            dbg_ops.append(G(lambda e: e.dma_start(out=dbg_o["dbg_z"], in_=z0_sb),
               deps=[z_done]))

        ld_wproj = ldg(wproj_sb, wprojd, deps=[attn_done] + dbg_ops)

        def ln_tile(xin, xout, deps):
            V_(lambda e: e.tensor_scalar(stat_sb, xin, 0.0, None, ALU.add,
                                         ALU.add, accum_out=col_sb[:, 0:1]),
               deps=deps)
            V_(lambda e: e.scalar_tensor_tensor(stat_sb, xin, 1.0, xin, ALU.mult,
                                                ALU.mult, accum_out=col_sb[:, 1:2]))
            V_(lambda e: e.tensor_scalar_mul(col_sb[:, 2:3], col_sb[:, 0:1],
                                             1.0 / D))
            V_(lambda e: e.tensor_scalar_mul(col_sb[:, 3:4], col_sb[:, 1:2],
                                             1.0 / D))
            V_(lambda e: e.scalar_tensor_tensor(col_sb[:, 4:5], col_sb[:, 2:3],
                                                -1.0, col_sb[:, 2:3], ALU.mult,
                                                ALU.mult))
            var = V_(lambda e: e.tensor_add(col_sb[:, 5:6], col_sb[:, 3:4],
                                            col_sb[:, 4:5]))
            srt = SC(lambda e: e.activation(col_sb[:, 6:7], col_sb[:, 5:6],
                                            AF.Sqrt, bias=eps_sb),
                     deps=[var, ms_eps])
            rst = V_(lambda e: e.reciprocal(col_sb[:, 7:8], col_sb[:, 6:7]),
                     deps=[srt])
            return V_(lambda e: e.tensor_scalar(xout, xin, col_sb[:, 2:3],
                                                col_sb[:, 7:8], ALU.subtract,
                                                ALU.mult), deps=[rst])

        ldo = [None, None]
        addo = [None, None]
        x1w = [None, None]
        for tt in range(16):
            sl = tt % 2
            ldo[sl] = lds(xo_sb[:, sl, :], x_own[tt * 128:(tt + 1) * 128, :],
                          deps=[addo[sl]] + ([attn_done] + dbg_ops if tt < 2 else []))
            for og in range(2):
                for ch in range(8):
                    dps = ([attn_done, ld_wproj, addo[sl]]
                           if og == 0 and ch == 0 else ())
                    o = T(lambda e, ch=ch, og=og, sl=sl, tt=tt:
                          e.matmul(bank(sl * 2 + og),
                                   y_sb[:, ch, tt * 128:(tt + 1) * 128],
                                   wproj_sb[:, ch, og * 512:(og + 1) * 512],
                                   start=(ch == 0), stop=(ch == 7)), deps=dps)
            pmm = o
            addo[sl] = V_(lambda e, sl=sl: e.tensor_add(
                x1p_sb[:, sl, :], bank2(sl * 2), xo_sb[:, sl, :]),
                deps=[pmm, ldo[sl], x1w[sl]])
            lt = ln_tile(x1p_sb[:, sl, :], x1_sb[:, sl, :], [addo[sl], x1w[sl]])
            x1w[sl] = SY(lambda e, sl=sl, tt=tt: e.dma_start(
                out=x1sp[tt * 128:(tt + 1) * 128, :], in_=x1_sb[:, sl, :]),
                deps=[lt])
            if dbg:
                SY(lambda e, sl=sl, tt=tt: e.dma_start(
                    out=dbg_o["dbg_x1"][tt * 128:(tt + 1) * 128, :],
                    in_=x1_sb[:, sl, :]), deps=[lt])

        af_war = None
        x1ld = [None, None]
        x1cast = [None, None]
        x1tr = [None, None]
        x1ev = [None, None]
        ldw12 = [None, None]
        w12_rd = [None, None]
        ldh = [None, None, None]
        wbuf_rd = [None, None, None]
        ldw3s = [None, None]
        w3_rd = None
        w3_rd2 = [None, None]
        outw = [None, None]
        addx_bank = {}
        x1f_rd = [None, None]
        fev = [None] * 8
        fsub_rd = None
        prev_tg = [x1w[0], x1w[1], attn_done]
        dbg_done = False
        for tg in range(2):
            t0 = tg * 1024
            sil = [None, None, None, None]
            aev = [None, None, None, None]
            for q8 in range(8):
                tt = t0 // 128 + q8
                sl = tt % 2
                x1ld[sl] = lds(x1f_sb[:, sl, :],
                               x1sp[tt * 128:(tt + 1) * 128, :],
                               deps=[x1w[sl], x1cast[sl], x1f_rd[sl]] + prev_tg)
                prev_tg = []
                x1cast[sl] = V_(lambda e, sl=sl: e.tensor_copy(
                    x1b_sb[:, sl, :], x1f_sb[:, sl, :]), deps=[x1ld[sl]])
                for ic in range(8):
                    o = T(transpose_op(
                        bankb(6 + sl)[:, ic * 128:(ic + 1) * 128],
                        x1b_sb[:, sl, ic * 128:(ic + 1) * 128], ident_sb),
                        deps=([x1cast[sl], x1ev[sl], w12_rd[0], w12_rd[1],
                               outw[0], outw[1], fsub_rd] +
                              ([attn_done, x1w[0], x1w[1]]
                               if tg == 0 and q8 < 2 else [])
                              if ic == 0 else ()))
                x1tr[sl] = o
                x1ev[sl] = V_(lambda e, sl=sl, q8=q8: e.tensor_copy(
                    x1fm_sb[:, :, q8 * 128:(q8 + 1) * 128],
                    bankb(6 + sl).rearrange("p (a b) -> p a b", a=8)),
                    deps=[x1tr[sl]])
            x1fm_done = [x1ev[0], x1ev[1]]
            for oc in range(32):
                wsl = oc % 2
                ldw12[wsl] = ldg(w12b_sb[:, wsl, :, :, :], w12d[oc],
                                 deps=[w12_rd[wsl], w3_rd] +
                                 ([attn_done, x1w[0], x1w[1]]
                                  if tg == 0 and oc < 2 else []))
                for half in range(2):
                    pb = half * 4 + wsl * 2
                    hs = slice(half * 512, (half + 1) * 512)
                    ss = half * 2 + wsl
                    for w_ in range(2):
                        for ic in range(8):
                            dps = ([ldw12[wsl]] + x1fm_done +
                                   [sil[ss], aev[ss]])                                 if (w_ == 0 and ic == 0) else ()
                            o = T(lambda e, w_=w_, ic=ic, pb=pb, wsl=wsl, hs=hs:
                                  e.matmul(bank(pb + w_),
                                           w12b_sb[:, wsl, w_, ic, :],
                                           x1fm_sb[:, ic, hs],
                                           start=(ic == 0), stop=(ic == 7)),
                                  deps=dps)
                    w12_rd[wsl] = o
                    sil[ss] = SC(lambda e, pb=pb, oc=oc, ss=ss:
                                 e.activation(silu_sb[:, ss, :], bank(pb),
                                              AF.Silu,
                                              scale=s1_sb[:, oc:oc + 1]),
                                 deps=[w12_rd[wsl], ld_s1, aev[ss]])
                    aev[ss] = V_(lambda e, pb=pb, oc=oc, ss=ss, hs=hs:
                                 e.scalar_tensor_tensor(
                                     af_sb[:, oc, hs], bank(pb + 1),
                                     s2_sb[:, oc:oc + 1], silu_sb[:, ss, :],
                                     ALU.mult, ALU.mult),
                                 deps=[sil[ss], ld_s2, af_war])
            a_done = [aev[0], aev[1], aev[2], aev[3]]
            if dbg and tg == 0:
                G(lambda e: e.dma_start(out=dbg_o["dbg_af"], in_=af_sb),
                  deps=a_done)

            def h_layer(src_sb, dst_sb, wsrc, scale_sb, scale_ld, dep_start,
                        ld_gate=()):
                gact = [None, None, None, None]
                for oc in range(32):
                    wsl = oc % 3
                    ldh[wsl] = ldg(wbuf_sb[:, wsl, :, :], wsrc[oc],
                                   deps=[wbuf_rd[wsl]] + list(ld_gate) +
                                   ([outw[0], outw[1]] if wsl == 2 else []))
                    for half in range(2):
                        pb = (oc * 2 + half) % 4
                        hs = slice(half * 512, (half + 1) * 512)
                        for rc in range(32):
                            dps = ([ldh[wsl], gact[pb]] + dep_start)                                 if rc == 0 else ()
                            o = T(lambda e, rc=rc, pb=pb, wsl=wsl, hs=hs:
                                  e.matmul(bank(pb), wbuf_sb[:, wsl, rc, :],
                                           src_sb[:, rc, hs],
                                           start=(rc == 0), stop=(rc == 31)),
                                  deps=dps)
                        wbuf_rd[wsl] = o
                        gact[pb] = SC(lambda e, oc=oc, pb=pb, hs=hs:
                                      e.activation(dst_sb[:, oc, hs], bank(pb),
                                                   AF.Gelu,
                                                   scale=scale_sb[:, oc:oc + 1]),
                                      deps=[wbuf_rd[wsl], scale_ld])
                return [gact[0], gact[1], gact[2], gact[3]]

            g_done = h_layer(af_sb, g_sb, h1wd, h1s_sb, ld_h1s,
                             a_done[2:],
                             ld_gate=([x1w[0], x1w[1]] if tg == 0 else ()))
            g2_done = h_layer(g_sb, af_sb, h2wd, h2s_sb, ld_h2s, g_done)

            if dbg and not dbg_done:
                G(lambda e: e.dma_start(out=dbg_o["dbg_a"], in_=g_sb),
                  deps=g_done)
                G(lambda e: e.dma_start(out=dbg_o["dbg_g"], in_=af_sb),
                  deps=g2_done)
                dbg_done = True

            for half in range(2):
                th0 = t0 + half * 512
                hs = slice(half * 512, (half + 1) * 512)
                for oc in range(8):
                    pb = oc % 2
                    wb3 = w3b_sb if oc % 2 == 0 else w3b2_sb
                    ldw3s[oc % 2] = ldg(wb3, w3wd[oc],
                                        deps=[w3_rd2[oc % 2], a_done[3],
                                              w12_rd[0], w12_rd[1]] +
                                        ([attn_done, x1w[0], x1w[1]]
                                         if tg == 0 and half == 0 and oc == 0
                                         else []))
                    for rc in range(32):
                        dps = ([ldw3s[oc % 2],
                                fev[oc - 2] if oc >= 2 else None,
                                fev[oc - 2 + 8] if oc < 2 else None] + g2_done) \
                            if rc == 0 else ()
                        o = T(lambda e, rc=rc, pb=pb, hs=hs, wb3=wb3:
                              e.matmul(bank(pb), wb3[:, rc, :],
                                       af_sb[:, rc, hs],
                                       start=(rc == 0), stop=(rc == 31)),
                              deps=dps)
                    w3_rd2[oc % 2] = o
                    w3_rd = o
                    fev[oc] = V_(lambda e, oc=oc, pb=pb: e.tensor_scalar_mul(
                        fsub_sb[:, oc, :], bank(pb), w3s_sb[:, oc:oc + 1]),
                        deps=[w3_rd, ld_w3s, fsub_rd])
                for q in range(4):
                    tt = th0 // 128 + q
                    sl = tt % 2
                    x1ld2 = lds(x1f_sb[:, sl, :],
                                x1sp[tt * 128:(tt + 1) * 128, :],
                                deps=[x1cast[sl], x1f_rd[sl], outw[sl]])
                    xb = 4 if q % 2 == 0 else 6
                    for oc in range(8):
                        o = T(transpose_op(
                            bank2(xb)[:, oc * 128:(oc + 1) * 128],
                            fsub_sb[:, oc, q * 128:(q + 1) * 128], identf_sb),
                            deps=([fev[7], ld_identf, addx_bank.get(xb)]
                                  if oc == 0 else ()))
                    ftr = o
                    if q == 3:
                        fsub_rd = ftr
                    addx = V_(lambda e, sl=sl, xb=xb: e.tensor_add(
                        x1f_sb[:, sl, :], bank2(xb), x1f_sb[:, sl, :]),
                        deps=[ftr, x1ld2])
                    addx_bank[xb] = addx
                    lt2 = ln_tile(x1f_sb[:, sl, :], x2_sb[:, sl, :],
                                  [addx, outw[sl]])
                    x1f_rd[sl] = lt2
                    outw[sl] = SY(lambda e, sl=sl, tt=tt: e.dma_start(
                        out=out[tt * 128:(tt + 1) * 128, :], in_=x2_sb[:, sl, :]),
                        deps=[lt2])
            af_war = w3_rd

        P.add("sync", lambda e: e.nop(), deps=[outw[0], outw[1]])

        block = es.enter_context(nc.Block())
        P.emit(nc, block, es)
    return nc


def _host_prep(inputs):
    f32 = np.float32
    Wq = np.asarray(inputs["L0_attn_Wq"], f32)
    Wk = np.asarray(inputs["L0_attn_Wk"], f32)
    Wv = np.asarray(inputs["L0_attn_Wv"], f32)
    kdec = np.asarray(inputs["L0_attn_key_decoder"], f32)
    vdec = np.asarray(inputs["L0_attn_value_decoder"], f32)
    Wproj = np.asarray(inputs["L0_attn_Wproj"], f32)
    w1w = np.asarray(inputs["L0_ffn_w1_w"], f32)
    w2w = np.asarray(inputs["L0_ffn_w2_w"], f32)
    h1w = np.asarray(inputs["L0_ffn_h1_w"], f32)
    h2w = np.asarray(inputs["L0_ffn_h2_w"], f32)
    w3w = np.asarray(inputs["L0_ffn_w3_w"], f32)

    com = {}
    hh, cc, ff = np.meshgrid(np.arange(H), np.arange(2), np.arange(NF),
                             indexing="ij")
    qperm = (64 * hh + 2 * ff + cc).reshape(-1)
    com["wq"] = np.ascontiguousarray(Wq[:, qperm].reshape(8, 128, D)
                                     .transpose(1, 0, 2))
    ffk, rrk = np.meshgrid(np.arange(NF), np.arange(KR), indexing="ij")
    kperm = np.concatenate([(6 * ffk + 2 * rrk + 0).reshape(-1),
                            (6 * ffk + 2 * rrk + 1).reshape(-1)])
    com["wk"] = np.ascontiguousarray(Wk[:, kperm].reshape(8, 128, VR)
                                     .transpose(1, 0, 2))
    com["wv"] = np.ascontiguousarray(Wv.reshape(8, 128, VR).transpose(1, 0, 2))
    Dm = np.zeros((VR, D), f32)
    br, bi = kdec[..., 0], kdec[..., 1]
    fi, hi, ri = np.meshgrid(np.arange(NF), np.arange(H), np.arange(KR),
                             indexing="ij")
    for c, top, bot in ((0, br, bi), (1, bi, br)):
        Dm[3 * fi + ri, 64 * hi + 32 * c + fi] = top
        Dm[96 + 3 * fi + ri, 64 * hi + 32 * c + fi] = bot
    com["d0"] = np.ascontiguousarray(Dm[:128])
    com["d1"] = np.ascontiguousarray(Dm[128:])
    VD = np.ascontiguousarray(vdec.transpose(1, 0, 2).reshape(VR, D))
    com["vd0"] = np.ascontiguousarray(VD[:128])
    com["vd1"] = np.ascontiguousarray(VD[128:])
    com["wproj"] = np.ascontiguousarray(Wproj.reshape(8, 128, D)
                                        .transpose(1, 0, 2))
    w1s_ = w1w.reshape(32, 128, 8, 128).transpose(0, 3, 2, 1)
    w2s_ = w2w.reshape(32, 128, 8, 128).transpose(0, 3, 2, 1)
    com["w12"] = np.ascontiguousarray(np.stack([w1s_, w2s_], 2))
    com["h1w"] = np.ascontiguousarray(h1w.reshape(32, 128, 32, 128)
                                      .transpose(0, 3, 2, 1))
    com["h2w"] = np.ascontiguousarray(h2w.reshape(32, 128, 32, 128)
                                      .transpose(0, 3, 2, 1))
    com["w3w"] = np.ascontiguousarray(w3w.reshape(8, 128, 32, 128)
                                      .transpose(0, 3, 2, 1))
    com["s1s"] = np.ascontiguousarray(
        np.asarray(inputs["L0_ffn_w1_s"], f32).reshape(32, 128).T)
    com["s2s"] = np.ascontiguousarray(
        np.asarray(inputs["L0_ffn_w2_s"], f32).reshape(32, 128).T)
    com["h1s"] = np.ascontiguousarray(
        np.asarray(inputs["L0_ffn_h1_s"], f32).reshape(32, 128).T)
    com["h2s"] = np.ascontiguousarray(
        np.asarray(inputs["L0_ffn_h2_s"], f32).reshape(32, 128).T)
    com["w3s"] = np.ascontiguousarray(
        np.asarray(inputs["L0_ffn_w3_s"], f32).reshape(8, 128).T)
    com["ident"] = np.eye(128, dtype=f32)
    sel = np.zeros((33, 128), f32)
    sel[0, 0:64] = 1.0
    sel[32, 64:128] = 1.0
    com["sel33"] = sel
    com["onesb"] = np.ones((128, 1), f32)
    half = HD // 2
    freq = (1.0 / ROPE_BASE ** (np.arange(half, dtype=f32) / half)).astype(f32)
    freqs = np.outer(np.arange(S, dtype=f32), freq)
    COS = np.cos(freqs).astype(f32)
    SIN = np.sin(freqs).astype(f32)
    com["ck"] = np.ascontiguousarray(COS.reshape(32, 128, 32).transpose(1, 0, 2))
    com["sk"] = np.ascontiguousarray(SIN.reshape(32, 128, 32).transpose(1, 0, 2))
    return com, COS, SIN


def _core_inputs(inputs, com, COS, SIN, core):
    b, hlf = core // 2, core % 2
    x = np.asarray(inputs["L0_attn_x"], np.float32)
    groups = [2 * j + 1 - hlf for j in range(4)]
    rows = np.concatenate([np.arange(g * 512, (g + 1) * 512) for g in groups])
    d = dict(com)
    d["x_own"] = np.ascontiguousarray(x[b][rows])
    d["x_full"] = np.ascontiguousarray(x[b])
    d["cq"] = np.ascontiguousarray(COS[rows].reshape(16, 128, 32)
                                   .transpose(1, 0, 2))
    d["sq"] = np.ascontiguousarray(SIN[rows].reshape(16, 128, 32)
                                   .transpose(1, 0, 2))
    m = np.zeros((4, NMSK, 128, 512), np.float32)
    for j in range(4):
        q0 = groups[j] * 512
        for s_ in range(NMSK):
            k0 = (CNT[j] - NMSK + s_) * 128
            kk = k0 + np.arange(128)[:, None]
            qq = q0 + np.arange(512)[None, :]
            m[j, s_] = (kk <= qq).astype(np.float32)
    d["masks"] = m
    return d, groups


def kernel(**inputs):
    com, COS, SIN = _host_prep(inputs)
    in_maps, group_list = [], []
    for c in range(8):
        d, groups = _core_inputs(inputs, com, COS, SIN, c)
        in_maps.append(d)
        group_list.append(groups)
    nc = build_nc()
    res = run_bass_kernel_spmd(nc, in_maps, core_ids=list(range(8)))
    outa = np.empty((BSZ, S, D), np.float32)
    for c in range(8):
        b = c // 2
        shard = np.asarray(res.results[c]["out"], np.float32).reshape(TQ, D)
        for j, g in enumerate(group_list[c]):
            outa[b, g * 512:(g + 1) * 512] = shard[j * 512:(j + 1) * 512]
    return outa


# version 15
# speedup vs baseline: 1.0144x; 1.0144x over previous
import contextlib
import numpy as np
import concourse.bass as bass
import concourse.mybir as mybir
from concourse.bass_utils import run_bass_kernel_spmd

F32 = mybir.dt.float32
BF16 = mybir.dt.bfloat16
AF = mybir.ActivationFunctionType
ALU = mybir.AluOpType

S, D, BSZ, H, HD, NF, KR, VR, DH = 4096, 1024, 4, 16, 64, 32, 3, 192, 4096
TQ = 2048
CNT = (8, 16, 24, 32)
NMSK = 8
EXPB = 12.0
LNEPS = 1e-5
ROPE_BASE = 10000.0
NHG = 4
NTG = 4

ENGS = ("tensor", "vector", "scalar", "gpsimd", "sync")


class Prog:

    def __init__(self):
        self.ops = []
        self.per_eng = {e: [] for e in ENGS}

    def add(self, eng, fn, deps=(), dma=False):
        i = len(self.ops)
        self.ops.append((eng, fn, tuple(d for d in deps if d is not None), dma))
        self.per_eng[eng].append(i)
        return i

    def emit(self, nc, block, es):
        need = set()
        for i, (eng, fn, deps, dma) in enumerate(self.ops):
            if dma:
                need.add(i)
            if eng in ("vector", "scalar"):
                need.add(i)
            for d in deps:
                pe, _, _, pdma = self.ops[d]
                if pdma or pe != eng:
                    need.add(d)
        sems = {e: es.enter_context(nc.semaphore("S" + e)) for e in ENGS}
        for e in ENGS:
            for s in range(3):
                sems[(e, s)] = es.enter_context(nc.semaphore(f"S{e}{s}"))
        val, cnt = {}, {e: 0 for e in ENGS}
        for e in ENGS:
            for i in self.per_eng[e]:
                if i in need and not self.ops[i][3]:
                    cnt[e] += 1
                    val[i] = cnt[e]
        waits = []
        for i, (eng, fn, deps, dma) in enumerate(self.ops):
            w = {}
            for d in deps:
                if d in val and not self.ops[d][3]:
                    pe = self.ops[d][0]
                    if pe != eng:
                        w[pe] = max(w.get(pe, 0), val[d])
            if eng in ("vector", "scalar") and val.get(i, 1) > 1:
                w[eng] = max(w.get(eng, 0), val[i] - 1)
            waits.append(w)
        B, R = 2, 3
        self.stripe_of = {}
        self.stripe_val = {}
        stripe_gate = {}
        for e in ENGS:
            dmas = [i for i in self.per_eng[e] if self.ops[i][3]]
            epochs, cur = [], []
            for i in dmas:
                boundary = len(cur) >= B
                if not boundary and cur:
                    start = cur[0]
                    if any(d >= start for d in self.ops[i][2]):
                        boundary = True
                if boundary and cur:
                    epochs.append(cur)
                    cur = []
                cur.append(i)
            if cur:
                epochs.append(cur)
            stot = [0] * R
            for k, ep in enumerate(epochs):
                s = k % R
                if stot[s] > 0:
                    stripe_gate[ep[0]] = (s, stot[s])
                stot[s] += 16 * len(ep)
                for i in ep:
                    self.stripe_of[i] = (e, s)
                    self.stripe_val[i] = stot[s]
        self._stripe_gate = stripe_gate
        waits2 = []
        for i, (eng, fn, deps, dma) in enumerate(self.ops):
            w = {}
            for pe, v in waits[i].items():
                w[pe] = max(w.get(pe, 0), v)
            for d in deps:
                if d in self.stripe_of:
                    pe, s = self.stripe_of[d]
                    key = (pe, s)
                    w[key] = max(w.get(key, 0), self.stripe_val[d])
            if i in self._stripe_gate:
                s, v = self._stripe_gate[i]
                key = (eng, s)
                w[key] = max(w.get(key, 0), v)
            waits2.append(w)
        waits = waits2
        seen = {e: {} for e in ENGS}
        for e in ENGS:
            for i in self.per_eng[e]:
                w = waits[i]
                for pe in list(w):
                    if seen[e].get(pe, 0) >= w[pe]:
                        del w[pe]
                    else:
                        seen[e][pe] = w[pe]
        for e in ENGS:
            def body(eng, _e=e):
                for i in self.per_eng[_e]:
                    _, fn, _, dma = self.ops[i]
                    for pe, v in sorted(waits[i].items(), key=str):
                        eng.wait_ge(sems[pe], v)
                    ins = fn(eng)
                    if dma:
                        ins.then_inc(sems[self.stripe_of[i]], 16)
                    elif i in val:
                        ins.then_inc(sems[_e], 1)
            getattr(block, e)(body)


class Carver:
    def __init__(self, pool_ap, limit, off=0):
        self.pool = pool_ap
        self.off = off
        self.limit = limit

    def take(self, shape, dt):
        p = shape[0]
        elems = int(np.prod(shape[1:]))
        width = elems * (2 if dt == F32 else 1)
        self.off = (self.off + 1) & ~1
        a = self.pool[0:p, self.off:self.off + width]
        self.off += width
        assert self.off <= self.limit, f"sbuf pool overflow {self.off} > {self.limit}"
        if dt == F32:
            a = a.bitcast(F32)
        if len(shape) == 3:
            a = a.rearrange("p (a b) -> p a b", a=shape[1])
        elif len(shape) == 4:
            a = a.rearrange("p (a b c) -> p a b c", a=shape[1], b=shape[2])
        elif len(shape) == 5:
            a = a.rearrange("p (a b c d) -> p a b c d", a=shape[1], b=shape[2],
                            c=shape[3])
        else:
            assert len(shape) == 2
        return a


def build_nc(dbg=False):
    nc = bass.Bass("TRN2", target_bir_lowering=False,
                   detect_race_conditions=False)

    def din(name, shape):
        return nc.dram_tensor(name, shape, F32, kind="ExternalInput").ap()

    x_own = din("x_own", [TQ, D])
    x_full = din("x_full", [S, D])
    cqd, sqd = din("cq", [128, 16, 32]), din("sq", [128, 16, 32])
    ckd, skd = din("ck", [128, 32, 32]), din("sk", [128, 32, 32])
    masksd = din("masks", [4, NMSK, 128, 512])
    wqd = din("wq", [128, 8, D])
    wkd = din("wk", [128, 8, VR])
    wvd = din("wv", [128, 8, VR])
    d0d, d1d = din("d0", [128, D]), din("d1", [64, D])
    vd0d, vd1d = din("vd0", [128, D]), din("vd1", [64, D])
    wprojd = din("wproj", [128, 8, D])
    w12d = din("w12", [32, 128, 2, 8, 128])
    h1wd = din("h1w", [32, 128, 32, 128])
    h2wd = din("h2w", [32, 128, 32, 128])
    w3wd = din("w3w", [8, 128, 32, 128])
    s1d, s2d = din("s1s", [128, 32]), din("s2s", [128, 32])
    h1sd, h2sd = din("h1s", [128, 32]), din("h2s", [128, 32])
    w3sd = din("w3s", [128, 8])
    identd = din("ident", [128, 128])
    sel33d = din("sel33", [33, 128])
    onesd = din("onesb", [128, 1])

    out = nc.dram_tensor("out", [TQ, D], F32, kind="ExternalOutput").ap()
    x1sp = nc.dram_tensor("x1sp", [TQ, D], F32, kind="ExternalOutput").ap()
    dbg_o = {}
    if dbg:
        for nm, shp in (("dbg_q", [128, 8, TQ]), ("dbg_k", [128, 2, S]),
                        ("dbg_v", [128, 32, 256]), ("dbg_y", [128, 8, TQ]),
                        ("dbg_x1", [TQ, D]), ("dbg_z", [128, S]),
                        ("dbg_a", [128, 32, 1024]), ("dbg_g", [128, 32, 1024]),
                        ("dbg_af", [128, 32, 1024])):
            dbg_o[nm] = nc.dram_tensor(nm, shp, F32, kind="ExternalOutput").ap()

    P = Prog()

    def G(fn, deps=()):
        return P.add("gpsimd", fn, deps, dma=True)

    def SY(fn, deps=()):
        return P.add("sync", fn, deps, dma=True)

    def T(fn, deps=()):
        return P.add("tensor", fn, deps)

    def V_(fn, deps=()):
        return P.add("vector", fn, deps)

    def SC(fn, deps=()):
        return P.add("scalar", fn, deps)

    def ldg(dst, src, deps=()):
        return G(lambda e, a=dst, b=src: e.dma_start(out=a, in_=b), deps)

    def lds(dst, src, deps=()):
        return SY(lambda e, a=dst, b=src: e.dma_start(out=a, in_=b), deps)

    es = contextlib.ExitStack()
    with es:
        POOLW = 106416
        pool = es.enter_context(nc.sbuf_tensor("pool", [128, POOLW], BF16)).ap()
        psum = es.enter_context(nc.psum_tensor("psum", [128, 8, 512], F32)).ap()

        def bank(b):
            return psum[:, b, :]

        def bank2(b):
            return psum[:, b:b + 2, :].rearrange("p a b -> p (a b)")

        def bankb(b):
            return psum[:, b, :].bitcast(BF16)

        C = Carver(pool, POOLW)
        ident_sb = C.take([128, 128], BF16)
        identf_sb = C.take([128, 128], F32)
        eps_sb = C.take([128, 1], F32)
        s1_sb = C.take([128, 32], F32)
        s2_sb = C.take([128, 32], F32)
        h1s_sb = C.take([128, 32], F32)
        h2s_sb = C.take([128, 32], F32)
        w3s_sb = C.take([128, 8], F32)
        col_sb = C.take([128, 8], F32)
        a_live_end = C.off
        ck_sb = C.take([128, 32, 32], F32)
        sk_sb = C.take([128, 32, 32], F32)
        cq_sb = C.take([128, 16, 32], F32)
        sq_sb = C.take([128, 16, 32], F32)
        mask_sb = C.take([128, NMSK, 512], BF16)
        sel_sb = C.take([33, 128], F32)
        ones_sb = C.take([128, 1], BF16)
        bias_sb = C.take([128, 1], F32)
        pt_sb = C.take([128, 4, 1024], BF16)
        sstage_sb = C.take([33, 512], F32)
        lnr_sb = C.take([128, 512], F32)
        rs_sb = C.take([128, 512], F32)
        wk_sb = C.take([128, 8, VR], BF16)
        wv_sb = C.take([128, 8, VR], BF16)
        d0_sb = C.take([128, D], BF16)
        d1_sb = C.take([64, D], BF16)
        vd0_sb = C.take([128, D], BF16)
        vd1_sb = C.take([64, D], BF16)
        a_end = C.off
        qd_sb = C.take([128, 8, TQ], BF16)
        y_sb = C.take([128, 8, TQ], BF16)
        bc_end = C.off
        z0_sb = C.take([128, S], BF16)
        z1_sb = C.take([64, S], BF16)
        xv0_sb = C.take([128, S], BF16)
        xv1_sb = C.take([64, S], BF16)
        kd_off = C.off
        kd_sb = C.take([128, 2, S], BF16)
        v_sb = C.take([128, 32, 256], BF16)
        d_stream = C.off
        wq_sb = Carver(pool, d_stream, off=kd_off).take([128, 8, D], BF16)
        xtm_sb = C.take([128, 2, D], BF16)
        xfm_sb = C.take([128, 2, 8, 128], BF16)
        qtm_sb = C.take([128, 2, D], BF16)
        ktm_sb = C.take([128, 2, 512], BF16)
        rsc1_sb = C.take([128, 512], F32)
        rsc2_sb = C.take([128, 512], F32)
        attn_peak = C.off
        wproj_sb = Carver(pool, bc_end + 8192, off=bc_end).take([128, 8, D],
                                                               BF16)
        CP = Carver(pool, POOLW, off=kd_off)
        xo_sb = CP.take([128, 2, D], F32)
        x1p_sb = CP.take([128, 2, D], F32)
        x1_sb = CP.take([128, 2, D], F32)
        CF1 = Carver(pool, a_end, off=a_live_end)
        stat_sb = CF1.take([128, D], F32)
        wbuf_off = CF1.off
        wbuf_sb = CF1.take([128, 3, 32, 128], BF16)
        w12b_off = CF1.off
        w12b_sb = CF1.take([128, 2, 2, 8, 128], BF16)
        sw3_off = CF1.off
        silu_sb = CF1.take([128, 4, 512], F32)
        x1b_sb = Carver(pool, a_end, off=wbuf_off).take([128, 2, D], BF16)
        CF2 = Carver(pool, bc_end, off=a_end)
        g_sb = CF2.take([128, 32, 1024], BF16)
        w3b_sb = Carver(pool, a_end, off=sw3_off).take([128, 32, 128], BF16)
        CF3 = Carver(pool, POOLW, off=bc_end)
        af_sb = CF3.take([128, 32, 1024], BF16)
        x1fm_off = CF3.off
        x1fm_sb = CF3.take([128, 8, 1024], BF16)
        x1f_sb = CF3.take([128, 2, D], F32)
        ffn_peak = CF3.off
        fsub_sb = Carver(pool, POOLW, off=x1fm_off).take([128, 8, 512], F32)
        x2_sb = Carver(pool, a_end,
                       off=wbuf_off + 2 * 4096).take([128, 2, D], F32)
        w3b2_sb = Carver(pool, a_end, off=w12b_off).take([128, 32, 128], BF16)

        ld_wk = ldg(wk_sb, wkd)
        ld_wv = ldg(wv_sb, wvd)
        ld_d0 = ldg(d0_sb, d0d)
        ld_d1 = ldg(d1_sb, d1d)
        ld_vd0 = ldg(vd0_sb, vd0d)
        ld_vd1 = ldg(vd1_sb, vd1d)
        ld_ident = ldg(ident_sb, identd)
        ld_ones = ldg(ones_sb, onesd)
        ld_wq = ldg(wq_sb, wqd)
        ld_ck = lds(ck_sb, ckd)
        ld_sk = lds(sk_sb, skd)
        ld_cq = lds(cq_sb, cqd)
        ld_sq = lds(sq_sb, sqd)
        ld_identf = lds(identf_sb, identd)
        ld_sel = lds(sel_sb, sel33d)
        ld_s1 = lds(s1_sb, s1d)
        ld_s2 = lds(s2_sb, s2d)
        ld_h1s = lds(h1s_sb, h1sd)
        ld_h2s = lds(h2s_sb, h2sd)
        ld_w3s = lds(w3s_sb, w3sd)
        ms_bias = V_(lambda e: e.memset(bias_sb, -EXPB))
        ms_sst = V_(lambda e: e.memset(sstage_sb, 0.0))
        ms_eps = V_(lambda e: e.memset(eps_sb, LNEPS))

        def reim(ap2, nh):
            return ap2.rearrange("p (h c f) -> p h c f", h=nh, c=2)

        neg0 = V_(lambda e: e.tensor_scalar_mul(
            reim(d0_sb[96:128, :], 16)[:, :, 0, :],
            reim(d0_sb[96:128, :], 16)[:, :, 0, :], -1.0), deps=[ld_d0])
        neg1 = V_(lambda e: e.tensor_scalar_mul(
            reim(d1_sb, 16)[:, :, 0, :],
            reim(d1_sb, 16)[:, :, 0, :], -1.0), deps=[ld_d1])

        def transpose_op(dst, src, idn):
            return lambda e, a=dst, b=src, c=idn: e.transpose(a, b, c)

        ldx = [None, None]
        tr_last = [None, None]
        ev_fm = [None, None]
        zx_ev = [None, None]
        mmz = None
        for i1 in range(33):
            if i1 < 32:
                kt = i1
                sl = kt % 2
                ldx[sl] = ldg(xtm_sb[:, sl, :],
                              x_full[kt * 128:(kt + 1) * 128, :],
                              deps=[tr_last[sl]])
                for ic in range(8):
                    o = T(transpose_op(
                        bankb(6 + sl)[:, ic * 128:(ic + 1) * 128],
                        xtm_sb[:, sl, ic * 128:(ic + 1) * 128], ident_sb),
                        deps=([ldx[sl], ld_ident, ev_fm[sl]] if ic == 0 else ()))
                tr_last[sl] = o
                ev_fm[sl] = V_(lambda e, sl=sl: e.tensor_copy(
                    xfm_sb[:, sl, :, :],
                    bankb(6 + sl).rearrange("p (a b) -> p a b", a=8)),
                    deps=[tr_last[sl]])
            if i1 >= 1:
                kt = i1 - 1
                sl = kt % 2
                zx = bank(sl).rearrange("p (a b) -> p a b", a=4)
                for ic in range(8):
                    st_, sp = (ic == 0), (ic == 7)
                    xf = xfm_sb[:, sl, ic, :]
                    dps = [ev_fm[sl], ld_wk, ld_wv, zx_ev[sl]] if ic == 0 else ()
                    T(lambda e, w=wk_sb[:, ic, 0:128], x=xf, s=st_, zx=zx:
                      e.matmul(zx[:, 0, :], w, x, start=s, stop=False), deps=dps)
                    T(lambda e, w=wk_sb[:, ic, 128:192], x=xf, zx=zx:
                      e.matmul(zx[0:64, 1, :], w, x, start=False, stop=False))
                    T(lambda e, w=wv_sb[:, ic, 128:192], x=xf, zx=zx:
                      e.matmul(zx[0:64, 3, :], w, x, start=False, stop=False))
                    mmz = T(lambda e, w=wv_sb[:, ic, 0:128], x=xf, p=sp, zx=zx:
                            e.matmul(zx[:, 2, :], w, x, start=False, stop=p))
                V_(lambda e, kt=kt, zx=zx: e.tensor_copy(
                    z0_sb[:, kt * 128:(kt + 1) * 128], zx[:, 0, :]), deps=[mmz])
                V_(lambda e, kt=kt, zx=zx: e.tensor_copy(
                    z1_sb[:, kt * 128:(kt + 1) * 128], zx[0:64, 1, :]))
                V_(lambda e, kt=kt, zx=zx: e.tensor_copy(
                    xv0_sb[:, kt * 128:(kt + 1) * 128], zx[:, 2, :]))
                zx_ev[sl] = V_(lambda e, kt=kt, zx=zx: e.tensor_copy(
                    xv1_sb[:, kt * 128:(kt + 1) * 128], zx[0:64, 3, :]))
        z_done = zx_ev[31 % 2]


        def rope_ops(ps_tm, nh, ctab, stab, out_tm, deps):
            src, dst = reim(ps_tm, nh), reim(out_tm, nh)
            cb = ctab.unsqueeze(1).broadcast_to([128, nh, 32])
            st = stab.unsqueeze(1).broadcast_to([128, nh, 32])
            r1 = rsc1_sb[:, 0:nh * 32].rearrange("p (h f) -> p h f", h=nh)
            r2 = rsc2_sb[:, 0:nh * 32].rearrange("p (h f) -> p h f", h=nh)
            V_(lambda e: e.tensor_tensor(r1, src[:, :, 0, :], cb, ALU.mult),
               deps=deps)
            V_(lambda e: e.tensor_tensor(r2, src[:, :, 1, :], st, ALU.mult))
            V_(lambda e: e.tensor_sub(dst[:, :, 0, :], r1, r2))
            V_(lambda e: e.tensor_tensor(r1, src[:, :, 0, :], st, ALU.mult))
            V_(lambda e: e.tensor_tensor(r2, src[:, :, 1, :], cb, ALU.mult))
            return V_(lambda e: e.tensor_add(dst[:, :, 1, :], r1, r2))

        ldq = [None, None]
        ev1 = [None, None]
        qd_ev = [None, None]
        rope_done = [None, None]
        qtr = [None, None]
        qmm = [None, None]
        for i2 in range(18):
            if i2 < 16:
                tt = i2
                sl = tt % 2
                ldq[sl] = ldg(xtm_sb[:, sl, :],
                              x_own[tt * 128:(tt + 1) * 128, :],
                              deps=[tr_last[sl]])
                for ic in range(8):
                    o = T(transpose_op(
                        bankb(6 + sl)[:, ic * 128:(ic + 1) * 128],
                        xtm_sb[:, sl, ic * 128:(ic + 1) * 128], ident_sb),
                        deps=([ldq[sl], ev_fm[sl] if tt < 2 else ev1[sl]]
                              if ic == 0 else ()))
                tr_last[sl] = o
                ev1[sl] = V_(lambda e, sl=sl: e.tensor_copy(
                    xfm_sb[:, sl, :, :],
                    bankb(6 + sl).rearrange("p (a b) -> p a b", a=8)),
                    deps=[tr_last[sl]])
            if 1 <= i2 <= 16:
                tt = i2 - 1
                sl = tt % 2
                qb = sl * 2
                for og in range(2):
                    for ic in range(8):
                        dps = ([ev1[sl], ld_wq, rope_done[sl], zx_ev[sl]]
                               if (og == 0 and ic == 0) else ())
                        o = T(lambda e, og=og, ic=ic, sl=sl, qb=qb:
                              e.matmul(bank(qb + og), xfm_sb[:, sl, ic, :],
                                       wq_sb[:, ic, og * 512:(og + 1) * 512],
                                       start=(ic == 0), stop=(ic == 7)), deps=dps)
                qmm[sl] = o
                rope_done[sl] = rope_ops(
                    bank2(qb), 16, cq_sb[:, tt, :], sq_sb[:, tt, :],
                    qtm_sb[:, sl, :],
                    deps=[qmm[sl], ld_cq, ld_sq, qtr[sl]])
            if i2 >= 2:
                tt = i2 - 2
                sl = tt % 2
                for p in range(8):
                    o = T(transpose_op(
                        bankb(4 + sl)[:, p * 128:(p + 1) * 128],
                        qtm_sb[:, sl, p * 128:(p + 1) * 128], ident_sb),
                        deps=([rope_done[sl], qd_ev[sl]] if p == 0 else ()))
                qtr[sl] = o
                qd_ev[sl] = V_(lambda e, sl=sl, tt=tt: e.tensor_copy(
                    qd_sb[:, :, tt * 128:(tt + 1) * 128],
                    bankb(4 + sl).rearrange("p (a b) -> p a b", a=8)),
                    deps=[qtr[sl]])
        q_done = qd_ev[15 % 2]
        q_done2 = qd_ev[14 % 2]

        fs = dict(it=0, exp=[], avs=[], y_ev=[None, None],
                  sstage_rd=None, bc_rd=[None, None], rs_rd=None, unit=0)
        y_final = None
        prev_flash_tail = [q_done, q_done2, rope_done[0], rope_done[1]]

        mask_rd = None
        for hg in range(NHG):
            dcol = slice(hg * 256, hg * 256 + 256)
            ropek = [None, None]
            trk = [None, None]
            evk = [None, None]
            vev = [None, None]
            kmm = [None, None]
            vmm = [None, None]

            def rope_pair(ps2, pi, out2, deps):
                src4 = ps2.rearrange("p t (h c f) -> p t h c f", h=4, c=2)
                dst4 = out2.rearrange("p t (h c f) -> p t h c f", h=4, c=2)
                cb = ck_sb[:, 2 * pi:2 * pi + 2, :].unsqueeze(2) \
                    .broadcast_to([128, 2, 4, 32])
                st = sk_sb[:, 2 * pi:2 * pi + 2, :].unsqueeze(2) \
                    .broadcast_to([128, 2, 4, 32])
                r1 = rsc1_sb[:, 0:256].rearrange("p (t h f) -> p t h f", t=2, h=4)
                r2 = rsc2_sb[:, 0:256].rearrange("p (t h f) -> p t h f", t=2, h=4)
                V_(lambda e: e.tensor_tensor(r1, src4[:, :, :, 0, :], cb,
                                             ALU.mult), deps=deps)
                V_(lambda e: e.tensor_tensor(r2, src4[:, :, :, 1, :], st,
                                             ALU.mult))
                V_(lambda e: e.tensor_sub(dst4[:, :, :, 0, :], r1, r2))
                V_(lambda e: e.tensor_tensor(r1, src4[:, :, :, 0, :], st,
                                             ALU.mult))
                V_(lambda e: e.tensor_tensor(r2, src4[:, :, :, 1, :], cb,
                                             ALU.mult))
                return V_(lambda e: e.tensor_add(dst4[:, :, :, 1, :], r1, r2))

            for pi in range(17):
                if pi < 16:
                    sl = pi % 2
                    b0 = sl * 2
                    for tl in range(2):
                        kt = 2 * pi + tl
                        dps = ([z_done, neg0, neg1, ld_ck, ld_sk, ropek[sl]] +
                               prev_flash_tail if pi == 0 else
                               ([ropek[sl]] if tl == 0 else []))
                        T(lambda e, kt=kt, b=b0 + tl, dc=dcol:
                          e.matmul(bank(b)[:, 0:256],
                                   z0_sb[:, kt * 128:(kt + 1) * 128],
                                   d0_sb[:, dc], start=True, stop=False),
                          deps=dps)
                        kmm[sl] = T(lambda e, kt=kt, b=b0 + tl, dc=dcol:
                                    e.matmul(bank(b)[:, 0:256],
                                             z1_sb[:, kt * 128:(kt + 1) * 128],
                                             d1_sb[:, dc], start=False,
                                             stop=True))
                        vdps = ([z_done, ld_vd0, ld_vd1, vev[kt % 2]] +
                                (prev_flash_tail if pi == 0 else []))
                        vmm[kt % 2] = T(lambda e, kt=kt, dc=dcol:
                                        e.matmul(bank(4 + kt % 2)[:, 0:256],
                                                 xv0_sb[:, kt * 128:(kt + 1) * 128],
                                                 vd0_sb[:, dc], start=True,
                                                 stop=False), deps=vdps)
                        vmm[kt % 2] = T(lambda e, kt=kt, dc=dcol:
                                        e.matmul(bank(4 + kt % 2)[:, 0:256],
                                                 xv1_sb[:, kt * 128:(kt + 1) * 128],
                                                 vd1_sb[:, dc], start=False,
                                                 stop=True))
                    ropek[sl] = rope_pair(
                        psum[:, b0:b0 + 2, 0:256], pi, ktm_sb[:, sl, :]
                        .rearrange("p (t c) -> p t c", t=2),
                        deps=[kmm[sl], trk[sl]])
                    for tl in range(2):
                        kt = 2 * pi + tl
                        vev[kt % 2] = V_(lambda e, kt=kt: e.tensor_copy(
                            v_sb[:, kt, :], bank(4 + kt % 2)[:, 0:256]),
                            deps=[vmm[kt % 2]])
                if pi >= 1:
                    pj = pi - 1
                    sl = pj % 2
                    for p in range(4):
                        o = T(transpose_op(
                            bankb(6 + sl)[:, p * 128:(p + 1) * 128],
                            ktm_sb[:, sl, p * 128:(p + 1) * 128], ident_sb),
                            deps=([ropek[sl], evk[sl]] +
                                  (prev_flash_tail if pj == 0 else [])
                                  if p == 0 else ()))
                    trk[sl] = o
                    evk[sl] = V_(lambda e, sl=sl, pj=pj: e.tensor_copy(
                        kd_sb[:, :, pj * 256:(pj + 1) * 256]
                        .rearrange("p r (t c) -> p r t c", t=2),
                        bankb(6 + sl)[:, 0:512]
                        .rearrange("p (t r c) -> p t r c", t=2, r=2)
                        .rearrange("p t r c -> p r t c")),
                        deps=[trk[sl]])
            kv_dep = [evk[0], evk[1], vev[0], vev[1]]

            mask_ld = None
            chunks = []
            for j in range(4):
                for pr in range(2):
                    for ci in range(CNT[j]):
                        chunks.append((j, pr, ci))
            stage = {}
            mask_lds = {}
            for idx in range(len(chunks) + 1):
                if idx < len(chunks):
                    j, pr, ci = chunks[idx]
                    cntj = CNT[j]
                    q0 = j * 512
                    pg = hg * 2 + pr
                    if pr == 0 and ci == 0:
                        mask_lds[j] = ldg(mask_sb,
                                          masksd[j].rearrange("s p q -> p s q"),
                                          deps=[mask_rd])
                    it = fs["it"]
                    scs, pts = it % 2, it % 4
                    sc_ap = bank2(scs * 2)
                    expdep = fs["exp"][it - 2] if it >= 2 else None
                    T(lambda e, sc=sc_ap, pr=pr, pg=pg, q0=q0, ci=ci:
                      e.matmul(sc[:, 0:512],
                               kd_sb[0:64, pr, ci * 128:(ci + 1) * 128],
                               qd_sb[0:64, pg, q0:q0 + 512],
                               start=True, stop=True),
                      deps=[expdep] + (kv_dep + [q_done] if idx == 0 else []))
                    mm_sc = T(lambda e, sc=sc_ap, pr=pr, pg=pg, q0=q0, ci=ci:
                              e.matmul(sc[:, 512:1024],
                                       kd_sb[64:128, pr, ci * 128:(ci + 1) * 128],
                                       qd_sb[64:128, pg, q0:q0 + 512],
                                       start=True, stop=True,
                                       tile_position=(64, 0)))
                    ptdep = fs["avs"][it - 4] if it >= 4 else None
                    exp_o = SC(lambda e, sc=sc_ap, pts=pts:
                               e.activation(pt_sb[:, pts, :], sc, AF.Exp,
                                            scale=0.125, bias=bias_sb),
                               deps=[mm_sc, ms_bias, ptdep])
                    fs["exp"].append(exp_o)
                    av_dep = exp_o
                    if ci >= cntj - NMSK:
                        s_ = ci - (cntj - NMSK)
                        mk = V_(lambda e, pts=pts, s_=s_: e.tensor_tensor(
                            pt_sb[:, pts, :].rearrange("p (a b) -> p a b", a=2),
                            pt_sb[:, pts, :].rearrange("p (a b) -> p a b", a=2),
                            mask_sb[:, s_, :].unsqueeze(1).broadcast_to(
                                [128, 2, 512]), ALU.mult),
                            deps=[exp_o, mask_lds[j]])
                        av_dep = mk
                        mask_rd = mk
                    stage[idx] = dict(av_dep=av_dep, pts=pts)
                    fs["it"] += 1
                if idx >= 1:
                    j, pr, ci = chunks[idx - 1]
                    cntj = CNT[j]
                    q0 = j * 512
                    pg = hg * 2 + pr
                    st_, sp = (ci == 0), (ci == cntj - 1)
                    up = fs["unit"] % 2
                    yb = 4 + 2 * up
                    sb = 5 + 2 * up
                    sg = stage.pop(idx - 1)
                    pts = sg["pts"]
                    ydeps = [sg["av_dep"]] + ([fs["y_ev"][up]] if st_ else [])
                    T(lambda e, pts=pts, pr=pr, yb=yb, s=st_, p=sp, ci=ci:
                      e.matmul(bank(yb)[0:64, :],
                               v_sb[:, ci, pr * 128:pr * 128 + 64],
                               pt_sb[:, pts, 0:512], start=s, stop=p),
                      deps=ydeps)
                    T(lambda e, pts=pts, pr=pr, yb=yb, s=st_, p=sp, ci=ci:
                      e.matmul(bank(yb)[64:128, :],
                               v_sb[:, ci, pr * 128 + 64:pr * 128 + 128],
                               pt_sb[:, pts, 512:1024], start=s, stop=p,
                               tile_position=(0, 64)))
                    T(lambda e, pts=pts, sb=sb, s=st_, p=sp:
                      e.matmul(bank(sb)[0:1, :], ones_sb, pt_sb[:, pts, 0:512],
                               start=s, stop=p),
                      deps=[ld_ones] + ([fs["bc_rd"][up]] if st_ else []))
                    mm_last = T(lambda e, pts=pts, sb=sb, s=st_, p=sp:
                                e.matmul(bank(sb)[32:33, :], ones_sb,
                                         pt_sb[:, pts, 512:1024],
                                         start=s, stop=p,
                                         tile_position=(0, 32)))
                    fs["avs"].append(mm_last)
                    if sp:
                        if fs["unit"] < 2:
                            V_(lambda e, sb=sb: e.tensor_copy(
                                sstage_sb[0:1, :], bank(sb)[0:1, :]),
                               deps=[mm_last, fs["sstage_rd"]])
                            cpB = V_(lambda e, sb=sb: e.tensor_copy(
                                sstage_sb[32:33, :], bank(sb)[32:33, :]))
                        else:
                            cpB = V_(lambda e, sb=sb: e.tensor_copy(
                                sstage_sb[0:33, :], bank(sb)[0:33, :]),
                                deps=[mm_last, fs["sstage_rd"]])
                        bc = T(lambda e, sb=sb: e.matmul(bank(sb), sel_sb,
                                                         sstage_sb,
                                                         start=True, stop=True),
                               deps=[cpB, ld_sel, ms_sst, fs["bc_rd"][up]])
                        fs["sstage_rd"] = bc
                        rc = V_(lambda e, sb=sb: e.reciprocal(rs_sb, bank(sb)),
                                deps=[bc, fs["rs_rd"]])
                        fs["bc_rd"][up] = rc
                        yev = V_(lambda e, pg=pg, q0=q0, yb=yb: e.tensor_tensor(
                            y_sb[:, pg, q0:q0 + 512], bank(yb), rs_sb, ALU.mult),
                            deps=[rc])
                        fs["rs_rd"] = yev
                        fs["y_ev"][up] = yev
                        y_final = yev
                        fs["unit"] += 1
            prev_flash_tail = [fs["exp"][-1], fs["avs"][-1], fs["bc_rd"][0],
                               fs["bc_rd"][1], fs["y_ev"][0], fs["y_ev"][1]]
        attn_done = y_final

        dbg_ops = []
        if dbg:
            dbg_ops.append(G(lambda e: e.dma_start(out=dbg_o["dbg_q"], in_=qd_sb),
               deps=[q_done]))
            dbg_ops.append(G(lambda e: e.dma_start(out=dbg_o["dbg_k"], in_=kd_sb),
               deps=[attn_done]))
            dbg_ops.append(G(lambda e: e.dma_start(out=dbg_o["dbg_v"], in_=v_sb),
               deps=[attn_done]))
            dbg_ops.append(G(lambda e: e.dma_start(out=dbg_o["dbg_y"], in_=y_sb),
               deps=[attn_done]))
            dbg_ops.append(G(lambda e: e.dma_start(out=dbg_o["dbg_z"], in_=z0_sb),
               deps=[z_done]))

        ld_wproj = ldg(wproj_sb, wprojd, deps=list(kv_dep) + dbg_ops)

        def ln_tile(xin, xout, deps):
            V_(lambda e: e.tensor_scalar(stat_sb, xin, 0.0, None, ALU.add,
                                         ALU.add, accum_out=col_sb[:, 0:1]),
               deps=deps)
            V_(lambda e: e.scalar_tensor_tensor(stat_sb, xin, 1.0, xin, ALU.mult,
                                                ALU.mult, accum_out=col_sb[:, 1:2]))
            V_(lambda e: e.tensor_scalar_mul(col_sb[:, 2:3], col_sb[:, 0:1],
                                             1.0 / D))
            V_(lambda e: e.tensor_scalar_mul(col_sb[:, 3:4], col_sb[:, 1:2],
                                             1.0 / D))
            V_(lambda e: e.scalar_tensor_tensor(col_sb[:, 4:5], col_sb[:, 2:3],
                                                -1.0, col_sb[:, 2:3], ALU.mult,
                                                ALU.mult))
            var = V_(lambda e: e.tensor_add(col_sb[:, 5:6], col_sb[:, 3:4],
                                            col_sb[:, 4:5]))
            srt = SC(lambda e: e.activation(col_sb[:, 6:7], col_sb[:, 5:6],
                                            AF.Sqrt, bias=eps_sb),
                     deps=[var, ms_eps])
            rst = V_(lambda e: e.reciprocal(col_sb[:, 7:8], col_sb[:, 6:7]),
                     deps=[srt])
            return V_(lambda e: e.tensor_scalar(xout, xin, col_sb[:, 2:3],
                                                col_sb[:, 7:8], ALU.subtract,
                                                ALU.mult), deps=[rst])

        x1ld = [None, None]
        x1cast = [None, None]
        x1tr = [None, None]
        x1ev = [None, None]
        ldw12 = [None, None]
        w12_rd = [None, None]
        ldh = [None, None, None]
        wbuf_rd = [None, None, None]
        ldw3s = [None, None]
        w3_rd = None
        w3_rd2 = [None, None]
        outw = [None, None]
        addx_bank = {}
        x1f_rd = [None, None]
        fev = [None] * 8
        x1w_all = []
        x1ld2_last = [None]

        def fm_tile(tt, q8, fsub_war, pace=None):
            sl = tt % 2
            x1ld[sl] = ldg(x1f_sb[:, sl, :],
                           x1sp[tt * 128:(tt + 1) * 128, :],
                           deps=[x1w_all[tt], x1cast[sl], x1f_rd[sl], pace])
            x1cast[sl] = V_(lambda e, sl=sl: e.tensor_copy(
                x1b_sb[:, sl, :], x1f_sb[:, sl, :]), deps=[x1ld[sl]])
            for ic in range(8):
                o = T(transpose_op(
                    bankb(6 + sl)[:, ic * 128:(ic + 1) * 128],
                    x1b_sb[:, sl, ic * 128:(ic + 1) * 128], ident_sb),
                    deps=([x1cast[sl], x1ev[sl], w12_rd[0], w12_rd[1],
                           outw[0], outw[1], fsub_war] +
                          ([attn_done] if tt < 2 else [])
                          if ic == 0 else ()))
            x1tr[sl] = o
            x1ev[sl] = V_(lambda e, sl=sl, q8=q8: e.tensor_copy(
                x1fm_sb[:, :, q8 * 128:(q8 + 1) * 128],
                bankb(6 + sl).rearrange("p (a b) -> p a b", a=8)),
                deps=[x1tr[sl]])

        ldo = [None, None]
        addo = [None, None]
        x1w = [None, None]
        for tt in range(16):
            sl = tt % 2
            ldo[sl] = lds(xo_sb[:, sl, :], x_own[tt * 128:(tt + 1) * 128, :],
                          deps=[addo[sl]] + ([attn_done] + dbg_ops if tt < 2 else []))
            for og in range(2):
                for ch in range(8):
                    dps = ([attn_done, ld_wproj, addo[sl]]
                           if og == 0 and ch == 0 else ())
                    o = T(lambda e, ch=ch, og=og, sl=sl, tt=tt:
                          e.matmul(bank(sl * 2 + og),
                                   y_sb[:, ch, tt * 128:(tt + 1) * 128],
                                   wproj_sb[:, ch, og * 512:(og + 1) * 512],
                                   start=(ch == 0), stop=(ch == 7)), deps=dps)
            pmm = o
            addo[sl] = V_(lambda e, sl=sl: e.tensor_add(
                x1p_sb[:, sl, :], bank2(sl * 2), xo_sb[:, sl, :]),
                deps=[pmm, ldo[sl], x1w[sl]])
            lt = ln_tile(x1p_sb[:, sl, :], x1_sb[:, sl, :], [addo[sl], x1w[sl]])
            x1w[sl] = SY(lambda e, sl=sl, tt=tt: e.dma_start(
                out=x1sp[tt * 128:(tt + 1) * 128, :], in_=x1_sb[:, sl, :]),
                deps=[lt])
            x1w_all.append(x1w[sl])
            if dbg:
                SY(lambda e, sl=sl, tt=tt: e.dma_start(
                    out=dbg_o["dbg_x1"][tt * 128:(tt + 1) * 128, :],
                    in_=x1_sb[:, sl, :]), deps=[lt])
            if tt >= 2 and tt < 10:
                fm_tile(tt - 2, tt - 2, None)
        x1fm_done_tg0 = [x1ev[0], x1ev[1]]

        af_war = None
        fsub_rd = None
        dbg_done = False
        for tg in range(2):
            t0 = tg * 1024
            sil = [None, None, None, None]
            aev = [None, None, None, None]
            if tg == 0:
                x1fm_done = x1fm_done_tg0
            else:
                for q8 in range(8):
                    fm_tile(8 + q8, q8, fsub_rd,
                            pace=(x1ld2_last[0] if q8 < 2 else None))
                x1fm_done = [x1ev[0], x1ev[1]]
            for oc in range(32):
                wsl = oc % 2
                ldw12[wsl] = ldg(w12b_sb[:, wsl, :, :, :], w12d[oc],
                                 deps=[w12_rd[wsl], w3_rd] +
                                 ([attn_done, x1w_all[7]]
                                  if tg == 0 and oc < 2 else []) +
                                 ([x1ld2_last[0]]
                                  if tg == 1 and oc < 2 else []))
                for half in range(2):
                    pb = half * 4 + wsl * 2
                    hs = slice(half * 512, (half + 1) * 512)
                    ss = half * 2 + wsl
                    for w_ in range(2):
                        for ic in range(8):
                            dps = ([ldw12[wsl]] + x1fm_done +
                                   [sil[ss], aev[ss]] +
                                   ([addo[0], addo[1]]
                                    if tg == 0 and oc < 2 else []))                                 if (w_ == 0 and ic == 0) else ()
                            o = T(lambda e, w_=w_, ic=ic, pb=pb, wsl=wsl, hs=hs:
                                  e.matmul(bank(pb + w_),
                                           w12b_sb[:, wsl, w_, ic, :],
                                           x1fm_sb[:, ic, hs],
                                           start=(ic == 0), stop=(ic == 7)),
                                  deps=dps)
                    w12_rd[wsl] = o
                    sil[ss] = SC(lambda e, pb=pb, oc=oc, ss=ss:
                                 e.activation(silu_sb[:, ss, :], bank(pb),
                                              AF.Silu,
                                              scale=s1_sb[:, oc:oc + 1]),
                                 deps=[w12_rd[wsl], ld_s1, aev[ss]])
                    aev[ss] = V_(lambda e, pb=pb, oc=oc, ss=ss, hs=hs:
                                 e.scalar_tensor_tensor(
                                     af_sb[:, oc, hs], bank(pb + 1),
                                     s2_sb[:, oc:oc + 1], silu_sb[:, ss, :],
                                     ALU.mult, ALU.mult),
                                 deps=[sil[ss], ld_s2, af_war] +
                                 ([x1w_all[15]]
                                  if tg == 0 and oc == 16 else []))
            a_done = [aev[0], aev[1], aev[2], aev[3]]
            if dbg and tg == 0:
                G(lambda e: e.dma_start(out=dbg_o["dbg_af"], in_=af_sb),
                  deps=a_done)

            def h_layer(src_sb, dst_sb, wsrc, scale_sb, scale_ld, dep_start,
                        ld_gate=()):
                gact = [None, None, None, None]
                for oc in range(32):
                    wsl = oc % 3
                    ldh[wsl] = ldg(wbuf_sb[:, wsl, :, :], wsrc[oc],
                                   deps=[wbuf_rd[wsl]] + list(ld_gate) +
                                   ([outw[0], outw[1]] if wsl == 2 else []))
                    for half in range(2):
                        pb = (oc * 2 + half) % 4
                        hs = slice(half * 512, (half + 1) * 512)
                        for rc in range(32):
                            dps = ([ldh[wsl], gact[pb]] + dep_start)                                 if rc == 0 else ()
                            o = T(lambda e, rc=rc, pb=pb, wsl=wsl, hs=hs:
                                  e.matmul(bank(pb), wbuf_sb[:, wsl, rc, :],
                                           src_sb[:, rc, hs],
                                           start=(rc == 0), stop=(rc == 31)),
                                  deps=dps)
                        wbuf_rd[wsl] = o
                        gact[pb] = SC(lambda e, oc=oc, pb=pb, hs=hs:
                                      e.activation(dst_sb[:, oc, hs], bank(pb),
                                                   AF.Gelu,
                                                   scale=scale_sb[:, oc:oc + 1]),
                                      deps=[wbuf_rd[wsl], scale_ld])
                return [gact[0], gact[1], gact[2], gact[3]]

            g_done = h_layer(af_sb, g_sb, h1wd, h1s_sb, ld_h1s,
                             a_done[2:],
                             ld_gate=([x1w[0], x1w[1]] if tg == 0 else ()))
            g2_done = h_layer(g_sb, af_sb, h2wd, h2s_sb, ld_h2s, g_done)

            if dbg and not dbg_done:
                G(lambda e: e.dma_start(out=dbg_o["dbg_a"], in_=g_sb),
                  deps=g_done)
                G(lambda e: e.dma_start(out=dbg_o["dbg_g"], in_=af_sb),
                  deps=g2_done)
                dbg_done = True

            for half in range(2):
                th0 = t0 + half * 512
                hs = slice(half * 512, (half + 1) * 512)
                for oc in range(8):
                    pb = oc % 2
                    wb3 = w3b_sb if oc % 2 == 0 else w3b2_sb
                    ldw3s[oc % 2] = ldg(wb3, w3wd[oc],
                                        deps=[w3_rd2[oc % 2], a_done[3],
                                              w12_rd[0], w12_rd[1]] +
                                        ([attn_done, x1w[0], x1w[1]]
                                         if tg == 0 and half == 0 and oc == 0
                                         else []))
                    for rc in range(32):
                        dps = ([ldw3s[oc % 2],
                                fev[oc - 2] if oc >= 2 else None,
                                fev[oc - 2 + 8] if oc < 2 else None] + g2_done) \
                            if rc == 0 else ()
                        o = T(lambda e, rc=rc, pb=pb, hs=hs, wb3=wb3:
                              e.matmul(bank(pb), wb3[:, rc, :],
                                       af_sb[:, rc, hs],
                                       start=(rc == 0), stop=(rc == 31)),
                              deps=dps)
                    w3_rd2[oc % 2] = o
                    w3_rd = o
                    fev[oc] = V_(lambda e, oc=oc, pb=pb: e.tensor_scalar_mul(
                        fsub_sb[:, oc, :], bank(pb), w3s_sb[:, oc:oc + 1]),
                        deps=[w3_rd, ld_w3s, fsub_rd])
                for q in range(4):
                    tt = th0 // 128 + q
                    sl = tt % 2
                    x1ld2 = lds(x1f_sb[:, sl, :],
                                x1sp[tt * 128:(tt + 1) * 128, :],
                                deps=[x1cast[sl], x1f_rd[sl], outw[sl]])
                    x1ld2_last[0] = x1ld2
                    xb = 4 if q % 2 == 0 else 6
                    for oc in range(8):
                        o = T(transpose_op(
                            bank2(xb)[:, oc * 128:(oc + 1) * 128],
                            fsub_sb[:, oc, q * 128:(q + 1) * 128], identf_sb),
                            deps=([fev[7], ld_identf, addx_bank.get(xb)]
                                  if oc == 0 else ()))
                    ftr = o
                    if q == 3:
                        fsub_rd = ftr
                    addx = V_(lambda e, sl=sl, xb=xb: e.tensor_add(
                        x1f_sb[:, sl, :], bank2(xb), x1f_sb[:, sl, :]),
                        deps=[ftr, x1ld2])
                    addx_bank[xb] = addx
                    lt2 = ln_tile(x1f_sb[:, sl, :], x2_sb[:, sl, :],
                                  [addx, outw[sl]])
                    x1f_rd[sl] = lt2
                    outw[sl] = SY(lambda e, sl=sl, tt=tt: e.dma_start(
                        out=out[tt * 128:(tt + 1) * 128, :], in_=x2_sb[:, sl, :]),
                        deps=[lt2])
            af_war = w3_rd

        P.add("sync", lambda e: e.nop(), deps=[outw[0], outw[1]])

        block = es.enter_context(nc.Block())
        P.emit(nc, block, es)
    return nc


def _host_prep(inputs):
    f32 = np.float32
    Wq = np.asarray(inputs["L0_attn_Wq"], f32)
    Wk = np.asarray(inputs["L0_attn_Wk"], f32)
    Wv = np.asarray(inputs["L0_attn_Wv"], f32)
    kdec = np.asarray(inputs["L0_attn_key_decoder"], f32)
    vdec = np.asarray(inputs["L0_attn_value_decoder"], f32)
    Wproj = np.asarray(inputs["L0_attn_Wproj"], f32)
    w1w = np.asarray(inputs["L0_ffn_w1_w"], f32)
    w2w = np.asarray(inputs["L0_ffn_w2_w"], f32)
    h1w = np.asarray(inputs["L0_ffn_h1_w"], f32)
    h2w = np.asarray(inputs["L0_ffn_h2_w"], f32)
    w3w = np.asarray(inputs["L0_ffn_w3_w"], f32)

    com = {}
    hh, cc, ff = np.meshgrid(np.arange(H), np.arange(2), np.arange(NF),
                             indexing="ij")
    qperm = (64 * hh + 2 * ff + cc).reshape(-1)
    com["wq"] = np.ascontiguousarray(Wq[:, qperm].reshape(8, 128, D)
                                     .transpose(1, 0, 2))
    ffk, rrk = np.meshgrid(np.arange(NF), np.arange(KR), indexing="ij")
    kperm = np.concatenate([(6 * ffk + 2 * rrk + 0).reshape(-1),
                            (6 * ffk + 2 * rrk + 1).reshape(-1)])
    com["wk"] = np.ascontiguousarray(Wk[:, kperm].reshape(8, 128, VR)
                                     .transpose(1, 0, 2))
    com["wv"] = np.ascontiguousarray(Wv.reshape(8, 128, VR).transpose(1, 0, 2))
    Dm = np.zeros((VR, D), f32)
    br, bi = kdec[..., 0], kdec[..., 1]
    fi, hi, ri = np.meshgrid(np.arange(NF), np.arange(H), np.arange(KR),
                             indexing="ij")
    for c, top, bot in ((0, br, bi), (1, bi, br)):
        Dm[3 * fi + ri, 64 * hi + 32 * c + fi] = top
        Dm[96 + 3 * fi + ri, 64 * hi + 32 * c + fi] = bot
    com["d0"] = np.ascontiguousarray(Dm[:128])
    com["d1"] = np.ascontiguousarray(Dm[128:])
    VD = np.ascontiguousarray(vdec.transpose(1, 0, 2).reshape(VR, D))
    com["vd0"] = np.ascontiguousarray(VD[:128])
    com["vd1"] = np.ascontiguousarray(VD[128:])
    com["wproj"] = np.ascontiguousarray(Wproj.reshape(8, 128, D)
                                        .transpose(1, 0, 2))
    w1s_ = w1w.reshape(32, 128, 8, 128).transpose(0, 3, 2, 1)
    w2s_ = w2w.reshape(32, 128, 8, 128).transpose(0, 3, 2, 1)
    com["w12"] = np.ascontiguousarray(np.stack([w1s_, w2s_], 2))
    com["h1w"] = np.ascontiguousarray(h1w.reshape(32, 128, 32, 128)
                                      .transpose(0, 3, 2, 1))
    com["h2w"] = np.ascontiguousarray(h2w.reshape(32, 128, 32, 128)
                                      .transpose(0, 3, 2, 1))
    com["w3w"] = np.ascontiguousarray(w3w.reshape(8, 128, 32, 128)
                                      .transpose(0, 3, 2, 1))
    com["s1s"] = np.ascontiguousarray(
        np.asarray(inputs["L0_ffn_w1_s"], f32).reshape(32, 128).T)
    com["s2s"] = np.ascontiguousarray(
        np.asarray(inputs["L0_ffn_w2_s"], f32).reshape(32, 128).T)
    com["h1s"] = np.ascontiguousarray(
        np.asarray(inputs["L0_ffn_h1_s"], f32).reshape(32, 128).T)
    com["h2s"] = np.ascontiguousarray(
        np.asarray(inputs["L0_ffn_h2_s"], f32).reshape(32, 128).T)
    com["w3s"] = np.ascontiguousarray(
        np.asarray(inputs["L0_ffn_w3_s"], f32).reshape(8, 128).T)
    com["ident"] = np.eye(128, dtype=f32)
    sel = np.zeros((33, 128), f32)
    sel[0, 0:64] = 1.0
    sel[32, 64:128] = 1.0
    com["sel33"] = sel
    com["onesb"] = np.ones((128, 1), f32)
    half = HD // 2
    freq = (1.0 / ROPE_BASE ** (np.arange(half, dtype=f32) / half)).astype(f32)
    freqs = np.outer(np.arange(S, dtype=f32), freq)
    COS = np.cos(freqs).astype(f32)
    SIN = np.sin(freqs).astype(f32)
    com["ck"] = np.ascontiguousarray(COS.reshape(32, 128, 32).transpose(1, 0, 2))
    com["sk"] = np.ascontiguousarray(SIN.reshape(32, 128, 32).transpose(1, 0, 2))
    return com, COS, SIN


def _core_inputs(inputs, com, COS, SIN, core):
    b, hlf = core // 2, core % 2
    x = np.asarray(inputs["L0_attn_x"], np.float32)
    groups = [2 * j + 1 - hlf for j in range(4)]
    rows = np.concatenate([np.arange(g * 512, (g + 1) * 512) for g in groups])
    d = dict(com)
    d["x_own"] = np.ascontiguousarray(x[b][rows])
    d["x_full"] = np.ascontiguousarray(x[b])
    d["cq"] = np.ascontiguousarray(COS[rows].reshape(16, 128, 32)
                                   .transpose(1, 0, 2))
    d["sq"] = np.ascontiguousarray(SIN[rows].reshape(16, 128, 32)
                                   .transpose(1, 0, 2))
    m = np.zeros((4, NMSK, 128, 512), np.float32)
    for j in range(4):
        q0 = groups[j] * 512
        for s_ in range(NMSK):
            k0 = (CNT[j] - NMSK + s_) * 128
            kk = k0 + np.arange(128)[:, None]
            qq = q0 + np.arange(512)[None, :]
            m[j, s_] = (kk <= qq).astype(np.float32)
    d["masks"] = m
    return d, groups


def kernel(**inputs):
    com, COS, SIN = _host_prep(inputs)
    in_maps, group_list = [], []
    for c in range(8):
        d, groups = _core_inputs(inputs, com, COS, SIN, c)
        in_maps.append(d)
        group_list.append(groups)
    nc = build_nc()
    res = run_bass_kernel_spmd(nc, in_maps, core_ids=list(range(8)))
    outa = np.empty((BSZ, S, D), np.float32)
    for c in range(8):
        b = c // 2
        shard = np.asarray(res.results[c]["out"], np.float32).reshape(TQ, D)
        for j, g in enumerate(group_list[c]):
            outa[b, g * 512:(g + 1) * 512] = shard[j * 512:(j + 1) * 512]
    return outa


# version 16
# speedup vs baseline: 1.0160x; 1.0015x over previous
import contextlib
import numpy as np
import concourse.bass as bass
import concourse.mybir as mybir
from concourse.bass_utils import run_bass_kernel_spmd

F32 = mybir.dt.float32
BF16 = mybir.dt.bfloat16
AF = mybir.ActivationFunctionType
ALU = mybir.AluOpType

S, D, BSZ, H, HD, NF, KR, VR, DH = 4096, 1024, 4, 16, 64, 32, 3, 192, 4096
TQ = 2048
CNT = (8, 16, 24, 32)
NMSK = 8
EXPB = 12.0
LNEPS = 1e-5
ROPE_BASE = 10000.0
NHG = 4
NTG = 4

ENGS = ("tensor", "vector", "scalar", "gpsimd", "sync")


class Prog:

    def __init__(self):
        self.ops = []
        self.per_eng = {e: [] for e in ENGS}

    def add(self, eng, fn, deps=(), dma=False):
        i = len(self.ops)
        self.ops.append((eng, fn, tuple(d for d in deps if d is not None), dma))
        self.per_eng[eng].append(i)
        return i

    def emit(self, nc, block, es):
        need = set()
        for i, (eng, fn, deps, dma) in enumerate(self.ops):
            if dma:
                need.add(i)
            if eng in ("vector", "scalar"):
                need.add(i)
            for d in deps:
                pe, _, _, pdma = self.ops[d]
                if pdma or pe != eng:
                    need.add(d)
        sems = {e: es.enter_context(nc.semaphore("S" + e)) for e in ENGS}
        for e in ENGS:
            for s in range(3):
                sems[(e, s)] = es.enter_context(nc.semaphore(f"S{e}{s}"))
        val, cnt = {}, {e: 0 for e in ENGS}
        for e in ENGS:
            for i in self.per_eng[e]:
                if i in need and not self.ops[i][3]:
                    cnt[e] += 1
                    val[i] = cnt[e]
        waits = []
        for i, (eng, fn, deps, dma) in enumerate(self.ops):
            w = {}
            for d in deps:
                if d in val and not self.ops[d][3]:
                    pe = self.ops[d][0]
                    if pe != eng:
                        w[pe] = max(w.get(pe, 0), val[d])
            if eng in ("vector", "scalar") and val.get(i, 1) > 1:
                w[eng] = max(w.get(eng, 0), val[i] - 1)
            waits.append(w)
        B, R = 2, 3
        self.stripe_of = {}
        self.stripe_val = {}
        stripe_gate = {}
        for e in ENGS:
            dmas = [i for i in self.per_eng[e] if self.ops[i][3]]
            epochs, cur = [], []
            for i in dmas:
                boundary = len(cur) >= B
                if not boundary and cur:
                    start = cur[0]
                    if any(d >= start for d in self.ops[i][2]):
                        boundary = True
                if boundary and cur:
                    epochs.append(cur)
                    cur = []
                cur.append(i)
            if cur:
                epochs.append(cur)
            stot = [0] * R
            for k, ep in enumerate(epochs):
                s = k % R
                if stot[s] > 0:
                    stripe_gate[ep[0]] = (s, stot[s])
                stot[s] += 16 * len(ep)
                for i in ep:
                    self.stripe_of[i] = (e, s)
                    self.stripe_val[i] = stot[s]
        self._stripe_gate = stripe_gate
        waits2 = []
        for i, (eng, fn, deps, dma) in enumerate(self.ops):
            w = {}
            for pe, v in waits[i].items():
                w[pe] = max(w.get(pe, 0), v)
            for d in deps:
                if d in self.stripe_of:
                    pe, s = self.stripe_of[d]
                    key = (pe, s)
                    w[key] = max(w.get(key, 0), self.stripe_val[d])
            if i in self._stripe_gate:
                s, v = self._stripe_gate[i]
                key = (eng, s)
                w[key] = max(w.get(key, 0), v)
            waits2.append(w)
        waits = waits2
        seen = {e: {} for e in ENGS}
        for e in ENGS:
            for i in self.per_eng[e]:
                w = waits[i]
                for pe in list(w):
                    if seen[e].get(pe, 0) >= w[pe]:
                        del w[pe]
                    else:
                        seen[e][pe] = w[pe]
        for e in ENGS:
            def body(eng, _e=e):
                for i in self.per_eng[_e]:
                    _, fn, _, dma = self.ops[i]
                    for pe, v in sorted(waits[i].items(), key=str):
                        eng.wait_ge(sems[pe], v)
                    ins = fn(eng)
                    if dma:
                        ins.then_inc(sems[self.stripe_of[i]], 16)
                    elif i in val:
                        ins.then_inc(sems[_e], 1)
            getattr(block, e)(body)


class Carver:
    def __init__(self, pool_ap, limit, off=0):
        self.pool = pool_ap
        self.off = off
        self.limit = limit

    def take(self, shape, dt):
        p = shape[0]
        elems = int(np.prod(shape[1:]))
        width = elems * (2 if dt == F32 else 1)
        self.off = (self.off + 1) & ~1
        a = self.pool[0:p, self.off:self.off + width]
        self.off += width
        assert self.off <= self.limit, f"sbuf pool overflow {self.off} > {self.limit}"
        if dt == F32:
            a = a.bitcast(F32)
        if len(shape) == 3:
            a = a.rearrange("p (a b) -> p a b", a=shape[1])
        elif len(shape) == 4:
            a = a.rearrange("p (a b c) -> p a b c", a=shape[1], b=shape[2])
        elif len(shape) == 5:
            a = a.rearrange("p (a b c d) -> p a b c d", a=shape[1], b=shape[2],
                            c=shape[3])
        else:
            assert len(shape) == 2
        return a


def build_nc(dbg=False):
    nc = bass.Bass("TRN2", target_bir_lowering=False,
                   detect_race_conditions=False)

    def din(name, shape):
        return nc.dram_tensor(name, shape, F32, kind="ExternalInput").ap()

    x_own = din("x_own", [TQ, D])
    x_full = din("x_full", [S, D])
    cqd, sqd = din("cq", [128, 16, 32]), din("sq", [128, 16, 32])
    ckd, skd = din("ck", [128, 32, 32]), din("sk", [128, 32, 32])
    masksd = din("masks", [4, NMSK, 128, 512])
    wqd = din("wq", [128, 8, D])
    wkd = din("wk", [128, 8, VR])
    wvd = din("wv", [128, 8, VR])
    d0d, d1d = din("d0", [128, D]), din("d1", [64, D])
    vd0d, vd1d = din("vd0", [128, D]), din("vd1", [64, D])
    wprojd = din("wproj", [128, 8, D])
    w12d = din("w12", [32, 128, 2, 8, 128])
    h1wd = din("h1w", [32, 128, 32, 128])
    h2wd = din("h2w", [32, 128, 32, 128])
    w3wd = din("w3w", [8, 128, 32, 128])
    s1d, s2d = din("s1s", [128, 32]), din("s2s", [128, 32])
    h1sd, h2sd = din("h1s", [128, 32]), din("h2s", [128, 32])
    w3sd = din("w3s", [128, 8])
    identd = din("ident", [128, 128])
    sel33d = din("sel33", [33, 128])
    onesd = din("onesb", [128, 1])

    out = nc.dram_tensor("out", [TQ, D], F32, kind="ExternalOutput").ap()
    x1sp = nc.dram_tensor("x1sp", [TQ, D], F32, kind="ExternalOutput").ap()
    dbg_o = {}
    if dbg:
        for nm, shp in (("dbg_q", [128, 8, TQ]), ("dbg_k", [128, 2, S]),
                        ("dbg_v", [128, 32, 256]), ("dbg_y", [128, 8, TQ]),
                        ("dbg_x1", [TQ, D]), ("dbg_z", [128, S]),
                        ("dbg_a", [128, 32, 1024]), ("dbg_g", [128, 32, 1024]),
                        ("dbg_af", [128, 32, 1024])):
            dbg_o[nm] = nc.dram_tensor(nm, shp, F32, kind="ExternalOutput").ap()

    P = Prog()

    def G(fn, deps=()):
        return P.add("gpsimd", fn, deps, dma=True)

    def SY(fn, deps=()):
        return P.add("sync", fn, deps, dma=True)

    def T(fn, deps=()):
        return P.add("tensor", fn, deps)

    def V_(fn, deps=()):
        return P.add("vector", fn, deps)

    def SC(fn, deps=()):
        return P.add("scalar", fn, deps)

    def ldg(dst, src, deps=()):
        return G(lambda e, a=dst, b=src: e.dma_start(out=a, in_=b), deps)

    def lds(dst, src, deps=()):
        return SY(lambda e, a=dst, b=src: e.dma_start(out=a, in_=b), deps)

    es = contextlib.ExitStack()
    with es:
        POOLW = 106416
        pool = es.enter_context(nc.sbuf_tensor("pool", [128, POOLW], BF16)).ap()
        psum = es.enter_context(nc.psum_tensor("psum", [128, 8, 512], F32)).ap()

        def bank(b):
            return psum[:, b, :]

        def bank2(b):
            return psum[:, b:b + 2, :].rearrange("p a b -> p (a b)")

        def bankb(b):
            return psum[:, b, :].bitcast(BF16)

        C = Carver(pool, POOLW)
        ident_sb = C.take([128, 128], BF16)
        identf_sb = C.take([128, 128], F32)
        eps_sb = C.take([128, 1], F32)
        s1_sb = C.take([128, 32], F32)
        s2_sb = C.take([128, 32], F32)
        h1s_sb = C.take([128, 32], F32)
        h2s_sb = C.take([128, 32], F32)
        w3s_sb = C.take([128, 8], F32)
        col_sb = C.take([128, 8], F32)
        a_live_end = C.off
        ck_sb = C.take([128, 32, 32], F32)
        sk_sb = C.take([128, 32, 32], F32)
        cq_sb = C.take([128, 16, 32], F32)
        sq_sb = C.take([128, 16, 32], F32)
        mask_sb = C.take([128, NMSK, 512], BF16)
        sel_sb = C.take([33, 128], F32)
        ones_sb = C.take([128, 1], BF16)
        bias_sb = C.take([128, 1], F32)
        pt_sb = C.take([128, 4, 1024], BF16)
        sstage_sb = C.take([33, 512], F32)
        lnr_sb = C.take([128, 512], F32)
        rs_sb = C.take([128, 512], F32)
        wk_sb = C.take([128, 8, VR], BF16)
        wv_sb = C.take([128, 8, VR], BF16)
        d0_sb = C.take([128, D], BF16)
        d1_sb = C.take([64, D], BF16)
        vd0_sb = C.take([128, D], BF16)
        vd1_sb = C.take([64, D], BF16)
        a_end = C.off
        qd_sb = C.take([128, 8, TQ], BF16)
        y_sb = C.take([128, 8, TQ], BF16)
        bc_end = C.off
        z0_sb = C.take([128, S], BF16)
        z1_sb = C.take([64, S], BF16)
        xv0_sb = C.take([128, S], BF16)
        xv1_sb = C.take([64, S], BF16)
        kd_off = C.off
        kd_sb = C.take([128, 2, S], BF16)
        v_sb = C.take([128, 32, 256], BF16)
        d_stream = C.off
        wq_sb = Carver(pool, d_stream, off=kd_off).take([128, 8, D], BF16)
        xtm_sb = C.take([128, 2, D], BF16)
        xfm_sb = C.take([128, 2, 8, 128], BF16)
        qtm_sb = C.take([128, 2, D], BF16)
        ktm_sb = C.take([128, 2, 512], BF16)
        rsc1_sb = C.take([128, 512], F32)
        rsc2_sb = C.take([128, 512], F32)
        attn_peak = C.off
        wproj_sb = Carver(pool, bc_end + 8192, off=bc_end).take([128, 8, D],
                                                               BF16)
        CP = Carver(pool, POOLW, off=kd_off)
        xo_sb = CP.take([128, 2, D], F32)
        x1p_sb = CP.take([128, 2, D], F32)
        x1_sb = CP.take([128, 2, D], F32)
        CF1 = Carver(pool, a_end, off=a_live_end)
        stat_sb = CF1.take([128, D], F32)
        wbuf_off = CF1.off
        wbuf_sb = CF1.take([128, 3, 32, 128], BF16)
        w12b_off = CF1.off
        w12b_sb = CF1.take([128, 2, 2, 8, 128], BF16)
        sw3_off = CF1.off
        silu_sb = CF1.take([128, 4, 512], F32)
        x1b_sb = Carver(pool, a_end, off=wbuf_off).take([128, 2, D], BF16)
        CF2 = Carver(pool, bc_end, off=a_end)
        g_sb = CF2.take([128, 32, 1024], BF16)
        w3b_sb = Carver(pool, a_end, off=sw3_off).take([128, 32, 128], BF16)
        CF3 = Carver(pool, POOLW, off=bc_end)
        af_sb = CF3.take([128, 32, 1024], BF16)
        x1fm_off = CF3.off
        x1fm_sb = CF3.take([128, 8, 1024], BF16)
        x1f_sb = CF3.take([128, 2, D], F32)
        ffn_peak = CF3.off
        fsub_sb = Carver(pool, POOLW, off=x1fm_off).take([128, 8, 512], F32)
        x2_sb = Carver(pool, a_end,
                       off=wbuf_off + 2 * 4096).take([128, 2, D], F32)
        w3b2_sb = Carver(pool, a_end, off=w12b_off).take([128, 32, 128], BF16)

        ld_wk = ldg(wk_sb, wkd)
        ld_wv = ldg(wv_sb, wvd)
        ld_d0 = ldg(d0_sb, d0d)
        ld_d1 = ldg(d1_sb, d1d)
        ld_vd0 = ldg(vd0_sb, vd0d)
        ld_vd1 = ldg(vd1_sb, vd1d)
        ld_ident = ldg(ident_sb, identd)
        ld_ones = ldg(ones_sb, onesd)
        ld_wq = ldg(wq_sb, wqd)
        ld_ck = lds(ck_sb, ckd)
        ld_sk = lds(sk_sb, skd)
        ld_cq = lds(cq_sb, cqd)
        ld_sq = lds(sq_sb, sqd)
        ld_identf = lds(identf_sb, identd)
        ld_sel = lds(sel_sb, sel33d)
        ld_s1 = lds(s1_sb, s1d)
        ld_s2 = lds(s2_sb, s2d)
        ld_h1s = lds(h1s_sb, h1sd)
        ld_h2s = lds(h2s_sb, h2sd)
        ld_w3s = lds(w3s_sb, w3sd)
        ms_bias = V_(lambda e: e.memset(bias_sb, -EXPB))
        ms_sst = V_(lambda e: e.memset(sstage_sb, 0.0))
        ms_eps = V_(lambda e: e.memset(eps_sb, LNEPS))

        def reim(ap2, nh):
            return ap2.rearrange("p (h c f) -> p h c f", h=nh, c=2)

        neg0 = V_(lambda e: e.tensor_scalar_mul(
            reim(d0_sb[96:128, :], 16)[:, :, 0, :],
            reim(d0_sb[96:128, :], 16)[:, :, 0, :], -1.0), deps=[ld_d0])
        neg1 = V_(lambda e: e.tensor_scalar_mul(
            reim(d1_sb, 16)[:, :, 0, :],
            reim(d1_sb, 16)[:, :, 0, :], -1.0), deps=[ld_d1])

        def transpose_op(dst, src, idn):
            return lambda e, a=dst, b=src, c=idn: e.transpose(a, b, c)

        ldx = [None, None]
        tr_last = [None, None]
        ev_fm = [None, None]
        zx_ev = [None, None]
        mmz = None
        for i1 in range(33):
            if i1 < 32:
                kt = i1
                sl = kt % 2
                ldx[sl] = ldg(xtm_sb[:, sl, :],
                              x_full[kt * 128:(kt + 1) * 128, :],
                              deps=[tr_last[sl]])
                for ic in range(8):
                    o = T(transpose_op(
                        bankb(6 + sl)[:, ic * 128:(ic + 1) * 128],
                        xtm_sb[:, sl, ic * 128:(ic + 1) * 128], ident_sb),
                        deps=([ldx[sl], ld_ident, ev_fm[sl]] if ic == 0 else ()))
                tr_last[sl] = o
                ev_fm[sl] = V_(lambda e, sl=sl: e.tensor_copy(
                    xfm_sb[:, sl, :, :],
                    bankb(6 + sl).rearrange("p (a b) -> p a b", a=8)),
                    deps=[tr_last[sl]])
            if i1 >= 1:
                kt = i1 - 1
                sl = kt % 2
                zx = bank(sl).rearrange("p (a b) -> p a b", a=4)
                for ic in range(8):
                    st_, sp = (ic == 0), (ic == 7)
                    xf = xfm_sb[:, sl, ic, :]
                    dps = [ev_fm[sl], ld_wk, ld_wv, zx_ev[sl]] if ic == 0 else ()
                    T(lambda e, w=wk_sb[:, ic, 0:128], x=xf, s=st_, zx=zx:
                      e.matmul(zx[:, 0, :], w, x, start=s, stop=False), deps=dps)
                    T(lambda e, w=wk_sb[:, ic, 128:192], x=xf, zx=zx:
                      e.matmul(zx[0:64, 1, :], w, x, start=False, stop=False))
                    T(lambda e, w=wv_sb[:, ic, 128:192], x=xf, zx=zx:
                      e.matmul(zx[0:64, 3, :], w, x, start=False, stop=False))
                    mmz = T(lambda e, w=wv_sb[:, ic, 0:128], x=xf, p=sp, zx=zx:
                            e.matmul(zx[:, 2, :], w, x, start=False, stop=p))
                V_(lambda e, kt=kt, zx=zx: e.tensor_copy(
                    z0_sb[:, kt * 128:(kt + 1) * 128], zx[:, 0, :]), deps=[mmz])
                V_(lambda e, kt=kt, zx=zx: e.tensor_copy(
                    z1_sb[:, kt * 128:(kt + 1) * 128], zx[0:64, 1, :]))
                V_(lambda e, kt=kt, zx=zx: e.tensor_copy(
                    xv0_sb[:, kt * 128:(kt + 1) * 128], zx[:, 2, :]))
                zx_ev[sl] = V_(lambda e, kt=kt, zx=zx: e.tensor_copy(
                    xv1_sb[:, kt * 128:(kt + 1) * 128], zx[0:64, 3, :]))
        z_done = zx_ev[31 % 2]


        def rope_ops(ps_tm, nh, ctab, stab, out_tm, deps):
            src, dst = reim(ps_tm, nh), reim(out_tm, nh)
            cb = ctab.unsqueeze(1).broadcast_to([128, nh, 32])
            st = stab.unsqueeze(1).broadcast_to([128, nh, 32])
            r1 = rsc1_sb[:, 0:nh * 32].rearrange("p (h f) -> p h f", h=nh)
            r2 = rsc2_sb[:, 0:nh * 32].rearrange("p (h f) -> p h f", h=nh)
            V_(lambda e: e.tensor_tensor(r1, src[:, :, 0, :], cb, ALU.mult),
               deps=deps)
            V_(lambda e: e.tensor_tensor(r2, src[:, :, 1, :], st, ALU.mult))
            V_(lambda e: e.tensor_sub(dst[:, :, 0, :], r1, r2))
            V_(lambda e: e.tensor_tensor(r1, src[:, :, 0, :], st, ALU.mult))
            V_(lambda e: e.tensor_tensor(r2, src[:, :, 1, :], cb, ALU.mult))
            return V_(lambda e: e.tensor_add(dst[:, :, 1, :], r1, r2))

        ldq = [None, None]
        ev1 = [None, None]
        qd_ev = [None, None]
        rope_done = [None, None]
        qtr = [None, None]
        qmm = [None, None]
        for i2 in range(18):
            if i2 < 16:
                tt = i2
                sl = tt % 2
                ldq[sl] = ldg(xtm_sb[:, sl, :],
                              x_own[tt * 128:(tt + 1) * 128, :],
                              deps=[tr_last[sl]])
                for ic in range(8):
                    o = T(transpose_op(
                        bankb(6 + sl)[:, ic * 128:(ic + 1) * 128],
                        xtm_sb[:, sl, ic * 128:(ic + 1) * 128], ident_sb),
                        deps=([ldq[sl], ev_fm[sl] if tt < 2 else ev1[sl]]
                              if ic == 0 else ()))
                tr_last[sl] = o
                ev1[sl] = V_(lambda e, sl=sl: e.tensor_copy(
                    xfm_sb[:, sl, :, :],
                    bankb(6 + sl).rearrange("p (a b) -> p a b", a=8)),
                    deps=[tr_last[sl]])
            if 1 <= i2 <= 16:
                tt = i2 - 1
                sl = tt % 2
                qb = sl * 2
                for og in range(2):
                    for ic in range(8):
                        dps = ([ev1[sl], ld_wq, rope_done[sl], zx_ev[sl]]
                               if (og == 0 and ic == 0) else ())
                        o = T(lambda e, og=og, ic=ic, sl=sl, qb=qb:
                              e.matmul(bank(qb + og), xfm_sb[:, sl, ic, :],
                                       wq_sb[:, ic, og * 512:(og + 1) * 512],
                                       start=(ic == 0), stop=(ic == 7)), deps=dps)
                qmm[sl] = o
                rope_done[sl] = rope_ops(
                    bank2(qb), 16, cq_sb[:, tt, :], sq_sb[:, tt, :],
                    qtm_sb[:, sl, :],
                    deps=[qmm[sl], ld_cq, ld_sq, qtr[sl]])
            if i2 >= 2:
                tt = i2 - 2
                sl = tt % 2
                for p in range(8):
                    o = T(transpose_op(
                        bankb(4 + sl)[:, p * 128:(p + 1) * 128],
                        qtm_sb[:, sl, p * 128:(p + 1) * 128], ident_sb),
                        deps=([rope_done[sl], qd_ev[sl]] if p == 0 else ()))
                qtr[sl] = o
                qd_ev[sl] = V_(lambda e, sl=sl, tt=tt: e.tensor_copy(
                    qd_sb[:, :, tt * 128:(tt + 1) * 128],
                    bankb(4 + sl).rearrange("p (a b) -> p a b", a=8)),
                    deps=[qtr[sl]])
        q_done = qd_ev[15 % 2]
        q_done2 = qd_ev[14 % 2]

        fs = dict(it=0, exp=[], avs=[], y_ev=[None, None],
                  sstage_rd=None, bc_rd=[None, None], rs_rd=None, unit=0)
        y_final = None
        prev_flash_tail = [q_done, q_done2, rope_done[0], rope_done[1]]

        mask_rd = None
        for hg in range(NHG):
            dcol = slice(hg * 256, hg * 256 + 256)
            ropek = [None, None]
            trk = [None, None]
            evk = [None, None]
            vev = [None, None]
            kmm = [None, None]
            vmm = [None, None]

            def rope_pair(ps2, pi, out2, deps):
                src4 = ps2.rearrange("p t (h c f) -> p t h c f", h=4, c=2)
                dst4 = out2.rearrange("p t (h c f) -> p t h c f", h=4, c=2)
                cb = ck_sb[:, 2 * pi:2 * pi + 2, :].unsqueeze(2) \
                    .broadcast_to([128, 2, 4, 32])
                st = sk_sb[:, 2 * pi:2 * pi + 2, :].unsqueeze(2) \
                    .broadcast_to([128, 2, 4, 32])
                r1 = rsc1_sb[:, 0:256].rearrange("p (t h f) -> p t h f", t=2, h=4)
                r2 = rsc2_sb[:, 0:256].rearrange("p (t h f) -> p t h f", t=2, h=4)
                V_(lambda e: e.tensor_tensor(r1, src4[:, :, :, 0, :], cb,
                                             ALU.mult), deps=deps)
                V_(lambda e: e.tensor_tensor(r2, src4[:, :, :, 1, :], st,
                                             ALU.mult))
                V_(lambda e: e.tensor_sub(dst4[:, :, :, 0, :], r1, r2))
                V_(lambda e: e.tensor_tensor(r1, src4[:, :, :, 0, :], st,
                                             ALU.mult))
                V_(lambda e: e.tensor_tensor(r2, src4[:, :, :, 1, :], cb,
                                             ALU.mult))
                return V_(lambda e: e.tensor_add(dst4[:, :, :, 1, :], r1, r2))

            for pi in range(17):
                if pi < 16:
                    sl = pi % 2
                    b0 = sl * 2
                    for tl in range(2):
                        kt = 2 * pi + tl
                        dps = ([z_done, neg0, neg1, ld_ck, ld_sk, ropek[sl]] +
                               prev_flash_tail if pi == 0 else
                               ([ropek[sl]] if tl == 0 else []))
                        T(lambda e, kt=kt, b=b0 + tl, dc=dcol:
                          e.matmul(bank(b)[:, 0:256],
                                   z0_sb[:, kt * 128:(kt + 1) * 128],
                                   d0_sb[:, dc], start=True, stop=False),
                          deps=dps)
                        kmm[sl] = T(lambda e, kt=kt, b=b0 + tl, dc=dcol:
                                    e.matmul(bank(b)[:, 0:256],
                                             z1_sb[:, kt * 128:(kt + 1) * 128],
                                             d1_sb[:, dc], start=False,
                                             stop=True))
                        vdps = ([z_done, ld_vd0, ld_vd1, vev[kt % 2]] +
                                (prev_flash_tail if pi == 0 else []))
                        vmm[kt % 2] = T(lambda e, kt=kt, dc=dcol:
                                        e.matmul(bank(4 + kt % 2)[:, 0:256],
                                                 xv0_sb[:, kt * 128:(kt + 1) * 128],
                                                 vd0_sb[:, dc], start=True,
                                                 stop=False), deps=vdps)
                        vmm[kt % 2] = T(lambda e, kt=kt, dc=dcol:
                                        e.matmul(bank(4 + kt % 2)[:, 0:256],
                                                 xv1_sb[:, kt * 128:(kt + 1) * 128],
                                                 vd1_sb[:, dc], start=False,
                                                 stop=True))
                    ropek[sl] = rope_pair(
                        psum[:, b0:b0 + 2, 0:256], pi, ktm_sb[:, sl, :]
                        .rearrange("p (t c) -> p t c", t=2),
                        deps=[kmm[sl], trk[sl]])
                    for tl in range(2):
                        kt = 2 * pi + tl
                        vev[kt % 2] = V_(lambda e, kt=kt: e.tensor_copy(
                            v_sb[:, kt, :], bank(4 + kt % 2)[:, 0:256]),
                            deps=[vmm[kt % 2]])
                if pi >= 1:
                    pj = pi - 1
                    sl = pj % 2
                    for p in range(4):
                        o = T(transpose_op(
                            bankb(6 + sl)[:, p * 128:(p + 1) * 128],
                            ktm_sb[:, sl, p * 128:(p + 1) * 128], ident_sb),
                            deps=([ropek[sl], evk[sl]] +
                                  (prev_flash_tail if pj == 0 else [])
                                  if p == 0 else ()))
                    trk[sl] = o
                    evk[sl] = V_(lambda e, sl=sl, pj=pj: e.tensor_copy(
                        kd_sb[:, :, pj * 256:(pj + 1) * 256]
                        .rearrange("p r (t c) -> p r t c", t=2),
                        bankb(6 + sl)[:, 0:512]
                        .rearrange("p (t r c) -> p t r c", t=2, r=2)
                        .rearrange("p t r c -> p r t c")),
                        deps=[trk[sl]])
            kv_dep = [evk[0], evk[1], vev[0], vev[1]]

            mask_ld = None
            chunks = []
            for j in range(4):
                for pr in range(2):
                    for ci in range(CNT[j]):
                        chunks.append((j, pr, ci))
            stage = {}
            mask_lds = {}
            for idx in range(len(chunks) + 1):
                if idx < len(chunks):
                    j, pr, ci = chunks[idx]
                    cntj = CNT[j]
                    q0 = j * 512
                    pg = hg * 2 + pr
                    if pr == 0 and ci == 0:
                        mask_lds[j] = ldg(mask_sb,
                                          masksd[j].rearrange("s p q -> p s q"),
                                          deps=[mask_rd])
                    it = fs["it"]
                    scs, pts = it % 2, it % 4
                    sc_ap = bank2(scs * 2)
                    expdep = fs["exp"][it - 2] if it >= 2 else None
                    T(lambda e, sc=sc_ap, pr=pr, pg=pg, q0=q0, ci=ci:
                      e.matmul(sc[:, 0:512],
                               kd_sb[0:64, pr, ci * 128:(ci + 1) * 128],
                               qd_sb[0:64, pg, q0:q0 + 512],
                               start=True, stop=True),
                      deps=[expdep] + (kv_dep + [q_done] if idx == 0 else []))
                    mm_sc = T(lambda e, sc=sc_ap, pr=pr, pg=pg, q0=q0, ci=ci:
                              e.matmul(sc[:, 512:1024],
                                       kd_sb[64:128, pr, ci * 128:(ci + 1) * 128],
                                       qd_sb[64:128, pg, q0:q0 + 512],
                                       start=True, stop=True,
                                       tile_position=(64, 0)))
                    ptdep = fs["avs"][it - 4] if it >= 4 else None
                    exp_o = SC(lambda e, sc=sc_ap, pts=pts:
                               e.activation(pt_sb[:, pts, :], sc, AF.Exp,
                                            scale=0.125, bias=bias_sb),
                               deps=[mm_sc, ms_bias, ptdep])
                    fs["exp"].append(exp_o)
                    av_dep = exp_o
                    if ci >= cntj - NMSK:
                        s_ = ci - (cntj - NMSK)
                        mk = V_(lambda e, pts=pts, s_=s_: e.tensor_tensor(
                            pt_sb[:, pts, :].rearrange("p (a b) -> p a b", a=2),
                            pt_sb[:, pts, :].rearrange("p (a b) -> p a b", a=2),
                            mask_sb[:, s_, :].unsqueeze(1).broadcast_to(
                                [128, 2, 512]), ALU.mult),
                            deps=[exp_o, mask_lds[j]])
                        av_dep = mk
                        mask_rd = mk
                    stage[idx] = dict(av_dep=av_dep, pts=pts)
                    fs["it"] += 1
                if idx >= 1:
                    j, pr, ci = chunks[idx - 1]
                    cntj = CNT[j]
                    q0 = j * 512
                    pg = hg * 2 + pr
                    st_, sp = (ci == 0), (ci == cntj - 1)
                    up = fs["unit"] % 2
                    yb = 4 + 2 * up
                    sb = 5 + 2 * up
                    sg = stage.pop(idx - 1)
                    pts = sg["pts"]
                    ydeps = [sg["av_dep"]] + ([fs["y_ev"][up]] if st_ else [])
                    T(lambda e, pts=pts, pr=pr, yb=yb, s=st_, p=sp, ci=ci:
                      e.matmul(bank(yb)[0:64, :],
                               v_sb[:, ci, pr * 128:pr * 128 + 64],
                               pt_sb[:, pts, 0:512], start=s, stop=p),
                      deps=ydeps)
                    T(lambda e, pts=pts, pr=pr, yb=yb, s=st_, p=sp, ci=ci:
                      e.matmul(bank(yb)[64:128, :],
                               v_sb[:, ci, pr * 128 + 64:pr * 128 + 128],
                               pt_sb[:, pts, 512:1024], start=s, stop=p,
                               tile_position=(0, 64)))
                    T(lambda e, pts=pts, sb=sb, s=st_, p=sp:
                      e.matmul(bank(sb)[0:1, :], ones_sb, pt_sb[:, pts, 0:512],
                               start=s, stop=p),
                      deps=[ld_ones] + ([fs["bc_rd"][up]] if st_ else []))
                    mm_last = T(lambda e, pts=pts, sb=sb, s=st_, p=sp:
                                e.matmul(bank(sb)[32:33, :], ones_sb,
                                         pt_sb[:, pts, 512:1024],
                                         start=s, stop=p,
                                         tile_position=(0, 32)))
                    fs["avs"].append(mm_last)
                    if sp:
                        if fs["unit"] < 2:
                            V_(lambda e, sb=sb: e.tensor_copy(
                                sstage_sb[0:1, :], bank(sb)[0:1, :]),
                               deps=[mm_last, fs["sstage_rd"]])
                            cpB = V_(lambda e, sb=sb: e.tensor_copy(
                                sstage_sb[32:33, :], bank(sb)[32:33, :]))
                        else:
                            cpB = V_(lambda e, sb=sb: e.tensor_copy(
                                sstage_sb[0:33, :], bank(sb)[0:33, :]),
                                deps=[mm_last, fs["sstage_rd"]])
                        bc = T(lambda e, sb=sb: e.matmul(bank(sb), sel_sb,
                                                         sstage_sb,
                                                         start=True, stop=True),
                               deps=[cpB, ld_sel, ms_sst, fs["bc_rd"][up]])
                        fs["sstage_rd"] = bc
                        rc = V_(lambda e, sb=sb: e.reciprocal(rs_sb, bank(sb)),
                                deps=[bc, fs["rs_rd"]])
                        fs["bc_rd"][up] = rc
                        yev = V_(lambda e, pg=pg, q0=q0, yb=yb: e.tensor_tensor(
                            y_sb[:, pg, q0:q0 + 512], bank(yb), rs_sb, ALU.mult),
                            deps=[rc])
                        fs["rs_rd"] = yev
                        fs["y_ev"][up] = yev
                        y_final = yev
                        fs["unit"] += 1
            prev_flash_tail = [fs["exp"][-1], fs["avs"][-1], fs["bc_rd"][0],
                               fs["bc_rd"][1], fs["y_ev"][0], fs["y_ev"][1]]
        attn_done = y_final

        dbg_ops = []
        if dbg:
            dbg_ops.append(G(lambda e: e.dma_start(out=dbg_o["dbg_q"], in_=qd_sb),
               deps=[q_done]))
            dbg_ops.append(G(lambda e: e.dma_start(out=dbg_o["dbg_k"], in_=kd_sb),
               deps=[attn_done]))
            dbg_ops.append(G(lambda e: e.dma_start(out=dbg_o["dbg_v"], in_=v_sb),
               deps=[attn_done]))
            dbg_ops.append(G(lambda e: e.dma_start(out=dbg_o["dbg_y"], in_=y_sb),
               deps=[attn_done]))
            dbg_ops.append(G(lambda e: e.dma_start(out=dbg_o["dbg_z"], in_=z0_sb),
               deps=[z_done]))

        ld_wproj = ldg(wproj_sb, wprojd, deps=list(kv_dep) + dbg_ops)

        ln_prev = [None]

        def ln_fused(bankap, xres, xmid, xout, deps1, deps9):
            add_o = V_(lambda e: e.scalar_tensor_tensor(
                xmid, bankap, 1.0, xres, ALU.mult, ALU.add,
                accum_out=col_sb[:, 0:1]), deps=deps1)
            V_(lambda e: e.scalar_tensor_tensor(stat_sb, xmid, 1.0, xmid,
                                                ALU.mult, ALU.mult,
                                                accum_out=col_sb[:, 1:2]))
            V_(lambda e: e.tensor_scalar_mul(col_sb[:, 2:3], col_sb[:, 0:1],
                                             1.0 / D))
            V_(lambda e: e.scalar_tensor_tensor(col_sb[:, 4:5], col_sb[:, 2:3],
                                                -1.0, col_sb[:, 2:3], ALU.mult,
                                                ALU.mult))
            var = V_(lambda e: e.tensor_scalar(col_sb[:, 5:6], col_sb[:, 1:2],
                                               1.0 / D, col_sb[:, 4:5],
                                               ALU.mult, ALU.add))
            srt = SC(lambda e: e.activation(col_sb[:, 6:7], col_sb[:, 5:6],
                                            AF.Sqrt, bias=eps_sb),
                     deps=[var, ms_eps])
            rst = V_(lambda e: e.reciprocal(col_sb[:, 7:8], col_sb[:, 6:7]),
                     deps=[srt, ln_prev[0]])
            nb = V_(lambda e: e.scalar_tensor_tensor(
                col_sb[:, 3:4], col_sb[:, 2:3], -1.0, col_sb[:, 7:8],
                ALU.mult, ALU.mult), deps=[rst])
            ap = SC(lambda e: e.activation(xout, xmid, AF.Identity,
                                           scale=col_sb[:, 7:8],
                                           bias=col_sb[:, 3:4]),
                    deps=deps9 + [nb])
            ln_prev[0] = ap
            return add_o, ap

        x1ld = [None, None]
        x1cast = [None, None]
        x1tr = [None, None]
        x1ev = [None, None]
        ldw12 = [None, None]
        w12_rd = [None, None]
        ldh = [None, None, None]
        wbuf_rd = [None, None, None]
        ldw3s = [None, None]
        w3_rd = None
        w3_rd2 = [None, None]
        outw = [None, None]
        addx_bank = {}
        x1f_rd = [None, None]
        fev = [None] * 8
        x1w_all = []
        x1ld2_last = [None]

        def fm_tile(tt, q8, fsub_war, pace=None):
            sl = tt % 2
            x1ld[sl] = ldg(x1f_sb[:, sl, :],
                           x1sp[tt * 128:(tt + 1) * 128, :],
                           deps=[x1w_all[tt], x1cast[sl], x1f_rd[sl], pace])
            x1cast[sl] = V_(lambda e, sl=sl: e.tensor_copy(
                x1b_sb[:, sl, :], x1f_sb[:, sl, :]), deps=[x1ld[sl]])
            for ic in range(8):
                o = T(transpose_op(
                    bankb(6 + sl)[:, ic * 128:(ic + 1) * 128],
                    x1b_sb[:, sl, ic * 128:(ic + 1) * 128], ident_sb),
                    deps=([x1cast[sl], x1ev[sl], w12_rd[0], w12_rd[1],
                           outw[0], outw[1], fsub_war] +
                          ([attn_done] if tt < 2 else [])
                          if ic == 0 else ()))
            x1tr[sl] = o
            x1ev[sl] = V_(lambda e, sl=sl, q8=q8: e.tensor_copy(
                x1fm_sb[:, :, q8 * 128:(q8 + 1) * 128],
                bankb(6 + sl).rearrange("p (a b) -> p a b", a=8)),
                deps=[x1tr[sl]])

        ldo = [None, None]
        addo = [None, None]
        x1w = [None, None]
        for tt in range(16):
            sl = tt % 2
            ldo[sl] = lds(xo_sb[:, sl, :], x_own[tt * 128:(tt + 1) * 128, :],
                          deps=[addo[sl]] + ([attn_done] + dbg_ops if tt < 2 else []))
            for og in range(2):
                for ch in range(8):
                    dps = ([attn_done, ld_wproj, addo[sl]]
                           if og == 0 and ch == 0 else ())
                    o = T(lambda e, ch=ch, og=og, sl=sl, tt=tt:
                          e.matmul(bank(sl * 2 + og),
                                   y_sb[:, ch, tt * 128:(tt + 1) * 128],
                                   wproj_sb[:, ch, og * 512:(og + 1) * 512],
                                   start=(ch == 0), stop=(ch == 7)), deps=dps)
            pmm = o
            addo[sl], lt = ln_fused(
                bank2(sl * 2), xo_sb[:, sl, :], x1p_sb[:, sl, :],
                x1_sb[:, sl, :],
                deps1=[pmm, ldo[sl], x1w[sl]], deps9=[x1w[sl]])
            x1w[sl] = SY(lambda e, sl=sl, tt=tt: e.dma_start(
                out=x1sp[tt * 128:(tt + 1) * 128, :], in_=x1_sb[:, sl, :]),
                deps=[lt])
            x1w_all.append(x1w[sl])
            if dbg:
                SY(lambda e, sl=sl, tt=tt: e.dma_start(
                    out=dbg_o["dbg_x1"][tt * 128:(tt + 1) * 128, :],
                    in_=x1_sb[:, sl, :]), deps=[lt])
            if tt >= 2 and tt < 10:
                fm_tile(tt - 2, tt - 2, None)
        x1fm_done_tg0 = [x1ev[0], x1ev[1]]

        af_war = None
        fsub_rd = None
        dbg_done = False
        for tg in range(2):
            t0 = tg * 1024
            sil = [None, None, None, None]
            aev = [None, None, None, None]
            if tg == 0:
                x1fm_done = x1fm_done_tg0
            else:
                for q8 in range(8):
                    fm_tile(8 + q8, q8, fsub_rd,
                            pace=(x1ld2_last[0] if q8 < 2 else None))
                x1fm_done = [x1ev[0], x1ev[1]]
            for oc in range(32):
                wsl = oc % 2
                ldw12[wsl] = ldg(w12b_sb[:, wsl, :, :, :], w12d[oc],
                                 deps=[w12_rd[wsl], w3_rd] +
                                 ([attn_done, x1w_all[7]]
                                  if tg == 0 and oc < 2 else []) +
                                 ([x1ld2_last[0]]
                                  if tg == 1 and oc < 2 else []))
                for half in range(2):
                    pb = half * 4 + wsl * 2
                    hs = slice(half * 512, (half + 1) * 512)
                    ss = half * 2 + wsl
                    for w_ in range(2):
                        for ic in range(8):
                            dps = ([ldw12[wsl]] + x1fm_done +
                                   [sil[ss], aev[ss]] +
                                   ([addo[0], addo[1]]
                                    if tg == 0 and oc < 2 else []))                                 if (w_ == 0 and ic == 0) else ()
                            o = T(lambda e, w_=w_, ic=ic, pb=pb, wsl=wsl, hs=hs:
                                  e.matmul(bank(pb + w_),
                                           w12b_sb[:, wsl, w_, ic, :],
                                           x1fm_sb[:, ic, hs],
                                           start=(ic == 0), stop=(ic == 7)),
                                  deps=dps)
                    w12_rd[wsl] = o
                    sil[ss] = SC(lambda e, pb=pb, oc=oc, ss=ss:
                                 e.activation(silu_sb[:, ss, :], bank(pb),
                                              AF.Silu,
                                              scale=s1_sb[:, oc:oc + 1]),
                                 deps=[w12_rd[wsl], ld_s1, aev[ss]])
                    aev[ss] = V_(lambda e, pb=pb, oc=oc, ss=ss, hs=hs:
                                 e.scalar_tensor_tensor(
                                     af_sb[:, oc, hs], bank(pb + 1),
                                     s2_sb[:, oc:oc + 1], silu_sb[:, ss, :],
                                     ALU.mult, ALU.mult),
                                 deps=[sil[ss], ld_s2, af_war] +
                                 ([x1w_all[15]]
                                  if tg == 0 and oc == 16 else []))
            a_done = [aev[0], aev[1], aev[2], aev[3]]
            if dbg and tg == 0:
                G(lambda e: e.dma_start(out=dbg_o["dbg_af"], in_=af_sb),
                  deps=a_done)

            def h_layer(src_sb, dst_sb, wsrc, scale_sb, scale_ld, dep_start,
                        ld_gate=()):
                gact = [None, None, None, None]
                for oc in range(32):
                    wsl = oc % 3
                    ldh[wsl] = ldg(wbuf_sb[:, wsl, :, :], wsrc[oc],
                                   deps=[wbuf_rd[wsl]] + list(ld_gate) +
                                   ([outw[0], outw[1]] if wsl == 2 else []))
                    for half in range(2):
                        pb = (oc * 2 + half) % 4
                        hs = slice(half * 512, (half + 1) * 512)
                        for rc in range(32):
                            dps = ([ldh[wsl], gact[pb]] + dep_start)                                 if rc == 0 else ()
                            o = T(lambda e, rc=rc, pb=pb, wsl=wsl, hs=hs:
                                  e.matmul(bank(pb), wbuf_sb[:, wsl, rc, :],
                                           src_sb[:, rc, hs],
                                           start=(rc == 0), stop=(rc == 31)),
                                  deps=dps)
                        wbuf_rd[wsl] = o
                        gact[pb] = SC(lambda e, oc=oc, pb=pb, hs=hs:
                                      e.activation(dst_sb[:, oc, hs], bank(pb),
                                                   AF.Gelu,
                                                   scale=scale_sb[:, oc:oc + 1]),
                                      deps=[wbuf_rd[wsl], scale_ld])
                return [gact[0], gact[1], gact[2], gact[3]]

            g_done = h_layer(af_sb, g_sb, h1wd, h1s_sb, ld_h1s,
                             a_done[2:],
                             ld_gate=([x1w[0], x1w[1]] if tg == 0 else ()))
            g2_done = h_layer(g_sb, af_sb, h2wd, h2s_sb, ld_h2s, g_done)

            if dbg and not dbg_done:
                G(lambda e: e.dma_start(out=dbg_o["dbg_a"], in_=g_sb),
                  deps=g_done)
                G(lambda e: e.dma_start(out=dbg_o["dbg_g"], in_=af_sb),
                  deps=g2_done)
                dbg_done = True

            for half in range(2):
                th0 = t0 + half * 512
                hs = slice(half * 512, (half + 1) * 512)
                for oc in range(8):
                    pb = oc % 2
                    wb3 = w3b_sb if oc % 2 == 0 else w3b2_sb
                    ldw3s[oc % 2] = ldg(wb3, w3wd[oc],
                                        deps=[w3_rd2[oc % 2], a_done[3],
                                              w12_rd[0], w12_rd[1]] +
                                        ([attn_done, x1w[0], x1w[1]]
                                         if tg == 0 and half == 0 and oc == 0
                                         else []))
                    for rc in range(32):
                        dps = ([ldw3s[oc % 2],
                                fev[oc - 2] if oc >= 2 else None,
                                fev[oc - 2 + 8] if oc < 2 else None] + g2_done) \
                            if rc == 0 else ()
                        o = T(lambda e, rc=rc, pb=pb, hs=hs, wb3=wb3:
                              e.matmul(bank(pb), wb3[:, rc, :],
                                       af_sb[:, rc, hs],
                                       start=(rc == 0), stop=(rc == 31)),
                              deps=dps)
                    w3_rd2[oc % 2] = o
                    w3_rd = o
                    fev[oc] = V_(lambda e, oc=oc, pb=pb: e.tensor_scalar_mul(
                        fsub_sb[:, oc, :], bank(pb), w3s_sb[:, oc:oc + 1]),
                        deps=[w3_rd, ld_w3s, fsub_rd])
                for q in range(4):
                    tt = th0 // 128 + q
                    sl = tt % 2
                    x1ld2 = lds(x1f_sb[:, sl, :],
                                x1sp[tt * 128:(tt + 1) * 128, :],
                                deps=[x1cast[sl], x1f_rd[sl], outw[sl]])
                    x1ld2_last[0] = x1ld2
                    xb = 4 if q % 2 == 0 else 6
                    for oc in range(8):
                        o = T(transpose_op(
                            bank2(xb)[:, oc * 128:(oc + 1) * 128],
                            fsub_sb[:, oc, q * 128:(q + 1) * 128], identf_sb),
                            deps=([fev[7], ld_identf, addx_bank.get(xb)]
                                  if oc == 0 else ()))
                    ftr = o
                    if q == 3:
                        fsub_rd = ftr
                    addx, lt2 = ln_fused(
                        bank2(xb), x1f_sb[:, sl, :], x1f_sb[:, sl, :],
                        x2_sb[:, sl, :],
                        deps1=[ftr, x1ld2], deps9=[outw[sl]])
                    addx_bank[xb] = addx
                    x1f_rd[sl] = lt2
                    outw[sl] = SY(lambda e, sl=sl, tt=tt: e.dma_start(
                        out=out[tt * 128:(tt + 1) * 128, :], in_=x2_sb[:, sl, :]),
                        deps=[lt2])
            af_war = w3_rd

        P.add("sync", lambda e: e.nop(), deps=[outw[0], outw[1]])

        block = es.enter_context(nc.Block())
        P.emit(nc, block, es)
    return nc


def _host_prep(inputs):
    f32 = np.float32
    Wq = np.asarray(inputs["L0_attn_Wq"], f32)
    Wk = np.asarray(inputs["L0_attn_Wk"], f32)
    Wv = np.asarray(inputs["L0_attn_Wv"], f32)
    kdec = np.asarray(inputs["L0_attn_key_decoder"], f32)
    vdec = np.asarray(inputs["L0_attn_value_decoder"], f32)
    Wproj = np.asarray(inputs["L0_attn_Wproj"], f32)
    w1w = np.asarray(inputs["L0_ffn_w1_w"], f32)
    w2w = np.asarray(inputs["L0_ffn_w2_w"], f32)
    h1w = np.asarray(inputs["L0_ffn_h1_w"], f32)
    h2w = np.asarray(inputs["L0_ffn_h2_w"], f32)
    w3w = np.asarray(inputs["L0_ffn_w3_w"], f32)

    com = {}
    hh, cc, ff = np.meshgrid(np.arange(H), np.arange(2), np.arange(NF),
                             indexing="ij")
    qperm = (64 * hh + 2 * ff + cc).reshape(-1)
    com["wq"] = np.ascontiguousarray(Wq[:, qperm].reshape(8, 128, D)
                                     .transpose(1, 0, 2))
    ffk, rrk = np.meshgrid(np.arange(NF), np.arange(KR), indexing="ij")
    kperm = np.concatenate([(6 * ffk + 2 * rrk + 0).reshape(-1),
                            (6 * ffk + 2 * rrk + 1).reshape(-1)])
    com["wk"] = np.ascontiguousarray(Wk[:, kperm].reshape(8, 128, VR)
                                     .transpose(1, 0, 2))
    com["wv"] = np.ascontiguousarray(Wv.reshape(8, 128, VR).transpose(1, 0, 2))
    Dm = np.zeros((VR, D), f32)
    br, bi = kdec[..., 0], kdec[..., 1]
    fi, hi, ri = np.meshgrid(np.arange(NF), np.arange(H), np.arange(KR),
                             indexing="ij")
    for c, top, bot in ((0, br, bi), (1, bi, br)):
        Dm[3 * fi + ri, 64 * hi + 32 * c + fi] = top
        Dm[96 + 3 * fi + ri, 64 * hi + 32 * c + fi] = bot
    com["d0"] = np.ascontiguousarray(Dm[:128])
    com["d1"] = np.ascontiguousarray(Dm[128:])
    VD = np.ascontiguousarray(vdec.transpose(1, 0, 2).reshape(VR, D))
    com["vd0"] = np.ascontiguousarray(VD[:128])
    com["vd1"] = np.ascontiguousarray(VD[128:])
    com["wproj"] = np.ascontiguousarray(Wproj.reshape(8, 128, D)
                                        .transpose(1, 0, 2))
    w1s_ = w1w.reshape(32, 128, 8, 128).transpose(0, 3, 2, 1)
    w2s_ = w2w.reshape(32, 128, 8, 128).transpose(0, 3, 2, 1)
    com["w12"] = np.ascontiguousarray(np.stack([w1s_, w2s_], 2))
    com["h1w"] = np.ascontiguousarray(h1w.reshape(32, 128, 32, 128)
                                      .transpose(0, 3, 2, 1))
    com["h2w"] = np.ascontiguousarray(h2w.reshape(32, 128, 32, 128)
                                      .transpose(0, 3, 2, 1))
    com["w3w"] = np.ascontiguousarray(w3w.reshape(8, 128, 32, 128)
                                      .transpose(0, 3, 2, 1))
    com["s1s"] = np.ascontiguousarray(
        np.asarray(inputs["L0_ffn_w1_s"], f32).reshape(32, 128).T)
    com["s2s"] = np.ascontiguousarray(
        np.asarray(inputs["L0_ffn_w2_s"], f32).reshape(32, 128).T)
    com["h1s"] = np.ascontiguousarray(
        np.asarray(inputs["L0_ffn_h1_s"], f32).reshape(32, 128).T)
    com["h2s"] = np.ascontiguousarray(
        np.asarray(inputs["L0_ffn_h2_s"], f32).reshape(32, 128).T)
    com["w3s"] = np.ascontiguousarray(
        np.asarray(inputs["L0_ffn_w3_s"], f32).reshape(8, 128).T)
    com["ident"] = np.eye(128, dtype=f32)
    sel = np.zeros((33, 128), f32)
    sel[0, 0:64] = 1.0
    sel[32, 64:128] = 1.0
    com["sel33"] = sel
    com["onesb"] = np.ones((128, 1), f32)
    half = HD // 2
    freq = (1.0 / ROPE_BASE ** (np.arange(half, dtype=f32) / half)).astype(f32)
    freqs = np.outer(np.arange(S, dtype=f32), freq)
    COS = np.cos(freqs).astype(f32)
    SIN = np.sin(freqs).astype(f32)
    com["ck"] = np.ascontiguousarray(COS.reshape(32, 128, 32).transpose(1, 0, 2))
    com["sk"] = np.ascontiguousarray(SIN.reshape(32, 128, 32).transpose(1, 0, 2))
    return com, COS, SIN


def _core_inputs(inputs, com, COS, SIN, core):
    b, hlf = core // 2, core % 2
    x = np.asarray(inputs["L0_attn_x"], np.float32)
    groups = [2 * j + 1 - hlf for j in range(4)]
    rows = np.concatenate([np.arange(g * 512, (g + 1) * 512) for g in groups])
    d = dict(com)
    d["x_own"] = np.ascontiguousarray(x[b][rows])
    d["x_full"] = np.ascontiguousarray(x[b])
    d["cq"] = np.ascontiguousarray(COS[rows].reshape(16, 128, 32)
                                   .transpose(1, 0, 2))
    d["sq"] = np.ascontiguousarray(SIN[rows].reshape(16, 128, 32)
                                   .transpose(1, 0, 2))
    m = np.zeros((4, NMSK, 128, 512), np.float32)
    for j in range(4):
        q0 = groups[j] * 512
        for s_ in range(NMSK):
            k0 = (CNT[j] - NMSK + s_) * 128
            kk = k0 + np.arange(128)[:, None]
            qq = q0 + np.arange(512)[None, :]
            m[j, s_] = (kk <= qq).astype(np.float32)
    d["masks"] = m
    return d, groups


def kernel(**inputs):
    com, COS, SIN = _host_prep(inputs)
    in_maps, group_list = [], []
    for c in range(8):
        d, groups = _core_inputs(inputs, com, COS, SIN, c)
        in_maps.append(d)
        group_list.append(groups)
    nc = build_nc()
    res = run_bass_kernel_spmd(nc, in_maps, core_ids=list(range(8)))
    outa = np.empty((BSZ, S, D), np.float32)
    for c in range(8):
        b = c // 2
        shard = np.asarray(res.results[c]["out"], np.float32).reshape(TQ, D)
        for j, g in enumerate(group_list[c]):
            outa[b, g * 512:(g + 1) * 512] = shard[j * 512:(j + 1) * 512]
    return outa


# version 18
# speedup vs baseline: 1.0288x; 1.0126x over previous
import contextlib
import numpy as np
import concourse.bass as bass
import concourse.mybir as mybir
from concourse.bass_utils import run_bass_kernel_spmd

F32 = mybir.dt.float32
BF16 = mybir.dt.bfloat16
AF = mybir.ActivationFunctionType
ALU = mybir.AluOpType

S, D, BSZ, H, HD, NF, KR, VR, DH = 4096, 1024, 4, 16, 64, 32, 3, 192, 4096
TQ = 2048
CNT = (8, 16, 24, 32)
NMSK = 8
EXPB = 12.0
LNEPS = 1e-5
ROPE_BASE = 10000.0
NHG = 4
NTG = 4

ENGS = ("tensor", "vector", "scalar", "gpsimd", "sync")


class Prog:

    def __init__(self):
        self.ops = []
        self.per_eng = {e: [] for e in ENGS}

    def add(self, eng, fn, deps=(), dma=False):
        i = len(self.ops)
        self.ops.append((eng, fn, tuple(d for d in deps if d is not None), dma))
        self.per_eng[eng].append(i)
        return i

    def emit(self, nc, block, es):
        need = set()
        for i, (eng, fn, deps, dma) in enumerate(self.ops):
            if dma:
                need.add(i)
            if eng in ("vector", "scalar"):
                need.add(i)
            for d in deps:
                pe, _, _, pdma = self.ops[d]
                if pdma or pe != eng:
                    need.add(d)
        sems = {e: es.enter_context(nc.semaphore("S" + e)) for e in ENGS}
        for e in ENGS:
            for s in range(3):
                sems[(e, s)] = es.enter_context(nc.semaphore(f"S{e}{s}"))
        val, cnt = {}, {e: 0 for e in ENGS}
        for e in ENGS:
            for i in self.per_eng[e]:
                if i in need and not self.ops[i][3]:
                    cnt[e] += 1
                    val[i] = cnt[e]
        waits = []
        for i, (eng, fn, deps, dma) in enumerate(self.ops):
            w = {}
            for d in deps:
                if d in val and not self.ops[d][3]:
                    pe = self.ops[d][0]
                    if pe != eng:
                        w[pe] = max(w.get(pe, 0), val[d])
            if eng == "vector" and val.get(i, 1) > 1:
                w[eng] = max(w.get(eng, 0), val[i] - 1)
            waits.append(w)
        B, R = 2, 3
        self.stripe_of = {}
        self.stripe_val = {}
        stripe_gate = {}
        for e in ENGS:
            dmas = [i for i in self.per_eng[e] if self.ops[i][3]]
            epochs, cur = [], []
            for i in dmas:
                boundary = len(cur) >= B
                if not boundary and cur:
                    start = cur[0]
                    if any(d >= start for d in self.ops[i][2]):
                        boundary = True
                if boundary and cur:
                    epochs.append(cur)
                    cur = []
                cur.append(i)
            if cur:
                epochs.append(cur)
            stot = [0] * R
            for k, ep in enumerate(epochs):
                s = k % R
                if stot[s] > 0:
                    stripe_gate[ep[0]] = (s, stot[s])
                stot[s] += 16 * len(ep)
                for i in ep:
                    self.stripe_of[i] = (e, s)
                    self.stripe_val[i] = stot[s]
        self._stripe_gate = stripe_gate
        waits2 = []
        for i, (eng, fn, deps, dma) in enumerate(self.ops):
            w = {}
            for pe, v in waits[i].items():
                w[pe] = max(w.get(pe, 0), v)
            for d in deps:
                if d in self.stripe_of:
                    pe, s = self.stripe_of[d]
                    key = (pe, s)
                    w[key] = max(w.get(key, 0), self.stripe_val[d])
            if i in self._stripe_gate:
                s, v = self._stripe_gate[i]
                key = (eng, s)
                w[key] = max(w.get(key, 0), v)
            waits2.append(w)
        waits = waits2
        seen = {e: {} for e in ENGS}
        for e in ENGS:
            for i in self.per_eng[e]:
                w = waits[i]
                for pe in list(w):
                    if seen[e].get(pe, 0) >= w[pe]:
                        del w[pe]
                    else:
                        seen[e][pe] = w[pe]
        for e in ENGS:
            def body(eng, _e=e):
                for i in self.per_eng[_e]:
                    _, fn, _, dma = self.ops[i]
                    for pe, v in sorted(waits[i].items(), key=str):
                        eng.wait_ge(sems[pe], v)
                    ins = fn(eng)
                    if dma:
                        ins.then_inc(sems[self.stripe_of[i]], 16)
                    elif i in val:
                        ins.then_inc(sems[_e], 1)
            getattr(block, e)(body)


class Carver:
    def __init__(self, pool_ap, limit, off=0):
        self.pool = pool_ap
        self.off = off
        self.limit = limit

    def take(self, shape, dt):
        p = shape[0]
        elems = int(np.prod(shape[1:]))
        width = elems * (2 if dt == F32 else 1)
        self.off = (self.off + 1) & ~1
        a = self.pool[0:p, self.off:self.off + width]
        self.off += width
        assert self.off <= self.limit, f"sbuf pool overflow {self.off} > {self.limit}"
        if dt == F32:
            a = a.bitcast(F32)
        if len(shape) == 3:
            a = a.rearrange("p (a b) -> p a b", a=shape[1])
        elif len(shape) == 4:
            a = a.rearrange("p (a b c) -> p a b c", a=shape[1], b=shape[2])
        elif len(shape) == 5:
            a = a.rearrange("p (a b c d) -> p a b c d", a=shape[1], b=shape[2],
                            c=shape[3])
        else:
            assert len(shape) == 2
        return a


def build_nc(dbg=False):
    nc = bass.Bass("TRN2", target_bir_lowering=False,
                   detect_race_conditions=False)

    def din(name, shape):
        return nc.dram_tensor(name, shape, F32, kind="ExternalInput").ap()

    x_own = din("x_own", [TQ, D])
    x_full = din("x_full", [S, D])
    cqd, sqd = din("cq", [128, 16, 32]), din("sq", [128, 16, 32])
    ckd, skd = din("ck", [128, 32, 32]), din("sk", [128, 32, 32])
    masksd = din("masks", [4, NMSK, 128, 512])
    wqd = din("wq", [128, 8, D])
    wkd = din("wk", [128, 8, VR])
    wvd = din("wv", [128, 8, VR])
    d0d, d1d = din("d0", [128, D]), din("d1", [64, D])
    vd0d, vd1d = din("vd0", [128, D]), din("vd1", [64, D])
    wprojd = din("wproj", [128, 8, D])
    w12d = din("w12", [32, 128, 2, 8, 128])
    h1wd = din("h1w", [32, 128, 32, 128])
    h2wd = din("h2w", [32, 128, 32, 128])
    w3wd = din("w3w", [8, 128, 32, 128])
    s1d, s2d = din("s1s", [128, 32]), din("s2s", [128, 32])
    h1sd, h2sd = din("h1s", [128, 32]), din("h2s", [128, 32])
    w3sd = din("w3s", [128, 8])
    identd = din("ident", [128, 128])
    sel33d = din("sel33", [33, 128])
    onesd = din("onesb", [128, 1])

    out = nc.dram_tensor("out", [TQ, D], F32, kind="ExternalOutput").ap()
    x1sp = nc.dram_tensor("x1sp", [TQ, D], F32, kind="ExternalOutput").ap()
    dbg_o = {}
    if dbg:
        for nm, shp in (("dbg_q", [128, 8, TQ]), ("dbg_k", [128, 2, S]),
                        ("dbg_v", [128, 32, 256]), ("dbg_y", [128, 8, TQ]),
                        ("dbg_x1", [TQ, D]), ("dbg_z", [128, S]),
                        ("dbg_a", [128, 32, 1024]), ("dbg_g", [128, 32, 1024]),
                        ("dbg_af", [128, 32, 1024])):
            dbg_o[nm] = nc.dram_tensor(nm, shp, F32, kind="ExternalOutput").ap()

    P = Prog()

    def G(fn, deps=()):
        return P.add("gpsimd", fn, deps, dma=True)

    def SY(fn, deps=()):
        return P.add("sync", fn, deps, dma=True)

    def T(fn, deps=()):
        return P.add("tensor", fn, deps)

    def V_(fn, deps=()):
        return P.add("vector", fn, deps)

    def SC(fn, deps=()):
        return P.add("scalar", fn, deps)

    def ldg(dst, src, deps=()):
        return G(lambda e, a=dst, b=src: e.dma_start(out=a, in_=b), deps)

    def lds(dst, src, deps=()):
        return SY(lambda e, a=dst, b=src: e.dma_start(out=a, in_=b), deps)

    es = contextlib.ExitStack()
    with es:
        POOLW = 106416
        pool = es.enter_context(nc.sbuf_tensor("pool", [128, POOLW], BF16)).ap()
        psum = es.enter_context(nc.psum_tensor("psum", [128, 8, 512], F32)).ap()

        def bank(b):
            return psum[:, b, :]

        def bank2(b):
            return psum[:, b:b + 2, :].rearrange("p a b -> p (a b)")

        def bankb(b):
            return psum[:, b, :].bitcast(BF16)

        C = Carver(pool, POOLW)
        ident_sb = C.take([128, 128], BF16)
        identf_sb = C.take([128, 128], F32)
        eps_sb = C.take([128, 1], F32)
        s1_sb = C.take([128, 32], F32)
        s2_sb = C.take([128, 32], F32)
        h1s_sb = C.take([128, 32], F32)
        h2s_sb = C.take([128, 32], F32)
        w3s_sb = C.take([128, 8], F32)
        col_sb = C.take([128, 8], F32)
        a_live_end = C.off
        ck_sb = C.take([128, 32, 32], F32)
        sk_sb = C.take([128, 32, 32], F32)
        cq_sb = C.take([128, 16, 32], F32)
        sq_sb = C.take([128, 16, 32], F32)
        mask_sb = C.take([128, NMSK, 512], BF16)
        sel_sb = C.take([33, 128], F32)
        ones_sb = C.take([128, 1], BF16)
        bias_sb = C.take([128, 1], F32)
        pt_sb = C.take([128, 4, 1024], BF16)
        sstage_sb = C.take([33, 512], F32)
        lnr_sb = C.take([128, 512], F32)
        rs_sb = C.take([128, 512], F32)
        wk_sb = C.take([128, 8, VR], BF16)
        wv_sb = C.take([128, 8, VR], BF16)
        d0_sb = C.take([128, D], BF16)
        d1_sb = C.take([64, D], BF16)
        vd0_sb = C.take([128, D], BF16)
        vd1_sb = C.take([64, D], BF16)
        a_end = C.off
        qd_sb = C.take([128, 8, TQ], BF16)
        y_sb = C.take([128, 8, TQ], BF16)
        bc_end = C.off
        z0_sb = C.take([128, S], BF16)
        z1_sb = C.take([64, S], BF16)
        xv0_sb = C.take([128, S], BF16)
        xv1_sb = C.take([64, S], BF16)
        kd_off = C.off
        kd_sb = C.take([128, 2, S], BF16)
        v_sb = C.take([128, 32, 256], BF16)
        d_stream = C.off
        wq_sb = Carver(pool, d_stream, off=kd_off).take([128, 8, D], BF16)
        xtm_sb = C.take([128, 2, D], BF16)
        xfm_sb = C.take([128, 2, 8, 128], BF16)
        qtm_sb = C.take([128, 2, D], BF16)
        ktm_sb = C.take([128, 2, 512], BF16)
        rsc1_sb = C.take([128, 512], F32)
        rsc2_sb = C.take([128, 512], F32)
        attn_peak = C.off
        wproj_sb = Carver(pool, bc_end + 8192, off=bc_end).take([128, 8, D],
                                                               BF16)
        CP = Carver(pool, POOLW, off=kd_off)
        xo_sb = CP.take([128, 2, D], F32)
        x1p_sb = CP.take([128, 2, D], F32)
        x1_sb = CP.take([128, 2, D], F32)
        CF1 = Carver(pool, a_end, off=a_live_end)
        stat_sb = CF1.take([128, D], F32)
        wbuf_off = CF1.off
        wbuf_sb = CF1.take([128, 3, 32, 128], BF16)
        w12b_off = CF1.off
        w12b_sb = CF1.take([128, 2, 2, 8, 128], BF16)
        sw3_off = CF1.off
        silu_sb = CF1.take([128, 4, 512], F32)
        x1b_sb = Carver(pool, a_end, off=wbuf_off).take([128, 2, D], BF16)
        CF2 = Carver(pool, bc_end, off=a_end)
        g_sb = CF2.take([128, 32, 1024], BF16)
        w3b_sb = Carver(pool, a_end, off=sw3_off).take([128, 32, 128], BF16)
        CF3 = Carver(pool, POOLW, off=bc_end)
        af_sb = CF3.take([128, 32, 1024], BF16)
        x1fm_off = CF3.off
        x1fm_sb = CF3.take([128, 8, 1024], BF16)
        x1f_sb = CF3.take([128, 3, D], F32)
        ffn_peak = CF3.off
        fsub_sb = Carver(pool, POOLW, off=x1fm_off).take([128, 8, 512], F32)
        x2_sb = Carver(pool, a_end,
                       off=wbuf_off + 2 * 4096).take([128, 2, D], F32)
        w3b2_sb = Carver(pool, a_end, off=w12b_off).take([128, 32, 128], BF16)

        ld_wk = ldg(wk_sb, wkd)
        ld_wv = ldg(wv_sb, wvd)
        ld_d0 = ldg(d0_sb, d0d)
        ld_d1 = ldg(d1_sb, d1d)
        ld_vd0 = ldg(vd0_sb, vd0d)
        ld_vd1 = ldg(vd1_sb, vd1d)
        ld_ident = ldg(ident_sb, identd)
        ld_ones = ldg(ones_sb, onesd)
        ld_wq = ldg(wq_sb, wqd)
        ld_ck = lds(ck_sb, ckd)
        ld_sk = lds(sk_sb, skd)
        ld_cq = lds(cq_sb, cqd)
        ld_sq = lds(sq_sb, sqd)
        ld_identf = lds(identf_sb, identd)
        ld_sel = lds(sel_sb, sel33d)
        ld_s1 = lds(s1_sb, s1d)
        ld_s2 = lds(s2_sb, s2d)
        ld_h1s = lds(h1s_sb, h1sd)
        ld_h2s = lds(h2s_sb, h2sd)
        ld_w3s = lds(w3s_sb, w3sd)
        ms_bias = V_(lambda e: e.memset(bias_sb, -EXPB))
        ms_sst = V_(lambda e: e.memset(sstage_sb, 0.0))
        ms_eps = V_(lambda e: e.memset(eps_sb, LNEPS))

        def reim(ap2, nh):
            return ap2.rearrange("p (h c f) -> p h c f", h=nh, c=2)

        neg0 = V_(lambda e: e.tensor_scalar_mul(
            reim(d0_sb[96:128, :], 16)[:, :, 0, :],
            reim(d0_sb[96:128, :], 16)[:, :, 0, :], -1.0), deps=[ld_d0])
        neg1 = V_(lambda e: e.tensor_scalar_mul(
            reim(d1_sb, 16)[:, :, 0, :],
            reim(d1_sb, 16)[:, :, 0, :], -1.0), deps=[ld_d1])

        def transpose_op(dst, src, idn):
            return lambda e, a=dst, b=src, c=idn: e.transpose(a, b, c)

        ldx = [None, None]
        tr_last = [None, None]
        ev_fm = [None, None]
        zx_ev = [None, None]
        mmz = None
        for i1 in range(33):
            if i1 < 32:
                kt = i1
                sl = kt % 2
                ldx[sl] = ldg(xtm_sb[:, sl, :],
                              x_full[kt * 128:(kt + 1) * 128, :],
                              deps=[tr_last[sl]])
                for ic in range(8):
                    o = T(transpose_op(
                        bankb(6 + sl)[:, ic * 128:(ic + 1) * 128],
                        xtm_sb[:, sl, ic * 128:(ic + 1) * 128], ident_sb),
                        deps=([ldx[sl], ld_ident, ev_fm[sl]] if ic == 0 else ()))
                tr_last[sl] = o
                ev_fm[sl] = V_(lambda e, sl=sl: e.tensor_copy(
                    xfm_sb[:, sl, :, :],
                    bankb(6 + sl).rearrange("p (a b) -> p a b", a=8)),
                    deps=[tr_last[sl]])
            if i1 >= 1:
                kt = i1 - 1
                sl = kt % 2
                zx = bank(sl).rearrange("p (a b) -> p a b", a=4)
                for ic in range(8):
                    st_, sp = (ic == 0), (ic == 7)
                    xf = xfm_sb[:, sl, ic, :]
                    dps = [ev_fm[sl], ld_wk, ld_wv, zx_ev[sl]] if ic == 0 else ()
                    T(lambda e, w=wk_sb[:, ic, 0:128], x=xf, s=st_, zx=zx:
                      e.matmul(zx[:, 0, :], w, x, start=s, stop=False), deps=dps)
                    T(lambda e, w=wk_sb[:, ic, 128:192], x=xf, zx=zx:
                      e.matmul(zx[0:64, 1, :], w, x, start=False, stop=False))
                    T(lambda e, w=wv_sb[:, ic, 128:192], x=xf, zx=zx:
                      e.matmul(zx[0:64, 3, :], w, x, start=False, stop=False))
                    mmz = T(lambda e, w=wv_sb[:, ic, 0:128], x=xf, p=sp, zx=zx:
                            e.matmul(zx[:, 2, :], w, x, start=False, stop=p))
                V_(lambda e, kt=kt, zx=zx: e.tensor_copy(
                    z0_sb[:, kt * 128:(kt + 1) * 128], zx[:, 0, :]), deps=[mmz])
                V_(lambda e, kt=kt, zx=zx: e.tensor_copy(
                    z1_sb[:, kt * 128:(kt + 1) * 128], zx[0:64, 1, :]))
                V_(lambda e, kt=kt, zx=zx: e.tensor_copy(
                    xv0_sb[:, kt * 128:(kt + 1) * 128], zx[:, 2, :]))
                zx_ev[sl] = V_(lambda e, kt=kt, zx=zx: e.tensor_copy(
                    xv1_sb[:, kt * 128:(kt + 1) * 128], zx[0:64, 3, :]))
        z_done = zx_ev[31 % 2]


        def rope_ops(ps_tm, nh, ctab, stab, out_tm, deps):
            src, dst = reim(ps_tm, nh), reim(out_tm, nh)
            cb = ctab.unsqueeze(1).broadcast_to([128, nh, 32])
            st = stab.unsqueeze(1).broadcast_to([128, nh, 32])
            r1 = rsc1_sb[:, 0:nh * 32].rearrange("p (h f) -> p h f", h=nh)
            r2 = rsc2_sb[:, 0:nh * 32].rearrange("p (h f) -> p h f", h=nh)
            V_(lambda e: e.tensor_tensor(r1, src[:, :, 0, :], cb, ALU.mult),
               deps=deps)
            V_(lambda e: e.tensor_tensor(r2, src[:, :, 1, :], st, ALU.mult))
            V_(lambda e: e.tensor_sub(dst[:, :, 0, :], r1, r2))
            V_(lambda e: e.tensor_tensor(r1, src[:, :, 0, :], st, ALU.mult))
            V_(lambda e: e.tensor_tensor(r2, src[:, :, 1, :], cb, ALU.mult))
            return V_(lambda e: e.tensor_add(dst[:, :, 1, :], r1, r2))

        ldq = [None, None]
        ev1 = [None, None]
        qd_ev = [None, None]
        rope_done = [None, None]
        qtr = [None, None]
        qmm = [None, None]
        for i2 in range(18):
            if i2 < 16:
                tt = i2
                sl = tt % 2
                ldq[sl] = ldg(xtm_sb[:, sl, :],
                              x_own[tt * 128:(tt + 1) * 128, :],
                              deps=[tr_last[sl]])
                for ic in range(8):
                    o = T(transpose_op(
                        bankb(6 + sl)[:, ic * 128:(ic + 1) * 128],
                        xtm_sb[:, sl, ic * 128:(ic + 1) * 128], ident_sb),
                        deps=([ldq[sl], ev_fm[sl] if tt < 2 else ev1[sl]]
                              if ic == 0 else ()))
                tr_last[sl] = o
                ev1[sl] = V_(lambda e, sl=sl: e.tensor_copy(
                    xfm_sb[:, sl, :, :],
                    bankb(6 + sl).rearrange("p (a b) -> p a b", a=8)),
                    deps=[tr_last[sl]])
            if 1 <= i2 <= 16:
                tt = i2 - 1
                sl = tt % 2
                qb = sl * 2
                for og in range(2):
                    for ic in range(8):
                        dps = ([ev1[sl], ld_wq, rope_done[sl], zx_ev[sl]]
                               if (og == 0 and ic == 0) else ())
                        o = T(lambda e, og=og, ic=ic, sl=sl, qb=qb:
                              e.matmul(bank(qb + og), xfm_sb[:, sl, ic, :],
                                       wq_sb[:, ic, og * 512:(og + 1) * 512],
                                       start=(ic == 0), stop=(ic == 7)), deps=dps)
                qmm[sl] = o
                rope_done[sl] = rope_ops(
                    bank2(qb), 16, cq_sb[:, tt, :], sq_sb[:, tt, :],
                    qtm_sb[:, sl, :],
                    deps=[qmm[sl], ld_cq, ld_sq, qtr[sl]])
            if i2 >= 2:
                tt = i2 - 2
                sl = tt % 2
                for p in range(8):
                    o = T(transpose_op(
                        bankb(4 + sl)[:, p * 128:(p + 1) * 128],
                        qtm_sb[:, sl, p * 128:(p + 1) * 128], ident_sb),
                        deps=([rope_done[sl], qd_ev[sl]] if p == 0 else ()))
                qtr[sl] = o
                qd_ev[sl] = V_(lambda e, sl=sl, tt=tt: e.tensor_copy(
                    qd_sb[:, :, tt * 128:(tt + 1) * 128],
                    bankb(4 + sl).rearrange("p (a b) -> p a b", a=8)),
                    deps=[qtr[sl]])
        q_done = qd_ev[15 % 2]
        q_done2 = qd_ev[14 % 2]

        fs = dict(it=0, exp=[], avs=[], y_ev=[None, None],
                  sstage_rd=None, bc_rd=[None, None], rs_rd=None, unit=0)
        y_final = None
        prev_flash_tail = [q_done, q_done2, rope_done[0], rope_done[1]]

        mask_rd = None
        for hg in range(NHG):
            dcol = slice(hg * 256, hg * 256 + 256)
            ropek = [None, None]
            trk = [None, None]
            evk = [None, None]
            vev = [None, None]
            kmm = [None, None]
            vmm = [None, None]

            def rope_pair(ps2, pi, out2, deps):
                src4 = ps2.rearrange("p t (h c f) -> p t h c f", h=4, c=2)
                dst4 = out2.rearrange("p t (h c f) -> p t h c f", h=4, c=2)
                cb = ck_sb[:, 2 * pi:2 * pi + 2, :].unsqueeze(2) \
                    .broadcast_to([128, 2, 4, 32])
                st = sk_sb[:, 2 * pi:2 * pi + 2, :].unsqueeze(2) \
                    .broadcast_to([128, 2, 4, 32])
                r1 = rsc1_sb[:, 0:256].rearrange("p (t h f) -> p t h f", t=2, h=4)
                r2 = rsc2_sb[:, 0:256].rearrange("p (t h f) -> p t h f", t=2, h=4)
                V_(lambda e: e.tensor_tensor(r1, src4[:, :, :, 0, :], cb,
                                             ALU.mult), deps=deps)
                V_(lambda e: e.tensor_tensor(r2, src4[:, :, :, 1, :], st,
                                             ALU.mult))
                V_(lambda e: e.tensor_sub(dst4[:, :, :, 0, :], r1, r2))
                V_(lambda e: e.tensor_tensor(r1, src4[:, :, :, 0, :], st,
                                             ALU.mult))
                V_(lambda e: e.tensor_tensor(r2, src4[:, :, :, 1, :], cb,
                                             ALU.mult))
                return V_(lambda e: e.tensor_add(dst4[:, :, :, 1, :], r1, r2))

            for pi in range(17):
                if pi < 16:
                    sl = pi % 2
                    b0 = sl * 2
                    for tl in range(2):
                        kt = 2 * pi + tl
                        dps = ([z_done, neg0, neg1, ld_ck, ld_sk, ropek[sl]] +
                               prev_flash_tail if pi == 0 else
                               ([ropek[sl]] if tl == 0 else []))
                        T(lambda e, kt=kt, b=b0 + tl, dc=dcol:
                          e.matmul(bank(b)[:, 0:256],
                                   z0_sb[:, kt * 128:(kt + 1) * 128],
                                   d0_sb[:, dc], start=True, stop=False),
                          deps=dps)
                        kmm[sl] = T(lambda e, kt=kt, b=b0 + tl, dc=dcol:
                                    e.matmul(bank(b)[:, 0:256],
                                             z1_sb[:, kt * 128:(kt + 1) * 128],
                                             d1_sb[:, dc], start=False,
                                             stop=True))
                        vdps = ([z_done, ld_vd0, ld_vd1, vev[kt % 2]] +
                                (prev_flash_tail if pi == 0 else []))
                        vmm[kt % 2] = T(lambda e, kt=kt, dc=dcol:
                                        e.matmul(bank(4 + kt % 2)[:, 0:256],
                                                 xv0_sb[:, kt * 128:(kt + 1) * 128],
                                                 vd0_sb[:, dc], start=True,
                                                 stop=False), deps=vdps)
                        vmm[kt % 2] = T(lambda e, kt=kt, dc=dcol:
                                        e.matmul(bank(4 + kt % 2)[:, 0:256],
                                                 xv1_sb[:, kt * 128:(kt + 1) * 128],
                                                 vd1_sb[:, dc], start=False,
                                                 stop=True))
                    ropek[sl] = rope_pair(
                        psum[:, b0:b0 + 2, 0:256], pi, ktm_sb[:, sl, :]
                        .rearrange("p (t c) -> p t c", t=2),
                        deps=[kmm[sl], trk[sl]])
                    for tl in range(2):
                        kt = 2 * pi + tl
                        vev[kt % 2] = V_(lambda e, kt=kt: e.tensor_copy(
                            v_sb[:, kt, :], bank(4 + kt % 2)[:, 0:256]),
                            deps=[vmm[kt % 2]])
                if pi >= 1:
                    pj = pi - 1
                    sl = pj % 2
                    for p in range(4):
                        o = T(transpose_op(
                            bankb(6 + sl)[:, p * 128:(p + 1) * 128],
                            ktm_sb[:, sl, p * 128:(p + 1) * 128], ident_sb),
                            deps=([ropek[sl], evk[sl]] +
                                  (prev_flash_tail if pj == 0 else [])
                                  if p == 0 else ()))
                    trk[sl] = o
                    evk[sl] = V_(lambda e, sl=sl, pj=pj: e.tensor_copy(
                        kd_sb[:, :, pj * 256:(pj + 1) * 256]
                        .rearrange("p r (t c) -> p r t c", t=2),
                        bankb(6 + sl)[:, 0:512]
                        .rearrange("p (t r c) -> p t r c", t=2, r=2)
                        .rearrange("p t r c -> p r t c")),
                        deps=[trk[sl]])
            kv_dep = [evk[0], evk[1], vev[0], vev[1]]

            mask_ld = None
            chunks = []
            for j in range(4):
                for pr in range(2):
                    for ci in range(CNT[j]):
                        chunks.append((j, pr, ci))
            stage = {}
            mask_lds = {}
            for idx in range(len(chunks) + 1):
                if idx < len(chunks):
                    j, pr, ci = chunks[idx]
                    cntj = CNT[j]
                    q0 = j * 512
                    pg = hg * 2 + pr
                    if pr == 0 and ci == 0:
                        mask_lds[j] = ldg(mask_sb,
                                          masksd[j].rearrange("s p q -> p s q"),
                                          deps=[mask_rd])
                    it = fs["it"]
                    scs, pts = it % 2, it % 4
                    sc_ap = bank2(scs * 2)
                    expdep = fs["exp"][it - 2] if it >= 2 else None
                    T(lambda e, sc=sc_ap, pr=pr, pg=pg, q0=q0, ci=ci:
                      e.matmul(sc[:, 0:512],
                               kd_sb[0:64, pr, ci * 128:(ci + 1) * 128],
                               qd_sb[0:64, pg, q0:q0 + 512],
                               start=True, stop=True),
                      deps=[expdep] + (kv_dep + [q_done] if idx == 0 else []))
                    mm_sc = T(lambda e, sc=sc_ap, pr=pr, pg=pg, q0=q0, ci=ci:
                              e.matmul(sc[:, 512:1024],
                                       kd_sb[64:128, pr, ci * 128:(ci + 1) * 128],
                                       qd_sb[64:128, pg, q0:q0 + 512],
                                       start=True, stop=True,
                                       tile_position=(64, 0)))
                    ptdep = fs["avs"][it - 4] if it >= 4 else None
                    exp_o = SC(lambda e, sc=sc_ap, pts=pts:
                               e.activation(pt_sb[:, pts, :], sc, AF.Exp,
                                            scale=0.125, bias=bias_sb),
                               deps=[mm_sc, ms_bias, ptdep])
                    fs["exp"].append(exp_o)
                    av_dep = exp_o
                    if ci >= cntj - NMSK:
                        s_ = ci - (cntj - NMSK)
                        mk = V_(lambda e, pts=pts, s_=s_: e.tensor_tensor(
                            pt_sb[:, pts, :].rearrange("p (a b) -> p a b", a=2),
                            pt_sb[:, pts, :].rearrange("p (a b) -> p a b", a=2),
                            mask_sb[:, s_, :].unsqueeze(1).broadcast_to(
                                [128, 2, 512]), ALU.mult),
                            deps=[exp_o, mask_lds[j]])
                        av_dep = mk
                        mask_rd = mk
                    stage[idx] = dict(av_dep=av_dep, pts=pts)
                    fs["it"] += 1
                if idx >= 1:
                    j, pr, ci = chunks[idx - 1]
                    cntj = CNT[j]
                    q0 = j * 512
                    pg = hg * 2 + pr
                    st_, sp = (ci == 0), (ci == cntj - 1)
                    up = fs["unit"] % 2
                    yb = 4 + 2 * up
                    sb = 5 + 2 * up
                    sg = stage.pop(idx - 1)
                    pts = sg["pts"]
                    ydeps = [sg["av_dep"]] + ([fs["y_ev"][up]] if st_ else [])
                    T(lambda e, pts=pts, pr=pr, yb=yb, s=st_, p=sp, ci=ci:
                      e.matmul(bank(yb)[0:64, :],
                               v_sb[:, ci, pr * 128:pr * 128 + 64],
                               pt_sb[:, pts, 0:512], start=s, stop=p),
                      deps=ydeps)
                    T(lambda e, pts=pts, pr=pr, yb=yb, s=st_, p=sp, ci=ci:
                      e.matmul(bank(yb)[64:128, :],
                               v_sb[:, ci, pr * 128 + 64:pr * 128 + 128],
                               pt_sb[:, pts, 512:1024], start=s, stop=p,
                               tile_position=(0, 64)))
                    T(lambda e, pts=pts, sb=sb, s=st_, p=sp:
                      e.matmul(bank(sb)[0:1, :], ones_sb, pt_sb[:, pts, 0:512],
                               start=s, stop=p),
                      deps=[ld_ones] + ([fs["bc_rd"][up]] if st_ else []))
                    mm_last = T(lambda e, pts=pts, sb=sb, s=st_, p=sp:
                                e.matmul(bank(sb)[32:33, :], ones_sb,
                                         pt_sb[:, pts, 512:1024],
                                         start=s, stop=p,
                                         tile_position=(0, 32)))
                    fs["avs"].append(mm_last)
                    if sp:
                        if fs["unit"] < 2:
                            V_(lambda e, sb=sb: e.tensor_copy(
                                sstage_sb[0:1, :], bank(sb)[0:1, :]),
                               deps=[mm_last, fs["sstage_rd"]])
                            cpB = V_(lambda e, sb=sb: e.tensor_copy(
                                sstage_sb[32:33, :], bank(sb)[32:33, :]))
                        else:
                            cpB = V_(lambda e, sb=sb: e.tensor_copy(
                                sstage_sb[0:33, :], bank(sb)[0:33, :]),
                                deps=[mm_last, fs["sstage_rd"]])
                        bc = T(lambda e, sb=sb: e.matmul(bank(sb), sel_sb,
                                                         sstage_sb,
                                                         start=True, stop=True),
                               deps=[cpB, ld_sel, ms_sst, fs["bc_rd"][up]])
                        fs["sstage_rd"] = bc
                        rc = V_(lambda e, sb=sb: e.reciprocal(rs_sb, bank(sb)),
                                deps=[bc, fs["rs_rd"]])
                        fs["bc_rd"][up] = rc
                        yev = V_(lambda e, pg=pg, q0=q0, yb=yb: e.tensor_tensor(
                            y_sb[:, pg, q0:q0 + 512], bank(yb), rs_sb, ALU.mult),
                            deps=[rc])
                        fs["rs_rd"] = yev
                        fs["y_ev"][up] = yev
                        y_final = yev
                        fs["unit"] += 1
            prev_flash_tail = [fs["exp"][-1], fs["avs"][-1], fs["bc_rd"][0],
                               fs["bc_rd"][1], fs["y_ev"][0], fs["y_ev"][1]]
        attn_done = y_final

        dbg_ops = []
        if dbg:
            dbg_ops.append(G(lambda e: e.dma_start(out=dbg_o["dbg_q"], in_=qd_sb),
               deps=[q_done]))
            dbg_ops.append(G(lambda e: e.dma_start(out=dbg_o["dbg_k"], in_=kd_sb),
               deps=[attn_done]))
            dbg_ops.append(G(lambda e: e.dma_start(out=dbg_o["dbg_v"], in_=v_sb),
               deps=[attn_done]))
            dbg_ops.append(G(lambda e: e.dma_start(out=dbg_o["dbg_y"], in_=y_sb),
               deps=[attn_done]))
            dbg_ops.append(G(lambda e: e.dma_start(out=dbg_o["dbg_z"], in_=z0_sb),
               deps=[z_done]))

        ld_wproj = ldg(wproj_sb, wprojd, deps=list(kv_dep) + dbg_ops)

        ln_prev = [None]

        def ln_fused(bankap, xres, xmid, xout, deps1, deps9):
            add_o = V_(lambda e: e.scalar_tensor_tensor(
                xmid, bankap, 1.0, xres, ALU.mult, ALU.add,
                accum_out=col_sb[:, 0:1]), deps=deps1)
            V_(lambda e: e.scalar_tensor_tensor(stat_sb, xmid, 1.0, xmid,
                                                ALU.mult, ALU.mult,
                                                accum_out=col_sb[:, 1:2]))
            V_(lambda e: e.tensor_scalar_mul(col_sb[:, 2:3], col_sb[:, 0:1],
                                             1.0 / D))
            V_(lambda e: e.scalar_tensor_tensor(col_sb[:, 4:5], col_sb[:, 2:3],
                                                -1.0, col_sb[:, 2:3], ALU.mult,
                                                ALU.mult))
            var = V_(lambda e: e.tensor_scalar(col_sb[:, 5:6], col_sb[:, 1:2],
                                               1.0 / D, col_sb[:, 4:5],
                                               ALU.mult, ALU.add))
            srt = SC(lambda e: e.activation(col_sb[:, 6:7], col_sb[:, 5:6],
                                            AF.Sqrt, bias=eps_sb),
                     deps=[var, ms_eps])
            rst = V_(lambda e: e.reciprocal(col_sb[:, 7:8], col_sb[:, 6:7]),
                     deps=[srt, ln_prev[0]])
            nb = V_(lambda e: e.scalar_tensor_tensor(
                col_sb[:, 3:4], col_sb[:, 2:3], -1.0, col_sb[:, 7:8],
                ALU.mult, ALU.mult), deps=[rst])
            ap = SC(lambda e: e.activation(xout, xmid, AF.Identity,
                                           scale=col_sb[:, 7:8],
                                           bias=col_sb[:, 3:4]),
                    deps=deps9 + [nb])
            ln_prev[0] = ap
            return add_o, ap

        x1ld = [None, None]
        x1cast = [None, None]
        x1tr = [None, None]
        x1ev = [None, None]
        ldw12 = [None, None]
        w12_rd = [None, None]
        ldh = [None, None, None]
        wbuf_rd = [None, None, None]
        ldw3s = [None, None]
        w3_rd = None
        w3_rd2 = [None, None]
        outw = [None, None]
        addx_bank = {}
        rf = [None, None, None]
        fev = [None] * 8
        x1w_all = []
        x1ld2_last = [None]

        def fm_tile(tt, q8, fsub_war, pace=None):
            sl = tt % 2
            x1ld[sl] = ldg(x1f_sb[:, sl, :],
                           x1sp[tt * 128:(tt + 1) * 128, :],
                           deps=[x1w_all[tt], rf[sl], pace])
            x1cast[sl] = V_(lambda e, sl=sl: e.tensor_copy(
                x1b_sb[:, sl, :], x1f_sb[:, sl, :]), deps=[x1ld[sl]])
            rf[sl] = x1cast[sl]
            for ic in range(8):
                o = T(transpose_op(
                    bankb(6 + sl)[:, ic * 128:(ic + 1) * 128],
                    x1b_sb[:, sl, ic * 128:(ic + 1) * 128], ident_sb),
                    deps=([x1cast[sl], x1ev[sl], w12_rd[0], w12_rd[1],
                           outw[0], outw[1], fsub_war] +
                          ([attn_done] if tt < 2 else [])
                          if ic == 0 else ()))
            x1tr[sl] = o
            x1ev[sl] = V_(lambda e, sl=sl, q8=q8: e.tensor_copy(
                x1fm_sb[:, :, q8 * 128:(q8 + 1) * 128],
                bankb(6 + sl).rearrange("p (a b) -> p a b", a=8)),
                deps=[x1tr[sl]])

        ldo = [None, None]
        addo = [None, None]
        x1w = [None, None]
        for tt in range(16):
            sl = tt % 2
            ldo[sl] = lds(xo_sb[:, sl, :], x_own[tt * 128:(tt + 1) * 128, :],
                          deps=[addo[sl]] + ([attn_done] + dbg_ops if tt < 2 else []))
            for og in range(2):
                for ch in range(8):
                    dps = ([attn_done, ld_wproj, addo[sl]]
                           if og == 0 and ch == 0 else ())
                    o = T(lambda e, ch=ch, og=og, sl=sl, tt=tt:
                          e.matmul(bank(sl * 2 + og),
                                   y_sb[:, ch, tt * 128:(tt + 1) * 128],
                                   wproj_sb[:, ch, og * 512:(og + 1) * 512],
                                   start=(ch == 0), stop=(ch == 7)), deps=dps)
            pmm = o
            addo[sl], lt = ln_fused(
                bank2(sl * 2), xo_sb[:, sl, :], x1p_sb[:, sl, :],
                x1_sb[:, sl, :],
                deps1=[pmm, ldo[sl], x1w[sl]], deps9=[x1w[sl]])
            x1w[sl] = SY(lambda e, sl=sl, tt=tt: e.dma_start(
                out=x1sp[tt * 128:(tt + 1) * 128, :], in_=x1_sb[:, sl, :]),
                deps=[lt])
            x1w_all.append(x1w[sl])
            if dbg:
                SY(lambda e, sl=sl, tt=tt: e.dma_start(
                    out=dbg_o["dbg_x1"][tt * 128:(tt + 1) * 128, :],
                    in_=x1_sb[:, sl, :]), deps=[lt])
            if tt >= 2 and tt < 10:
                fm_tile(tt - 2, tt - 2, None)
        x1fm_done_tg0 = [x1ev[0], x1ev[1]]

        af_war = None
        fsub_rd = None
        dbg_done = False
        for tg in range(2):
            t0 = tg * 1024
            sil = [None, None, None, None]
            aev = [None, None, None, None]
            if tg == 0:
                x1fm_done = x1fm_done_tg0
            else:
                for q8 in range(8):
                    fm_tile(8 + q8, q8, fsub_rd,
                            pace=(x1ld2_last[0] if q8 < 2 else None))
                x1fm_done = [x1ev[0], x1ev[1]]
            for oc in range(32):
                wsl = oc % 2
                ldw12[wsl] = ldg(w12b_sb[:, wsl, :, :, :], w12d[oc],
                                 deps=[w12_rd[wsl], w3_rd] +
                                 ([attn_done, x1w_all[7]]
                                  if tg == 0 and oc < 2 else []) +
                                 ([x1ld2_last[0]]
                                  if tg == 1 and oc < 2 else []))
                for half in range(2):
                    pb = half * 4 + wsl * 2
                    hs = slice(half * 512, (half + 1) * 512)
                    ss = half * 2 + wsl
                    for w_ in range(2):
                        for ic in range(8):
                            dps = ([ldw12[wsl]] + x1fm_done +
                                   [sil[ss], aev[ss]] +
                                   ([addo[0], addo[1]]
                                    if tg == 0 and oc < 2 else []))                                 if (w_ == 0 and ic == 0) else ()
                            o = T(lambda e, w_=w_, ic=ic, pb=pb, wsl=wsl, hs=hs:
                                  e.matmul(bank(pb + w_),
                                           w12b_sb[:, wsl, w_, ic, :],
                                           x1fm_sb[:, ic, hs],
                                           start=(ic == 0), stop=(ic == 7)),
                                  deps=dps)
                    w12_rd[wsl] = o
                    sil[ss] = SC(lambda e, pb=pb, oc=oc, ss=ss:
                                 e.activation(silu_sb[:, ss, :], bank(pb),
                                              AF.Silu,
                                              scale=s1_sb[:, oc:oc + 1]),
                                 deps=[w12_rd[wsl], ld_s1, aev[ss]])
                    aev[ss] = V_(lambda e, pb=pb, oc=oc, ss=ss, hs=hs:
                                 e.scalar_tensor_tensor(
                                     af_sb[:, oc, hs], bank(pb + 1),
                                     s2_sb[:, oc:oc + 1], silu_sb[:, ss, :],
                                     ALU.mult, ALU.mult),
                                 deps=[sil[ss], ld_s2, af_war] +
                                 ([x1w_all[15]]
                                  if tg == 0 and oc == 16 else []))
            a_done = [aev[0], aev[1], aev[2], aev[3]]
            if dbg and tg == 0:
                G(lambda e: e.dma_start(out=dbg_o["dbg_af"], in_=af_sb),
                  deps=a_done)

            def h_layer(src_sb, dst_sb, wsrc, scale_sb, scale_ld, dep_start,
                        ld_gate=()):
                gact = [None, None, None, None]
                for oc in range(32):
                    wsl = oc % 3
                    ldh[wsl] = ldg(wbuf_sb[:, wsl, :, :], wsrc[oc],
                                   deps=[wbuf_rd[wsl]] + list(ld_gate) +
                                   ([outw[0], outw[1]] if wsl == 2 else []))
                    for half in range(2):
                        pb = (oc * 2 + half) % 4
                        hs = slice(half * 512, (half + 1) * 512)
                        for rc in range(32):
                            dps = ([ldh[wsl], gact[pb]] + dep_start)                                 if rc == 0 else ()
                            o = T(lambda e, rc=rc, pb=pb, wsl=wsl, hs=hs:
                                  e.matmul(bank(pb), wbuf_sb[:, wsl, rc, :],
                                           src_sb[:, rc, hs],
                                           start=(rc == 0), stop=(rc == 31)),
                                  deps=dps)
                        wbuf_rd[wsl] = o
                        gact[pb] = SC(lambda e, oc=oc, pb=pb, hs=hs:
                                      e.activation(dst_sb[:, oc, hs], bank(pb),
                                                   AF.Gelu,
                                                   scale=scale_sb[:, oc:oc + 1]),
                                      deps=[wbuf_rd[wsl], scale_ld])
                return [gact[0], gact[1], gact[2], gact[3]]

            g_done = h_layer(af_sb, g_sb, h1wd, h1s_sb, ld_h1s,
                             a_done[2:],
                             ld_gate=([x1w[0], x1w[1]] if tg == 0 else ()))
            g2_done = h_layer(g_sb, af_sb, h2wd, h2s_sb, ld_h2s, g_done)

            if dbg and not dbg_done:
                G(lambda e: e.dma_start(out=dbg_o["dbg_a"], in_=g_sb),
                  deps=g_done)
                G(lambda e: e.dma_start(out=dbg_o["dbg_g"], in_=af_sb),
                  deps=g2_done)
                dbg_done = True

            for half in range(2):
                th0 = t0 + half * 512
                hs = slice(half * 512, (half + 1) * 512)
                for oc in range(8):
                    pb = oc % 2
                    wb3 = w3b_sb if oc % 2 == 0 else w3b2_sb
                    ldw3s[oc % 2] = ldg(wb3, w3wd[oc],
                                        deps=[w3_rd2[oc % 2], a_done[3],
                                              w12_rd[0], w12_rd[1]] +
                                        ([attn_done, x1w[0], x1w[1]]
                                         if tg == 0 and half == 0 and oc == 0
                                         else []))
                    for rc in range(32):
                        dps = ([ldw3s[oc % 2],
                                fev[oc - 2] if oc >= 2 else None,
                                fev[oc - 2 + 8] if oc < 2 else None] + g2_done) \
                            if rc == 0 else ()
                        o = T(lambda e, rc=rc, pb=pb, hs=hs, wb3=wb3:
                              e.matmul(bank(pb), wb3[:, rc, :],
                                       af_sb[:, rc, hs],
                                       start=(rc == 0), stop=(rc == 31)),
                              deps=dps)
                    w3_rd2[oc % 2] = o
                    w3_rd = o
                    fev[oc] = V_(lambda e, oc=oc, pb=pb: e.tensor_scalar_mul(
                        fsub_sb[:, oc, :], bank(pb), w3s_sb[:, oc:oc + 1]),
                        deps=[w3_rd, ld_w3s, fsub_rd])
                for q in range(4):
                    tt = th0 // 128 + q
                    sl = tt % 2
                    s3 = tt % 3
                    x1ld2 = lds(x1f_sb[:, s3, :],
                                x1sp[tt * 128:(tt + 1) * 128, :],
                                deps=[rf[s3], fev[4 + q]])
                    x1ld2_last[0] = x1ld2
                    xb = 4 if q % 2 == 0 else 6
                    for oc in range(8):
                        o = T(transpose_op(
                            bank2(xb)[:, oc * 128:(oc + 1) * 128],
                            fsub_sb[:, oc, q * 128:(q + 1) * 128], identf_sb),
                            deps=([fev[7], ld_identf, addx_bank.get(xb)]
                                  if oc == 0 else ()))
                    ftr = o
                    if q == 3:
                        fsub_rd = ftr
                    addx, lt2 = ln_fused(
                        bank2(xb), x1f_sb[:, s3, :], x1f_sb[:, s3, :],
                        x2_sb[:, sl, :],
                        deps1=[ftr, x1ld2], deps9=[outw[sl]])
                    addx_bank[xb] = addx
                    rf[s3] = lt2
                    outw[sl] = SY(lambda e, sl=sl, tt=tt: e.dma_start(
                        out=out[tt * 128:(tt + 1) * 128, :], in_=x2_sb[:, sl, :]),
                        deps=[lt2])
            af_war = w3_rd

        P.add("sync", lambda e: e.nop(), deps=[outw[0], outw[1]])

        block = es.enter_context(nc.Block())
        P.emit(nc, block, es)
    return nc


def _host_prep(inputs):
    f32 = np.float32
    Wq = np.asarray(inputs["L0_attn_Wq"], f32)
    Wk = np.asarray(inputs["L0_attn_Wk"], f32)
    Wv = np.asarray(inputs["L0_attn_Wv"], f32)
    kdec = np.asarray(inputs["L0_attn_key_decoder"], f32)
    vdec = np.asarray(inputs["L0_attn_value_decoder"], f32)
    Wproj = np.asarray(inputs["L0_attn_Wproj"], f32)
    w1w = np.asarray(inputs["L0_ffn_w1_w"], f32)
    w2w = np.asarray(inputs["L0_ffn_w2_w"], f32)
    h1w = np.asarray(inputs["L0_ffn_h1_w"], f32)
    h2w = np.asarray(inputs["L0_ffn_h2_w"], f32)
    w3w = np.asarray(inputs["L0_ffn_w3_w"], f32)

    com = {}
    hh, cc, ff = np.meshgrid(np.arange(H), np.arange(2), np.arange(NF),
                             indexing="ij")
    qperm = (64 * hh + 2 * ff + cc).reshape(-1)
    com["wq"] = np.ascontiguousarray(Wq[:, qperm].reshape(8, 128, D)
                                     .transpose(1, 0, 2))
    ffk, rrk = np.meshgrid(np.arange(NF), np.arange(KR), indexing="ij")
    kperm = np.concatenate([(6 * ffk + 2 * rrk + 0).reshape(-1),
                            (6 * ffk + 2 * rrk + 1).reshape(-1)])
    com["wk"] = np.ascontiguousarray(Wk[:, kperm].reshape(8, 128, VR)
                                     .transpose(1, 0, 2))
    com["wv"] = np.ascontiguousarray(Wv.reshape(8, 128, VR).transpose(1, 0, 2))
    Dm = np.zeros((VR, D), f32)
    br, bi = kdec[..., 0], kdec[..., 1]
    fi, hi, ri = np.meshgrid(np.arange(NF), np.arange(H), np.arange(KR),
                             indexing="ij")
    for c, top, bot in ((0, br, bi), (1, bi, br)):
        Dm[3 * fi + ri, 64 * hi + 32 * c + fi] = top
        Dm[96 + 3 * fi + ri, 64 * hi + 32 * c + fi] = bot
    com["d0"] = np.ascontiguousarray(Dm[:128])
    com["d1"] = np.ascontiguousarray(Dm[128:])
    VD = np.ascontiguousarray(vdec.transpose(1, 0, 2).reshape(VR, D))
    com["vd0"] = np.ascontiguousarray(VD[:128])
    com["vd1"] = np.ascontiguousarray(VD[128:])
    com["wproj"] = np.ascontiguousarray(Wproj.reshape(8, 128, D)
                                        .transpose(1, 0, 2))
    w1s_ = w1w.reshape(32, 128, 8, 128).transpose(0, 3, 2, 1)
    w2s_ = w2w.reshape(32, 128, 8, 128).transpose(0, 3, 2, 1)
    com["w12"] = np.ascontiguousarray(np.stack([w1s_, w2s_], 2))
    com["h1w"] = np.ascontiguousarray(h1w.reshape(32, 128, 32, 128)
                                      .transpose(0, 3, 2, 1))
    com["h2w"] = np.ascontiguousarray(h2w.reshape(32, 128, 32, 128)
                                      .transpose(0, 3, 2, 1))
    com["w3w"] = np.ascontiguousarray(w3w.reshape(8, 128, 32, 128)
                                      .transpose(0, 3, 2, 1))
    com["s1s"] = np.ascontiguousarray(
        np.asarray(inputs["L0_ffn_w1_s"], f32).reshape(32, 128).T)
    com["s2s"] = np.ascontiguousarray(
        np.asarray(inputs["L0_ffn_w2_s"], f32).reshape(32, 128).T)
    com["h1s"] = np.ascontiguousarray(
        np.asarray(inputs["L0_ffn_h1_s"], f32).reshape(32, 128).T)
    com["h2s"] = np.ascontiguousarray(
        np.asarray(inputs["L0_ffn_h2_s"], f32).reshape(32, 128).T)
    com["w3s"] = np.ascontiguousarray(
        np.asarray(inputs["L0_ffn_w3_s"], f32).reshape(8, 128).T)
    com["ident"] = np.eye(128, dtype=f32)
    sel = np.zeros((33, 128), f32)
    sel[0, 0:64] = 1.0
    sel[32, 64:128] = 1.0
    com["sel33"] = sel
    com["onesb"] = np.ones((128, 1), f32)
    half = HD // 2
    freq = (1.0 / ROPE_BASE ** (np.arange(half, dtype=f32) / half)).astype(f32)
    freqs = np.outer(np.arange(S, dtype=f32), freq)
    COS = np.cos(freqs).astype(f32)
    SIN = np.sin(freqs).astype(f32)
    com["ck"] = np.ascontiguousarray(COS.reshape(32, 128, 32).transpose(1, 0, 2))
    com["sk"] = np.ascontiguousarray(SIN.reshape(32, 128, 32).transpose(1, 0, 2))
    return com, COS, SIN


def _core_inputs(inputs, com, COS, SIN, core):
    b, hlf = core // 2, core % 2
    x = np.asarray(inputs["L0_attn_x"], np.float32)
    groups = [2 * j + 1 - hlf for j in range(4)]
    rows = np.concatenate([np.arange(g * 512, (g + 1) * 512) for g in groups])
    d = dict(com)
    d["x_own"] = np.ascontiguousarray(x[b][rows])
    d["x_full"] = np.ascontiguousarray(x[b])
    d["cq"] = np.ascontiguousarray(COS[rows].reshape(16, 128, 32)
                                   .transpose(1, 0, 2))
    d["sq"] = np.ascontiguousarray(SIN[rows].reshape(16, 128, 32)
                                   .transpose(1, 0, 2))
    m = np.zeros((4, NMSK, 128, 512), np.float32)
    for j in range(4):
        q0 = groups[j] * 512
        for s_ in range(NMSK):
            k0 = (CNT[j] - NMSK + s_) * 128
            kk = k0 + np.arange(128)[:, None]
            qq = q0 + np.arange(512)[None, :]
            m[j, s_] = (kk <= qq).astype(np.float32)
    d["masks"] = m
    return d, groups


def kernel(**inputs):
    com, COS, SIN = _host_prep(inputs)
    in_maps, group_list = [], []
    for c in range(8):
        d, groups = _core_inputs(inputs, com, COS, SIN, c)
        in_maps.append(d)
        group_list.append(groups)
    nc = build_nc()
    res = run_bass_kernel_spmd(nc, in_maps, core_ids=list(range(8)))
    outa = np.empty((BSZ, S, D), np.float32)
    for c in range(8):
        b = c // 2
        shard = np.asarray(res.results[c]["out"], np.float32).reshape(TQ, D)
        for j, g in enumerate(group_list[c]):
            outa[b, g * 512:(g + 1) * 512] = shard[j * 512:(j + 1) * 512]
    return outa


# version 19
# speedup vs baseline: 1.0541x; 1.0246x over previous
import contextlib
import numpy as np
import concourse.bass as bass
import concourse.mybir as mybir
from concourse.bass_utils import run_bass_kernel_spmd

F32 = mybir.dt.float32
BF16 = mybir.dt.bfloat16
AF = mybir.ActivationFunctionType
ALU = mybir.AluOpType

S, D, BSZ, H, HD, NF, KR, VR, DH = 4096, 1024, 4, 16, 64, 32, 3, 192, 4096
TQ = 2048
CNT = (8, 16, 24, 32)
NMSK = 8
EXPB = 12.0
LNEPS = 1e-5
ROPE_BASE = 10000.0
NHG = 4
NTG = 4

ENGS = ("tensor", "vector", "scalar", "gpsimd", "sync")


class Prog:

    def __init__(self):
        self.ops = []
        self.per_eng = {e: [] for e in ENGS}

    def add(self, eng, fn, deps=(), dma=False):
        i = len(self.ops)
        self.ops.append((eng, fn, tuple(d for d in deps if d is not None), dma))
        self.per_eng[eng].append(i)
        return i

    def emit(self, nc, block, es):
        need = set()
        for i, (eng, fn, deps, dma) in enumerate(self.ops):
            if dma:
                need.add(i)
            if eng in ("vector", "scalar"):
                need.add(i)
            for d in deps:
                pe, _, _, pdma = self.ops[d]
                if pdma or pe != eng:
                    need.add(d)
        sems = {e: es.enter_context(nc.semaphore("S" + e)) for e in ENGS}
        for e in ENGS:
            for s in range(3):
                sems[(e, s)] = es.enter_context(nc.semaphore(f"S{e}{s}"))
        val, cnt = {}, {e: 0 for e in ENGS}
        for e in ENGS:
            for i in self.per_eng[e]:
                if i in need and not self.ops[i][3]:
                    cnt[e] += 1
                    val[i] = cnt[e]
        waits = []
        for i, (eng, fn, deps, dma) in enumerate(self.ops):
            w = {}
            for d in deps:
                if d in val and not self.ops[d][3]:
                    pe = self.ops[d][0]
                    if pe != eng:
                        w[pe] = max(w.get(pe, 0), val[d])
            if eng == "vector" and val.get(i, 1) > 1:
                w[eng] = max(w.get(eng, 0), val[i] - 1)
            waits.append(w)
        B, R = 2, 3
        self.stripe_of = {}
        self.stripe_val = {}
        stripe_gate = {}
        for e in ENGS:
            dmas = [i for i in self.per_eng[e] if self.ops[i][3]]
            epochs, cur = [], []
            for i in dmas:
                boundary = len(cur) >= B
                if not boundary and cur:
                    start = cur[0]
                    if any(d >= start for d in self.ops[i][2]):
                        boundary = True
                if boundary and cur:
                    epochs.append(cur)
                    cur = []
                cur.append(i)
            if cur:
                epochs.append(cur)
            stot = [0] * R
            for k, ep in enumerate(epochs):
                s = k % R
                if stot[s] > 0:
                    stripe_gate[ep[0]] = (s, stot[s])
                stot[s] += 16 * len(ep)
                for i in ep:
                    self.stripe_of[i] = (e, s)
                    self.stripe_val[i] = stot[s]
        self._stripe_gate = stripe_gate
        waits2 = []
        for i, (eng, fn, deps, dma) in enumerate(self.ops):
            w = {}
            for pe, v in waits[i].items():
                w[pe] = max(w.get(pe, 0), v)
            for d in deps:
                if d in self.stripe_of:
                    pe, s = self.stripe_of[d]
                    key = (pe, s)
                    w[key] = max(w.get(key, 0), self.stripe_val[d])
            if i in self._stripe_gate:
                s, v = self._stripe_gate[i]
                key = (eng, s)
                w[key] = max(w.get(key, 0), v)
            waits2.append(w)
        waits = waits2
        seen = {e: {} for e in ENGS}
        for e in ENGS:
            for i in self.per_eng[e]:
                w = waits[i]
                for pe in list(w):
                    if seen[e].get(pe, 0) >= w[pe]:
                        del w[pe]
                    else:
                        seen[e][pe] = w[pe]
        for e in ENGS:
            def body(eng, _e=e):
                for i in self.per_eng[_e]:
                    _, fn, _, dma = self.ops[i]
                    for pe, v in sorted(waits[i].items(), key=str):
                        eng.wait_ge(sems[pe], v)
                    ins = fn(eng)
                    if dma:
                        ins.then_inc(sems[self.stripe_of[i]], 16)
                    elif i in val:
                        ins.then_inc(sems[_e], 1)
            getattr(block, e)(body)


class Carver:
    def __init__(self, pool_ap, limit, off=0):
        self.pool = pool_ap
        self.off = off
        self.limit = limit

    def take(self, shape, dt):
        p = shape[0]
        elems = int(np.prod(shape[1:]))
        width = elems * (2 if dt == F32 else 1)
        self.off = (self.off + 1) & ~1
        a = self.pool[0:p, self.off:self.off + width]
        self.off += width
        assert self.off <= self.limit, f"sbuf pool overflow {self.off} > {self.limit}"
        if dt == F32:
            a = a.bitcast(F32)
        if len(shape) == 3:
            a = a.rearrange("p (a b) -> p a b", a=shape[1])
        elif len(shape) == 4:
            a = a.rearrange("p (a b c) -> p a b c", a=shape[1], b=shape[2])
        elif len(shape) == 5:
            a = a.rearrange("p (a b c d) -> p a b c d", a=shape[1], b=shape[2],
                            c=shape[3])
        else:
            assert len(shape) == 2
        return a


def build_nc(dbg=False):
    nc = bass.Bass("TRN2", target_bir_lowering=False,
                   detect_race_conditions=False)

    def din(name, shape):
        return nc.dram_tensor(name, shape, F32, kind="ExternalInput").ap()

    x_own = din("x_own", [TQ, D])
    x_full = din("x_full", [S, D])
    cqd, sqd = din("cq", [128, 16, 32]), din("sq", [128, 16, 32])
    ckd, skd = din("ck", [128, 32, 32]), din("sk", [128, 32, 32])
    masksd = din("masks", [4, NMSK, 128, 512])
    wqd = din("wq", [128, 8, D])
    wkd = din("wk", [128, 8, VR])
    wvd = din("wv", [128, 8, VR])
    d0d, d1d = din("d0", [128, D]), din("d1", [64, D])
    vd0d, vd1d = din("vd0", [128, D]), din("vd1", [64, D])
    wprojd = din("wproj", [128, 8, D])
    w12d = din("w12", [32, 128, 2, 8, 128])
    h1wd = din("h1w", [32, 128, 32, 128])
    h2wd = din("h2w", [32, 128, 32, 128])
    w3wd = din("w3w", [8, 128, 32, 128])
    s1d, s2d = din("s1s", [128, 32]), din("s2s", [128, 32])
    h1sd, h2sd = din("h1s", [128, 32]), din("h2s", [128, 32])
    w3sd = din("w3s", [128, 8])
    identd = din("ident", [128, 128])
    sel33d = din("sel33", [33, 128])
    onesd = din("onesb", [128, 1])

    out = nc.dram_tensor("out", [TQ, D], F32, kind="ExternalOutput").ap()
    x1sp = nc.dram_tensor("x1sp", [TQ, D], F32, kind="ExternalOutput").ap()
    dbg_o = {}
    if dbg:
        for nm, shp in (("dbg_q", [128, 8, TQ]), ("dbg_k", [128, 2, S]),
                        ("dbg_v", [128, 32, 256]), ("dbg_y", [128, 8, TQ]),
                        ("dbg_x1", [TQ, D]), ("dbg_z", [128, S]),
                        ("dbg_a", [128, 32, 1024]), ("dbg_g", [128, 32, 1024]),
                        ("dbg_af", [128, 32, 1024])):
            dbg_o[nm] = nc.dram_tensor(nm, shp, F32, kind="ExternalOutput").ap()

    P = Prog()

    def G(fn, deps=()):
        return P.add("gpsimd", fn, deps, dma=True)

    def SY(fn, deps=()):
        return P.add("sync", fn, deps, dma=True)

    def T(fn, deps=()):
        return P.add("tensor", fn, deps)

    def V_(fn, deps=()):
        return P.add("vector", fn, deps)

    def SC(fn, deps=()):
        return P.add("scalar", fn, deps)

    def ldg(dst, src, deps=()):
        return G(lambda e, a=dst, b=src: e.dma_start(out=a, in_=b), deps)

    def lds(dst, src, deps=()):
        return SY(lambda e, a=dst, b=src: e.dma_start(out=a, in_=b), deps)

    es = contextlib.ExitStack()
    with es:
        POOLW = 106416
        pool = es.enter_context(nc.sbuf_tensor("pool", [128, POOLW], BF16)).ap()
        psum = es.enter_context(nc.psum_tensor("psum", [128, 8, 512], F32)).ap()

        def bank(b):
            return psum[:, b, :]

        def bank2(b):
            return psum[:, b:b + 2, :].rearrange("p a b -> p (a b)")

        def bankb(b):
            return psum[:, b, :].bitcast(BF16)

        C = Carver(pool, POOLW)
        ident_sb = C.take([128, 128], BF16)
        identf_sb = C.take([128, 128], F32)
        eps_sb = C.take([128, 1], F32)
        s1_sb = C.take([128, 32], F32)
        s2_sb = C.take([128, 32], F32)
        h1s_sb = C.take([128, 32], F32)
        h2s_sb = C.take([128, 32], F32)
        w3s_sb = C.take([128, 8], F32)
        col_sb = C.take([128, 8], F32)
        a_live_end = C.off
        ck_sb = C.take([128, 32, 32], F32)
        sk_sb = C.take([128, 32, 32], F32)
        cq_sb = C.take([128, 16, 32], F32)
        sq_sb = C.take([128, 16, 32], F32)
        mask_sb = C.take([128, NMSK, 512], BF16)
        sel_sb = C.take([33, 128], F32)
        ones_sb = C.take([128, 1], BF16)
        bias_sb = C.take([128, 1], F32)
        pt_sb = C.take([128, 4, 1024], BF16)
        sstage_sb = C.take([33, 512], F32)
        lnr_sb = C.take([128, 512], F32)
        rs_sb = C.take([128, 512], F32)
        wk_sb = C.take([128, 8, VR], BF16)
        wv_sb = C.take([128, 8, VR], BF16)
        d0_sb = C.take([128, D], BF16)
        d1_sb = C.take([64, D], BF16)
        vd0_sb = C.take([128, D], BF16)
        vd1_sb = C.take([64, D], BF16)
        a_end = C.off
        qd_sb = C.take([128, 8, TQ], BF16)
        y_sb = C.take([128, 8, TQ], BF16)
        bc_end = C.off
        z0_sb = C.take([128, S], BF16)
        z1_sb = C.take([64, S], BF16)
        xv0_sb = C.take([128, S], BF16)
        xv1_sb = C.take([64, S], BF16)
        kd_off = C.off
        kd_sb = C.take([128, 2, S], BF16)
        v_sb = C.take([128, 32, 256], BF16)
        d_stream = C.off
        wq_sb = Carver(pool, d_stream, off=kd_off).take([128, 8, D], BF16)
        xtm_sb = C.take([128, 2, D], BF16)
        xfm_sb = C.take([128, 2, 8, 128], BF16)
        qtm_sb = C.take([128, 2, D], BF16)
        ktm_sb = C.take([128, 2, 512], BF16)
        rsc1_sb = C.take([128, 512], F32)
        rsc2_sb = C.take([128, 512], F32)
        attn_peak = C.off
        wproj_sb = Carver(pool, bc_end + 8192, off=bc_end).take([128, 8, D],
                                                               BF16)
        CP = Carver(pool, POOLW, off=kd_off)
        xo_sb = CP.take([128, 2, D], F32)
        x1p_sb = CP.take([128, 2, D], F32)
        x1_sb = CP.take([128, 2, D], F32)
        CF1 = Carver(pool, a_end, off=a_live_end)
        stat_sb = CF1.take([128, D], F32)
        wbuf_off = CF1.off
        wbuf_sb = CF1.take([128, 3, 32, 128], BF16)
        w12b_off = CF1.off
        w12b_sb = CF1.take([128, 2, 2, 8, 128], BF16)
        sw3_off = CF1.off
        silu_sb = CF1.take([128, 4, 512], F32)
        x1b_sb = Carver(pool, a_end, off=wbuf_off).take([128, 2, D], BF16)
        CF2 = Carver(pool, bc_end, off=a_end)
        g_sb = CF2.take([128, 32, 1024], BF16)
        w3b_sb = Carver(pool, a_end, off=sw3_off).take([128, 32, 128], BF16)
        CF3 = Carver(pool, POOLW, off=bc_end)
        af_sb = CF3.take([128, 32, 1024], BF16)
        x1fm_off = CF3.off
        x1fm_sb = CF3.take([128, 8, 1024], BF16)
        x1f_sb = CF3.take([128, 3, D], F32)
        ffn_peak = CF3.off
        fsub_sb = Carver(pool, POOLW, off=x1fm_off).take([128, 8, 512], F32)
        x2_sb = Carver(pool, a_end,
                       off=wbuf_off + 2 * 4096).take([128, 2, D], F32)
        w3b2_sb = Carver(pool, a_end, off=w12b_off).take([128, 32, 128], BF16)

        ld_wk = ldg(wk_sb, wkd)
        ld_wv = ldg(wv_sb, wvd)
        ld_d0 = ldg(d0_sb, d0d)
        ld_d1 = ldg(d1_sb, d1d)
        ld_vd0 = ldg(vd0_sb, vd0d)
        ld_vd1 = ldg(vd1_sb, vd1d)
        ld_ident = ldg(ident_sb, identd)
        ld_ones = ldg(ones_sb, onesd)
        ld_wq = ldg(wq_sb, wqd)
        ld_ck = lds(ck_sb, ckd)
        ld_sk = lds(sk_sb, skd)
        ld_cq = lds(cq_sb, cqd)
        ld_sq = lds(sq_sb, sqd)
        ld_identf = lds(identf_sb, identd)
        ld_sel = lds(sel_sb, sel33d)
        ld_s1 = lds(s1_sb, s1d)
        ld_s2 = lds(s2_sb, s2d)
        ld_h1s = lds(h1s_sb, h1sd)
        ld_h2s = lds(h2s_sb, h2sd)
        ld_w3s = lds(w3s_sb, w3sd)
        ms_bias = V_(lambda e: e.memset(bias_sb, -EXPB))
        ms_sst = V_(lambda e: e.memset(sstage_sb, 0.0))
        ms_eps = V_(lambda e: e.memset(eps_sb, LNEPS))

        def reim(ap2, nh):
            return ap2.rearrange("p (h c f) -> p h c f", h=nh, c=2)

        neg0 = V_(lambda e: e.tensor_scalar_mul(
            reim(d0_sb[96:128, :], 16)[:, :, 0, :],
            reim(d0_sb[96:128, :], 16)[:, :, 0, :], -1.0), deps=[ld_d0])
        neg1 = V_(lambda e: e.tensor_scalar_mul(
            reim(d1_sb, 16)[:, :, 0, :],
            reim(d1_sb, 16)[:, :, 0, :], -1.0), deps=[ld_d1])

        def transpose_op(dst, src, idn):
            return lambda e, a=dst, b=src, c=idn: e.transpose(a, b, c)

        ldx = [None, None]
        tr_last = [None, None]
        ev_fm = [None, None]
        zx_ev = [None, None]
        mmz = None
        for i1 in range(33):
            if i1 < 32:
                kt = i1
                sl = kt % 2
                ldx[sl] = ldg(xtm_sb[:, sl, :],
                              x_full[kt * 128:(kt + 1) * 128, :],
                              deps=[tr_last[sl]])
                for ic in range(8):
                    o = T(transpose_op(
                        bankb(6 + sl)[:, ic * 128:(ic + 1) * 128],
                        xtm_sb[:, sl, ic * 128:(ic + 1) * 128], ident_sb),
                        deps=([ldx[sl], ld_ident, ev_fm[sl]] if ic == 0 else ()))
                tr_last[sl] = o
                ev_fm[sl] = V_(lambda e, sl=sl: e.tensor_copy(
                    xfm_sb[:, sl, :, :],
                    bankb(6 + sl).rearrange("p (a b) -> p a b", a=8)),
                    deps=[tr_last[sl]])
            if i1 >= 1:
                kt = i1 - 1
                sl = kt % 2
                zx = bank(sl).rearrange("p (a b) -> p a b", a=4)
                for ic in range(8):
                    st_, sp = (ic == 0), (ic == 7)
                    xf = xfm_sb[:, sl, ic, :]
                    dps = [ev_fm[sl], ld_wk, ld_wv, zx_ev[sl]] if ic == 0 else ()
                    T(lambda e, w=wk_sb[:, ic, 0:128], x=xf, s=st_, zx=zx:
                      e.matmul(zx[:, 0, :], w, x, start=s, stop=False), deps=dps)
                    T(lambda e, w=wk_sb[:, ic, 128:192], x=xf, zx=zx:
                      e.matmul(zx[0:64, 1, :], w, x, start=False, stop=False))
                    T(lambda e, w=wv_sb[:, ic, 128:192], x=xf, zx=zx:
                      e.matmul(zx[0:64, 3, :], w, x, start=False, stop=False))
                    mmz = T(lambda e, w=wv_sb[:, ic, 0:128], x=xf, p=sp, zx=zx:
                            e.matmul(zx[:, 2, :], w, x, start=False, stop=p))
                V_(lambda e, kt=kt, zx=zx: e.tensor_copy(
                    z0_sb[:, kt * 128:(kt + 1) * 128], zx[:, 0, :]), deps=[mmz])
                V_(lambda e, kt=kt, zx=zx: e.tensor_copy(
                    z1_sb[:, kt * 128:(kt + 1) * 128], zx[0:64, 1, :]))
                V_(lambda e, kt=kt, zx=zx: e.tensor_copy(
                    xv0_sb[:, kt * 128:(kt + 1) * 128], zx[:, 2, :]))
                zx_ev[sl] = V_(lambda e, kt=kt, zx=zx: e.tensor_copy(
                    xv1_sb[:, kt * 128:(kt + 1) * 128], zx[0:64, 3, :]))
        z_done = zx_ev[31 % 2]


        def rope_ops(ps_tm, nh, ctab, stab, out_tm, deps):
            src, dst = reim(ps_tm, nh), reim(out_tm, nh)
            cb = ctab.unsqueeze(1).broadcast_to([128, nh, 32])
            st = stab.unsqueeze(1).broadcast_to([128, nh, 32])
            r1 = rsc1_sb[:, 0:nh * 32].rearrange("p (h f) -> p h f", h=nh)
            r2 = rsc2_sb[:, 0:nh * 32].rearrange("p (h f) -> p h f", h=nh)
            V_(lambda e: e.tensor_tensor(r1, src[:, :, 0, :], cb, ALU.mult),
               deps=deps)
            V_(lambda e: e.tensor_tensor(r2, src[:, :, 1, :], st, ALU.mult))
            V_(lambda e: e.tensor_sub(dst[:, :, 0, :], r1, r2))
            V_(lambda e: e.tensor_tensor(r1, src[:, :, 0, :], st, ALU.mult))
            V_(lambda e: e.tensor_tensor(r2, src[:, :, 1, :], cb, ALU.mult))
            return V_(lambda e: e.tensor_add(dst[:, :, 1, :], r1, r2))

        ldq = [None, None]
        ev1 = [None, None]
        qd_ev = [None, None]
        rope_done = [None, None]
        qtr = [None, None]
        qmm = [None, None]
        for i2 in range(18):
            if i2 < 16:
                tt = i2
                sl = tt % 2
                ldq[sl] = ldg(xtm_sb[:, sl, :],
                              x_own[tt * 128:(tt + 1) * 128, :],
                              deps=[tr_last[sl]])
                for ic in range(8):
                    o = T(transpose_op(
                        bankb(6 + sl)[:, ic * 128:(ic + 1) * 128],
                        xtm_sb[:, sl, ic * 128:(ic + 1) * 128], ident_sb),
                        deps=([ldq[sl], ev_fm[sl] if tt < 2 else ev1[sl]]
                              if ic == 0 else ()))
                tr_last[sl] = o
                ev1[sl] = V_(lambda e, sl=sl: e.tensor_copy(
                    xfm_sb[:, sl, :, :],
                    bankb(6 + sl).rearrange("p (a b) -> p a b", a=8)),
                    deps=[tr_last[sl]])
            if 1 <= i2 <= 16:
                tt = i2 - 1
                sl = tt % 2
                qb = sl * 2
                for og in range(2):
                    for ic in range(8):
                        dps = ([ev1[sl], ld_wq, rope_done[sl], zx_ev[sl]]
                               if (og == 0 and ic == 0) else ())
                        o = T(lambda e, og=og, ic=ic, sl=sl, qb=qb:
                              e.matmul(bank(qb + og), xfm_sb[:, sl, ic, :],
                                       wq_sb[:, ic, og * 512:(og + 1) * 512],
                                       start=(ic == 0), stop=(ic == 7)), deps=dps)
                qmm[sl] = o
                rope_done[sl] = rope_ops(
                    bank2(qb), 16, cq_sb[:, tt, :], sq_sb[:, tt, :],
                    qtm_sb[:, sl, :],
                    deps=[qmm[sl], ld_cq, ld_sq, qtr[sl]])
            if i2 >= 2:
                tt = i2 - 2
                sl = tt % 2
                for p in range(8):
                    o = T(transpose_op(
                        bankb(4 + sl)[:, p * 128:(p + 1) * 128],
                        qtm_sb[:, sl, p * 128:(p + 1) * 128], ident_sb),
                        deps=([rope_done[sl], qd_ev[sl]] if p == 0 else ()))
                qtr[sl] = o
                qd_ev[sl] = V_(lambda e, sl=sl, tt=tt: e.tensor_copy(
                    qd_sb[:, :, tt * 128:(tt + 1) * 128],
                    bankb(4 + sl).rearrange("p (a b) -> p a b", a=8)),
                    deps=[qtr[sl]])
        q_done = qd_ev[15 % 2]
        q_done2 = qd_ev[14 % 2]

        fs = dict(it=0, exp=[], avs=[], y_ev=[None, None],
                  sstage_rd=None, bc_rd=[None, None], rs_rd=None, unit=0)
        y_final = None
        prev_flash_tail = [q_done, q_done2, rope_done[0], rope_done[1]]

        mask_rd = None
        for hg in range(NHG):
            dcol = slice(hg * 256, hg * 256 + 256)
            ropek = [None, None]
            trk = [None, None]
            evk = [None, None]
            vev = [None, None]
            kmm = [None, None]
            vmm = [None, None]

            def rope_pair(ps2, pi, out2, deps):
                src4 = ps2.rearrange("p t (h c f) -> p t h c f", h=4, c=2)
                dst4 = out2.rearrange("p t (h c f) -> p t h c f", h=4, c=2)
                cb = ck_sb[:, 2 * pi:2 * pi + 2, :].unsqueeze(2) \
                    .broadcast_to([128, 2, 4, 32])
                st = sk_sb[:, 2 * pi:2 * pi + 2, :].unsqueeze(2) \
                    .broadcast_to([128, 2, 4, 32])
                r1 = rsc1_sb[:, 0:256].rearrange("p (t h f) -> p t h f", t=2, h=4)
                r2 = rsc2_sb[:, 0:256].rearrange("p (t h f) -> p t h f", t=2, h=4)
                V_(lambda e: e.tensor_tensor(r1, src4[:, :, :, 0, :], cb,
                                             ALU.mult), deps=deps)
                V_(lambda e: e.tensor_tensor(r2, src4[:, :, :, 1, :], st,
                                             ALU.mult))
                V_(lambda e: e.tensor_sub(dst4[:, :, :, 0, :], r1, r2))
                V_(lambda e: e.tensor_tensor(r1, src4[:, :, :, 0, :], st,
                                             ALU.mult))
                V_(lambda e: e.tensor_tensor(r2, src4[:, :, :, 1, :], cb,
                                             ALU.mult))
                return V_(lambda e: e.tensor_add(dst4[:, :, :, 1, :], r1, r2))

            for pi in range(17):
                if pi < 16:
                    sl = pi % 2
                    b0 = sl * 2
                    for tl in range(2):
                        kt = 2 * pi + tl
                        dps = ([z_done, neg0, neg1, ld_ck, ld_sk, ropek[sl]] +
                               prev_flash_tail if pi == 0 else
                               ([ropek[sl]] if tl == 0 else []))
                        T(lambda e, kt=kt, b=b0 + tl, dc=dcol:
                          e.matmul(bank(b)[:, 0:256],
                                   z0_sb[:, kt * 128:(kt + 1) * 128],
                                   d0_sb[:, dc], start=True, stop=False),
                          deps=dps)
                        kmm[sl] = T(lambda e, kt=kt, b=b0 + tl, dc=dcol:
                                    e.matmul(bank(b)[:, 0:256],
                                             z1_sb[:, kt * 128:(kt + 1) * 128],
                                             d1_sb[:, dc], start=False,
                                             stop=True))
                        vdps = ([z_done, ld_vd0, ld_vd1, vev[kt % 2]] +
                                (prev_flash_tail if pi == 0 else []))
                        vmm[kt % 2] = T(lambda e, kt=kt, dc=dcol:
                                        e.matmul(bank(4 + kt % 2)[:, 0:256],
                                                 xv0_sb[:, kt * 128:(kt + 1) * 128],
                                                 vd0_sb[:, dc], start=True,
                                                 stop=False), deps=vdps)
                        vmm[kt % 2] = T(lambda e, kt=kt, dc=dcol:
                                        e.matmul(bank(4 + kt % 2)[:, 0:256],
                                                 xv1_sb[:, kt * 128:(kt + 1) * 128],
                                                 vd1_sb[:, dc], start=False,
                                                 stop=True))
                    ropek[sl] = rope_pair(
                        psum[:, b0:b0 + 2, 0:256], pi, ktm_sb[:, sl, :]
                        .rearrange("p (t c) -> p t c", t=2),
                        deps=[kmm[sl], trk[sl]])
                    for tl in range(2):
                        kt = 2 * pi + tl
                        vev[kt % 2] = SC(lambda e, kt=kt: e.copy(
                            v_sb[:, kt, :], bank(4 + kt % 2)[:, 0:256]),
                            deps=[vmm[kt % 2]])
                if pi >= 1:
                    pj = pi - 1
                    sl = pj % 2
                    for p in range(4):
                        o = T(transpose_op(
                            bankb(6 + sl)[:, p * 128:(p + 1) * 128],
                            ktm_sb[:, sl, p * 128:(p + 1) * 128], ident_sb),
                            deps=([ropek[sl], evk[sl]] +
                                  (prev_flash_tail if pj == 0 else [])
                                  if p == 0 else ()))
                    trk[sl] = o
                    evk[sl] = SC(lambda e, sl=sl, pj=pj: e.copy(
                        kd_sb[:, :, pj * 256:(pj + 1) * 256]
                        .rearrange("p r (t c) -> p r t c", t=2),
                        bankb(6 + sl)[:, 0:512]
                        .rearrange("p (t r c) -> p t r c", t=2, r=2)
                        .rearrange("p t r c -> p r t c")),
                        deps=[trk[sl]])
            kv_dep = [evk[0], evk[1], vev[0], vev[1]]

            mask_ld = None
            chunks = []
            for j in range(4):
                for pr in range(2):
                    for ci in range(CNT[j]):
                        chunks.append((j, pr, ci))
            stage = {}
            mask_lds = {}
            for idx in range(len(chunks) + 1):
                if idx < len(chunks):
                    j, pr, ci = chunks[idx]
                    cntj = CNT[j]
                    q0 = j * 512
                    pg = hg * 2 + pr
                    if pr == 0 and ci == 0:
                        mask_lds[j] = ldg(mask_sb,
                                          masksd[j].rearrange("s p q -> p s q"),
                                          deps=[mask_rd])
                    it = fs["it"]
                    scs, pts = it % 2, it % 4
                    sc_ap = bank2(scs * 2)
                    expdep = fs["exp"][it - 2] if it >= 2 else None
                    T(lambda e, sc=sc_ap, pr=pr, pg=pg, q0=q0, ci=ci:
                      e.matmul(sc[:, 0:512],
                               kd_sb[0:64, pr, ci * 128:(ci + 1) * 128],
                               qd_sb[0:64, pg, q0:q0 + 512],
                               start=True, stop=True),
                      deps=[expdep] + (kv_dep + [q_done] if idx == 0 else []))
                    mm_sc = T(lambda e, sc=sc_ap, pr=pr, pg=pg, q0=q0, ci=ci:
                              e.matmul(sc[:, 512:1024],
                                       kd_sb[64:128, pr, ci * 128:(ci + 1) * 128],
                                       qd_sb[64:128, pg, q0:q0 + 512],
                                       start=True, stop=True,
                                       tile_position=(64, 0)))
                    ptdep = fs["avs"][it - 4] if it >= 4 else None
                    exp_o = SC(lambda e, sc=sc_ap, pts=pts:
                               e.activation(pt_sb[:, pts, :], sc, AF.Exp,
                                            scale=0.125, bias=bias_sb),
                               deps=[mm_sc, ms_bias, ptdep])
                    fs["exp"].append(exp_o)
                    av_dep = exp_o
                    if ci >= cntj - NMSK:
                        s_ = ci - (cntj - NMSK)
                        mk = V_(lambda e, pts=pts, s_=s_: e.tensor_tensor(
                            pt_sb[:, pts, :].rearrange("p (a b) -> p a b", a=2),
                            pt_sb[:, pts, :].rearrange("p (a b) -> p a b", a=2),
                            mask_sb[:, s_, :].unsqueeze(1).broadcast_to(
                                [128, 2, 512]), ALU.mult),
                            deps=[exp_o, mask_lds[j]])
                        av_dep = mk
                        mask_rd = mk
                    stage[idx] = dict(av_dep=av_dep, pts=pts)
                    fs["it"] += 1
                if idx >= 1:
                    j, pr, ci = chunks[idx - 1]
                    cntj = CNT[j]
                    q0 = j * 512
                    pg = hg * 2 + pr
                    st_, sp = (ci == 0), (ci == cntj - 1)
                    up = fs["unit"] % 2
                    yb = 4 + 2 * up
                    sb = 5 + 2 * up
                    sg = stage.pop(idx - 1)
                    pts = sg["pts"]
                    ydeps = [sg["av_dep"]] + ([fs["y_ev"][up]] if st_ else [])
                    T(lambda e, pts=pts, pr=pr, yb=yb, s=st_, p=sp, ci=ci:
                      e.matmul(bank(yb)[0:64, :],
                               v_sb[:, ci, pr * 128:pr * 128 + 64],
                               pt_sb[:, pts, 0:512], start=s, stop=p),
                      deps=ydeps)
                    T(lambda e, pts=pts, pr=pr, yb=yb, s=st_, p=sp, ci=ci:
                      e.matmul(bank(yb)[64:128, :],
                               v_sb[:, ci, pr * 128 + 64:pr * 128 + 128],
                               pt_sb[:, pts, 512:1024], start=s, stop=p,
                               tile_position=(0, 64)))
                    T(lambda e, pts=pts, sb=sb, s=st_, p=sp:
                      e.matmul(bank(sb)[0:1, :], ones_sb, pt_sb[:, pts, 0:512],
                               start=s, stop=p),
                      deps=[ld_ones] + ([fs["bc_rd"][up]] if st_ else []))
                    mm_last = T(lambda e, pts=pts, sb=sb, s=st_, p=sp:
                                e.matmul(bank(sb)[32:33, :], ones_sb,
                                         pt_sb[:, pts, 512:1024],
                                         start=s, stop=p,
                                         tile_position=(0, 32)))
                    fs["avs"].append(mm_last)
                    if sp:
                        if fs["unit"] < 2:
                            V_(lambda e, sb=sb: e.tensor_copy(
                                sstage_sb[0:1, :], bank(sb)[0:1, :]),
                               deps=[mm_last, fs["sstage_rd"]])
                            cpB = V_(lambda e, sb=sb: e.tensor_copy(
                                sstage_sb[32:33, :], bank(sb)[32:33, :]))
                        else:
                            cpB = V_(lambda e, sb=sb: e.tensor_copy(
                                sstage_sb[0:33, :], bank(sb)[0:33, :]),
                                deps=[mm_last, fs["sstage_rd"]])
                        bc = T(lambda e, sb=sb: e.matmul(bank(sb), sel_sb,
                                                         sstage_sb,
                                                         start=True, stop=True),
                               deps=[cpB, ld_sel, ms_sst, fs["bc_rd"][up]])
                        fs["sstage_rd"] = bc
                        rc = V_(lambda e, sb=sb: e.reciprocal(rs_sb, bank(sb)),
                                deps=[bc, fs["rs_rd"]])
                        fs["bc_rd"][up] = rc
                        yev = V_(lambda e, pg=pg, q0=q0, yb=yb: e.tensor_tensor(
                            y_sb[:, pg, q0:q0 + 512], bank(yb), rs_sb, ALU.mult),
                            deps=[rc])
                        fs["rs_rd"] = yev
                        fs["y_ev"][up] = yev
                        y_final = yev
                        fs["unit"] += 1
            prev_flash_tail = [fs["exp"][-1], fs["avs"][-1], fs["bc_rd"][0],
                               fs["bc_rd"][1], fs["y_ev"][0], fs["y_ev"][1]]
        attn_done = y_final

        dbg_ops = []
        if dbg:
            dbg_ops.append(G(lambda e: e.dma_start(out=dbg_o["dbg_q"], in_=qd_sb),
               deps=[q_done]))
            dbg_ops.append(G(lambda e: e.dma_start(out=dbg_o["dbg_k"], in_=kd_sb),
               deps=[attn_done]))
            dbg_ops.append(G(lambda e: e.dma_start(out=dbg_o["dbg_v"], in_=v_sb),
               deps=[attn_done]))
            dbg_ops.append(G(lambda e: e.dma_start(out=dbg_o["dbg_y"], in_=y_sb),
               deps=[attn_done]))
            dbg_ops.append(G(lambda e: e.dma_start(out=dbg_o["dbg_z"], in_=z0_sb),
               deps=[z_done]))

        ld_wproj = ldg(wproj_sb, wprojd, deps=list(kv_dep) + dbg_ops)

        ln_prev = [None]

        def ln_fused(bankap, xres, xmid, xout, deps1, deps9):
            add_o = V_(lambda e: e.scalar_tensor_tensor(
                xmid, bankap, 1.0, xres, ALU.mult, ALU.add,
                accum_out=col_sb[:, 0:1]), deps=deps1)
            V_(lambda e: e.scalar_tensor_tensor(stat_sb, xmid, 1.0, xmid,
                                                ALU.mult, ALU.mult,
                                                accum_out=col_sb[:, 1:2]))
            V_(lambda e: e.tensor_scalar_mul(col_sb[:, 2:3], col_sb[:, 0:1],
                                             1.0 / D))
            V_(lambda e: e.scalar_tensor_tensor(col_sb[:, 4:5], col_sb[:, 2:3],
                                                -1.0, col_sb[:, 2:3], ALU.mult,
                                                ALU.mult))
            var = V_(lambda e: e.tensor_scalar(col_sb[:, 5:6], col_sb[:, 1:2],
                                               1.0 / D, col_sb[:, 4:5],
                                               ALU.mult, ALU.add))
            srt = SC(lambda e: e.activation(col_sb[:, 6:7], col_sb[:, 5:6],
                                            AF.Sqrt, bias=eps_sb),
                     deps=[var, ms_eps])
            rst = V_(lambda e: e.reciprocal(col_sb[:, 7:8], col_sb[:, 6:7]),
                     deps=[srt, ln_prev[0]])
            nb = V_(lambda e: e.scalar_tensor_tensor(
                col_sb[:, 3:4], col_sb[:, 2:3], -1.0, col_sb[:, 7:8],
                ALU.mult, ALU.mult), deps=[rst])
            ap = SC(lambda e: e.activation(xout, xmid, AF.Identity,
                                           scale=col_sb[:, 7:8],
                                           bias=col_sb[:, 3:4]),
                    deps=deps9 + [nb])
            ln_prev[0] = ap
            return add_o, ap

        x1ld = [None, None]
        x1cast = [None, None]
        x1tr = [None, None]
        x1ev = [None, None]
        ldw12 = [None, None]
        w12_rd = [None, None]
        ldh = [None, None, None]
        wbuf_rd = [None, None, None]
        ldw3s = [None, None]
        w3_rd = None
        w3_rd2 = [None, None]
        outw = [None, None]
        addx_bank = {}
        rf = [None, None, None]
        fev = [None] * 8
        x1w_all = []
        x1ld2_last = [None]

        def fm_tile(tt, q8, fsub_war, pace=None):
            sl = tt % 2
            x1ld[sl] = ldg(x1f_sb[:, sl, :],
                           x1sp[tt * 128:(tt + 1) * 128, :],
                           deps=[x1w_all[tt], rf[sl], pace])
            x1cast[sl] = V_(lambda e, sl=sl: e.tensor_copy(
                x1b_sb[:, sl, :], x1f_sb[:, sl, :]), deps=[x1ld[sl]])
            rf[sl] = x1cast[sl]
            for ic in range(8):
                o = T(transpose_op(
                    bankb(6 + sl)[:, ic * 128:(ic + 1) * 128],
                    x1b_sb[:, sl, ic * 128:(ic + 1) * 128], ident_sb),
                    deps=([x1cast[sl], x1ev[sl], w12_rd[0], w12_rd[1],
                           outw[0], outw[1], fsub_war] +
                          ([attn_done] if tt < 2 else [])
                          if ic == 0 else ()))
            x1tr[sl] = o
            x1ev[sl] = V_(lambda e, sl=sl, q8=q8: e.tensor_copy(
                x1fm_sb[:, :, q8 * 128:(q8 + 1) * 128],
                bankb(6 + sl).rearrange("p (a b) -> p a b", a=8)),
                deps=[x1tr[sl]])

        ldo = [None, None]
        addo = [None, None]
        x1w = [None, None]
        for tt in range(16):
            sl = tt % 2
            ldo[sl] = lds(xo_sb[:, sl, :], x_own[tt * 128:(tt + 1) * 128, :],
                          deps=[addo[sl]] + ([attn_done] + dbg_ops if tt < 2 else []))
            for og in range(2):
                for ch in range(8):
                    dps = ([attn_done, ld_wproj, addo[sl]]
                           if og == 0 and ch == 0 else ())
                    o = T(lambda e, ch=ch, og=og, sl=sl, tt=tt:
                          e.matmul(bank(sl * 2 + og),
                                   y_sb[:, ch, tt * 128:(tt + 1) * 128],
                                   wproj_sb[:, ch, og * 512:(og + 1) * 512],
                                   start=(ch == 0), stop=(ch == 7)), deps=dps)
            pmm = o
            addo[sl], lt = ln_fused(
                bank2(sl * 2), xo_sb[:, sl, :], x1p_sb[:, sl, :],
                x1_sb[:, sl, :],
                deps1=[pmm, ldo[sl], x1w[sl]], deps9=[x1w[sl]])
            x1w[sl] = SY(lambda e, sl=sl, tt=tt: e.dma_start(
                out=x1sp[tt * 128:(tt + 1) * 128, :], in_=x1_sb[:, sl, :]),
                deps=[lt])
            x1w_all.append(x1w[sl])
            if dbg:
                SY(lambda e, sl=sl, tt=tt: e.dma_start(
                    out=dbg_o["dbg_x1"][tt * 128:(tt + 1) * 128, :],
                    in_=x1_sb[:, sl, :]), deps=[lt])
            if tt >= 2 and tt < 10:
                fm_tile(tt - 2, tt - 2, None)
        x1fm_done_tg0 = [x1ev[0], x1ev[1]]

        af_war = None
        fsub_rd = None
        dbg_done = False
        for tg in range(2):
            t0 = tg * 1024
            sil = [None, None, None, None]
            aev = [None, None, None, None]
            if tg == 0:
                x1fm_done = x1fm_done_tg0
            else:
                for q8 in range(8):
                    fm_tile(8 + q8, q8, fsub_rd,
                            pace=(x1ld2_last[0] if q8 < 2 else None))
                x1fm_done = [x1ev[0], x1ev[1]]
            for oc in range(32):
                wsl = oc % 2
                ldw12[wsl] = ldg(w12b_sb[:, wsl, :, :, :], w12d[oc],
                                 deps=[w12_rd[wsl], w3_rd] +
                                 ([attn_done, x1w_all[7]]
                                  if tg == 0 and oc < 2 else []) +
                                 ([x1ld2_last[0]]
                                  if tg == 1 and oc < 2 else []))
                for half in range(2):
                    pb = half * 4 + wsl * 2
                    hs = slice(half * 512, (half + 1) * 512)
                    ss = half * 2 + wsl
                    for w_ in range(2):
                        for ic in range(8):
                            dps = ([ldw12[wsl]] + x1fm_done +
                                   [sil[ss], aev[ss]] +
                                   ([addo[0], addo[1]]
                                    if tg == 0 and oc < 2 else []))                                 if (w_ == 0 and ic == 0) else ()
                            o = T(lambda e, w_=w_, ic=ic, pb=pb, wsl=wsl, hs=hs:
                                  e.matmul(bank(pb + w_),
                                           w12b_sb[:, wsl, w_, ic, :],
                                           x1fm_sb[:, ic, hs],
                                           start=(ic == 0), stop=(ic == 7)),
                                  deps=dps)
                    w12_rd[wsl] = o
                    sil[ss] = SC(lambda e, pb=pb, oc=oc, ss=ss:
                                 e.activation(silu_sb[:, ss, :], bank(pb),
                                              AF.Silu,
                                              scale=s1_sb[:, oc:oc + 1]),
                                 deps=[w12_rd[wsl], ld_s1, aev[ss]])
                    aev[ss] = V_(lambda e, pb=pb, oc=oc, ss=ss, hs=hs:
                                 e.scalar_tensor_tensor(
                                     af_sb[:, oc, hs], bank(pb + 1),
                                     s2_sb[:, oc:oc + 1], silu_sb[:, ss, :],
                                     ALU.mult, ALU.mult),
                                 deps=[sil[ss], ld_s2, af_war] +
                                 ([x1w_all[15]]
                                  if tg == 0 and oc == 16 else []))
            a_done = [aev[0], aev[1], aev[2], aev[3]]
            if dbg and tg == 0:
                G(lambda e: e.dma_start(out=dbg_o["dbg_af"], in_=af_sb),
                  deps=a_done)

            def h_layer(src_sb, dst_sb, wsrc, scale_sb, scale_ld, dep_start,
                        ld_gate=()):
                gact = [None, None, None, None]
                for oc in range(32):
                    wsl = oc % 3
                    ldh[wsl] = ldg(wbuf_sb[:, wsl, :, :], wsrc[oc],
                                   deps=[wbuf_rd[wsl]] + list(ld_gate) +
                                   ([outw[0], outw[1]] if wsl == 2 else []))
                    for half in range(2):
                        pb = (oc * 2 + half) % 4
                        hs = slice(half * 512, (half + 1) * 512)
                        for rc in range(32):
                            dps = ([ldh[wsl], gact[pb]] + dep_start)                                 if rc == 0 else ()
                            o = T(lambda e, rc=rc, pb=pb, wsl=wsl, hs=hs:
                                  e.matmul(bank(pb), wbuf_sb[:, wsl, rc, :],
                                           src_sb[:, rc, hs],
                                           start=(rc == 0), stop=(rc == 31)),
                                  deps=dps)
                        wbuf_rd[wsl] = o
                        gact[pb] = SC(lambda e, oc=oc, pb=pb, hs=hs:
                                      e.activation(dst_sb[:, oc, hs], bank(pb),
                                                   AF.Gelu,
                                                   scale=scale_sb[:, oc:oc + 1]),
                                      deps=[wbuf_rd[wsl], scale_ld])
                return [gact[0], gact[1], gact[2], gact[3]]

            g_done = h_layer(af_sb, g_sb, h1wd, h1s_sb, ld_h1s,
                             a_done[2:],
                             ld_gate=([x1w[0], x1w[1]] if tg == 0 else ()))
            g2_done = h_layer(g_sb, af_sb, h2wd, h2s_sb, ld_h2s, g_done)

            if dbg and not dbg_done:
                G(lambda e: e.dma_start(out=dbg_o["dbg_a"], in_=g_sb),
                  deps=g_done)
                G(lambda e: e.dma_start(out=dbg_o["dbg_g"], in_=af_sb),
                  deps=g2_done)
                dbg_done = True

            for half in range(2):
                th0 = t0 + half * 512
                hs = slice(half * 512, (half + 1) * 512)
                for oc in range(8):
                    pb = oc % 2
                    wb3 = w3b_sb if oc % 2 == 0 else w3b2_sb
                    ldw3s[oc % 2] = ldg(wb3, w3wd[oc],
                                        deps=[w3_rd2[oc % 2], a_done[3],
                                              w12_rd[0], w12_rd[1]] +
                                        ([attn_done, x1w[0], x1w[1]]
                                         if tg == 0 and half == 0 and oc == 0
                                         else []))
                    for rc in range(32):
                        dps = ([ldw3s[oc % 2],
                                fev[oc - 2] if oc >= 2 else None,
                                fev[oc - 2 + 8] if oc < 2 else None] + g2_done) \
                            if rc == 0 else ()
                        o = T(lambda e, rc=rc, pb=pb, hs=hs, wb3=wb3:
                              e.matmul(bank(pb), wb3[:, rc, :],
                                       af_sb[:, rc, hs],
                                       start=(rc == 0), stop=(rc == 31)),
                              deps=dps)
                    w3_rd2[oc % 2] = o
                    w3_rd = o
                    fev[oc] = V_(lambda e, oc=oc, pb=pb: e.tensor_scalar_mul(
                        fsub_sb[:, oc, :], bank(pb), w3s_sb[:, oc:oc + 1]),
                        deps=[w3_rd, ld_w3s, fsub_rd])
                for q in range(4):
                    tt = th0 // 128 + q
                    sl = tt % 2
                    s3 = tt % 3
                    x1ld2 = lds(x1f_sb[:, s3, :],
                                x1sp[tt * 128:(tt + 1) * 128, :],
                                deps=[rf[s3], fev[4 + q]])
                    x1ld2_last[0] = x1ld2
                    xb = 4 if q % 2 == 0 else 6
                    for oc in range(8):
                        o = T(transpose_op(
                            bank2(xb)[:, oc * 128:(oc + 1) * 128],
                            fsub_sb[:, oc, q * 128:(q + 1) * 128], identf_sb),
                            deps=([fev[7], ld_identf, addx_bank.get(xb)]
                                  if oc == 0 else ()))
                    ftr = o
                    if q == 3:
                        fsub_rd = ftr
                    addx, lt2 = ln_fused(
                        bank2(xb), x1f_sb[:, s3, :], x1f_sb[:, s3, :],
                        x2_sb[:, sl, :],
                        deps1=[ftr, x1ld2], deps9=[outw[sl]])
                    addx_bank[xb] = addx
                    rf[s3] = lt2
                    outw[sl] = SY(lambda e, sl=sl, tt=tt: e.dma_start(
                        out=out[tt * 128:(tt + 1) * 128, :], in_=x2_sb[:, sl, :]),
                        deps=[lt2])
            af_war = w3_rd

        P.add("sync", lambda e: e.nop(), deps=[outw[0], outw[1]])

        block = es.enter_context(nc.Block())
        P.emit(nc, block, es)
    return nc


def _host_prep(inputs):
    f32 = np.float32
    Wq = np.asarray(inputs["L0_attn_Wq"], f32)
    Wk = np.asarray(inputs["L0_attn_Wk"], f32)
    Wv = np.asarray(inputs["L0_attn_Wv"], f32)
    kdec = np.asarray(inputs["L0_attn_key_decoder"], f32)
    vdec = np.asarray(inputs["L0_attn_value_decoder"], f32)
    Wproj = np.asarray(inputs["L0_attn_Wproj"], f32)
    w1w = np.asarray(inputs["L0_ffn_w1_w"], f32)
    w2w = np.asarray(inputs["L0_ffn_w2_w"], f32)
    h1w = np.asarray(inputs["L0_ffn_h1_w"], f32)
    h2w = np.asarray(inputs["L0_ffn_h2_w"], f32)
    w3w = np.asarray(inputs["L0_ffn_w3_w"], f32)

    com = {}
    hh, cc, ff = np.meshgrid(np.arange(H), np.arange(2), np.arange(NF),
                             indexing="ij")
    qperm = (64 * hh + 2 * ff + cc).reshape(-1)
    com["wq"] = np.ascontiguousarray(Wq[:, qperm].reshape(8, 128, D)
                                     .transpose(1, 0, 2))
    ffk, rrk = np.meshgrid(np.arange(NF), np.arange(KR), indexing="ij")
    kperm = np.concatenate([(6 * ffk + 2 * rrk + 0).reshape(-1),
                            (6 * ffk + 2 * rrk + 1).reshape(-1)])
    com["wk"] = np.ascontiguousarray(Wk[:, kperm].reshape(8, 128, VR)
                                     .transpose(1, 0, 2))
    com["wv"] = np.ascontiguousarray(Wv.reshape(8, 128, VR).transpose(1, 0, 2))
    Dm = np.zeros((VR, D), f32)
    br, bi = kdec[..., 0], kdec[..., 1]
    fi, hi, ri = np.meshgrid(np.arange(NF), np.arange(H), np.arange(KR),
                             indexing="ij")
    for c, top, bot in ((0, br, bi), (1, bi, br)):
        Dm[3 * fi + ri, 64 * hi + 32 * c + fi] = top
        Dm[96 + 3 * fi + ri, 64 * hi + 32 * c + fi] = bot
    com["d0"] = np.ascontiguousarray(Dm[:128])
    com["d1"] = np.ascontiguousarray(Dm[128:])
    VD = np.ascontiguousarray(vdec.transpose(1, 0, 2).reshape(VR, D))
    com["vd0"] = np.ascontiguousarray(VD[:128])
    com["vd1"] = np.ascontiguousarray(VD[128:])
    com["wproj"] = np.ascontiguousarray(Wproj.reshape(8, 128, D)
                                        .transpose(1, 0, 2))
    w1s_ = w1w.reshape(32, 128, 8, 128).transpose(0, 3, 2, 1)
    w2s_ = w2w.reshape(32, 128, 8, 128).transpose(0, 3, 2, 1)
    com["w12"] = np.ascontiguousarray(np.stack([w1s_, w2s_], 2))
    com["h1w"] = np.ascontiguousarray(h1w.reshape(32, 128, 32, 128)
                                      .transpose(0, 3, 2, 1))
    com["h2w"] = np.ascontiguousarray(h2w.reshape(32, 128, 32, 128)
                                      .transpose(0, 3, 2, 1))
    com["w3w"] = np.ascontiguousarray(w3w.reshape(8, 128, 32, 128)
                                      .transpose(0, 3, 2, 1))
    com["s1s"] = np.ascontiguousarray(
        np.asarray(inputs["L0_ffn_w1_s"], f32).reshape(32, 128).T)
    com["s2s"] = np.ascontiguousarray(
        np.asarray(inputs["L0_ffn_w2_s"], f32).reshape(32, 128).T)
    com["h1s"] = np.ascontiguousarray(
        np.asarray(inputs["L0_ffn_h1_s"], f32).reshape(32, 128).T)
    com["h2s"] = np.ascontiguousarray(
        np.asarray(inputs["L0_ffn_h2_s"], f32).reshape(32, 128).T)
    com["w3s"] = np.ascontiguousarray(
        np.asarray(inputs["L0_ffn_w3_s"], f32).reshape(8, 128).T)
    com["ident"] = np.eye(128, dtype=f32)
    sel = np.zeros((33, 128), f32)
    sel[0, 0:64] = 1.0
    sel[32, 64:128] = 1.0
    com["sel33"] = sel
    com["onesb"] = np.ones((128, 1), f32)
    half = HD // 2
    freq = (1.0 / ROPE_BASE ** (np.arange(half, dtype=f32) / half)).astype(f32)
    freqs = np.outer(np.arange(S, dtype=f32), freq)
    COS = np.cos(freqs).astype(f32)
    SIN = np.sin(freqs).astype(f32)
    com["ck"] = np.ascontiguousarray(COS.reshape(32, 128, 32).transpose(1, 0, 2))
    com["sk"] = np.ascontiguousarray(SIN.reshape(32, 128, 32).transpose(1, 0, 2))
    return com, COS, SIN


def _core_inputs(inputs, com, COS, SIN, core):
    b, hlf = core // 2, core % 2
    x = np.asarray(inputs["L0_attn_x"], np.float32)
    groups = [2 * j + 1 - hlf for j in range(4)]
    rows = np.concatenate([np.arange(g * 512, (g + 1) * 512) for g in groups])
    d = dict(com)
    d["x_own"] = np.ascontiguousarray(x[b][rows])
    d["x_full"] = np.ascontiguousarray(x[b])
    d["cq"] = np.ascontiguousarray(COS[rows].reshape(16, 128, 32)
                                   .transpose(1, 0, 2))
    d["sq"] = np.ascontiguousarray(SIN[rows].reshape(16, 128, 32)
                                   .transpose(1, 0, 2))
    m = np.zeros((4, NMSK, 128, 512), np.float32)
    for j in range(4):
        q0 = groups[j] * 512
        for s_ in range(NMSK):
            k0 = (CNT[j] - NMSK + s_) * 128
            kk = k0 + np.arange(128)[:, None]
            qq = q0 + np.arange(512)[None, :]
            m[j, s_] = (kk <= qq).astype(np.float32)
    d["masks"] = m
    return d, groups


def kernel(**inputs):
    com, COS, SIN = _host_prep(inputs)
    in_maps, group_list = [], []
    for c in range(8):
        d, groups = _core_inputs(inputs, com, COS, SIN, c)
        in_maps.append(d)
        group_list.append(groups)
    nc = build_nc()
    res = run_bass_kernel_spmd(nc, in_maps, core_ids=list(range(8)))
    outa = np.empty((BSZ, S, D), np.float32)
    for c in range(8):
        b = c // 2
        shard = np.asarray(res.results[c]["out"], np.float32).reshape(TQ, D)
        for j, g in enumerate(group_list[c]):
            outa[b, g * 512:(g + 1) * 512] = shard[j * 512:(j + 1) * 512]
    return outa


# version 21 (final)
# speedup vs baseline: 1.0696x; 1.0147x over previous
import contextlib
import numpy as np
import concourse.bass as bass
import concourse.mybir as mybir
from concourse.bass_utils import run_bass_kernel_spmd

F32 = mybir.dt.float32
BF16 = mybir.dt.bfloat16
AF = mybir.ActivationFunctionType
ALU = mybir.AluOpType

S, D, BSZ, H, HD, NF, KR, VR, DH = 4096, 1024, 4, 16, 64, 32, 3, 192, 4096
TQ = 2048
CNT = (8, 16, 24, 32)
NMSK = 8
EXPB = 12.0
LNEPS = 1e-5
ROPE_BASE = 10000.0
NHG = 4
NTG = 4

ENGS = ("tensor", "vector", "scalar", "gpsimd", "sync")


class Prog:

    def __init__(self):
        self.ops = []
        self.per_eng = {e: [] for e in ENGS}

    def add(self, eng, fn, deps=(), dma=False):
        i = len(self.ops)
        self.ops.append((eng, fn, tuple(d for d in deps if d is not None), dma))
        self.per_eng[eng].append(i)
        return i

    def emit(self, nc, block, es):
        need = set()
        for i, (eng, fn, deps, dma) in enumerate(self.ops):
            if dma:
                need.add(i)
            if eng in ("vector", "scalar"):
                need.add(i)
            for d in deps:
                pe, _, _, pdma = self.ops[d]
                if pdma or pe != eng:
                    need.add(d)
        sems = {e: es.enter_context(nc.semaphore("S" + e)) for e in ENGS}
        for e in ENGS:
            for s in range(3):
                sems[(e, s)] = es.enter_context(nc.semaphore(f"S{e}{s}"))
        val, cnt = {}, {e: 0 for e in ENGS}
        for e in ENGS:
            for i in self.per_eng[e]:
                if i in need and not self.ops[i][3]:
                    cnt[e] += 1
                    val[i] = cnt[e]
        waits = []
        for i, (eng, fn, deps, dma) in enumerate(self.ops):
            w = {}
            for d in deps:
                if d in val and not self.ops[d][3]:
                    pe = self.ops[d][0]
                    if pe != eng:
                        w[pe] = max(w.get(pe, 0), val[d])
            if eng == "vector" and val.get(i, 1) > 1:
                w[eng] = max(w.get(eng, 0), val[i] - 1)
            waits.append(w)
        B, R = 2, 3
        self.stripe_of = {}
        self.stripe_val = {}
        stripe_gate = {}
        for e in ENGS:
            dmas = [i for i in self.per_eng[e] if self.ops[i][3]]
            epochs, cur = [], []
            for i in dmas:
                boundary = len(cur) >= B
                if not boundary and cur:
                    start = cur[0]
                    if any(d >= start for d in self.ops[i][2]):
                        boundary = True
                if boundary and cur:
                    epochs.append(cur)
                    cur = []
                cur.append(i)
            if cur:
                epochs.append(cur)
            stot = [0] * R
            for k, ep in enumerate(epochs):
                s = k % R
                if stot[s] > 0:
                    stripe_gate[ep[0]] = (s, stot[s])
                stot[s] += 16 * len(ep)
                for i in ep:
                    self.stripe_of[i] = (e, s)
                    self.stripe_val[i] = stot[s]
        self._stripe_gate = stripe_gate
        waits2 = []
        for i, (eng, fn, deps, dma) in enumerate(self.ops):
            w = {}
            for pe, v in waits[i].items():
                w[pe] = max(w.get(pe, 0), v)
            for d in deps:
                if d in self.stripe_of:
                    pe, s = self.stripe_of[d]
                    key = (pe, s)
                    w[key] = max(w.get(key, 0), self.stripe_val[d])
            if i in self._stripe_gate:
                s, v = self._stripe_gate[i]
                key = (eng, s)
                w[key] = max(w.get(key, 0), v)
            waits2.append(w)
        waits = waits2
        seen = {e: {} for e in ENGS}
        for e in ENGS:
            for i in self.per_eng[e]:
                w = waits[i]
                for pe in list(w):
                    if seen[e].get(pe, 0) >= w[pe]:
                        del w[pe]
                    else:
                        seen[e][pe] = w[pe]
        for e in ENGS:
            def body(eng, _e=e):
                for i in self.per_eng[_e]:
                    _, fn, _, dma = self.ops[i]
                    for pe, v in sorted(waits[i].items(), key=str):
                        eng.wait_ge(sems[pe], v)
                    ins = fn(eng)
                    if dma:
                        ins.then_inc(sems[self.stripe_of[i]], 16)
                    elif i in val:
                        ins.then_inc(sems[_e], 1)
            getattr(block, e)(body)


class Carver:
    def __init__(self, pool_ap, limit, off=0):
        self.pool = pool_ap
        self.off = off
        self.limit = limit

    def take(self, shape, dt):
        p = shape[0]
        elems = int(np.prod(shape[1:]))
        width = elems * (2 if dt == F32 else 1)
        self.off = (self.off + 1) & ~1
        a = self.pool[0:p, self.off:self.off + width]
        self.off += width
        assert self.off <= self.limit, f"sbuf pool overflow {self.off} > {self.limit}"
        if dt == F32:
            a = a.bitcast(F32)
        if len(shape) == 3:
            a = a.rearrange("p (a b) -> p a b", a=shape[1])
        elif len(shape) == 4:
            a = a.rearrange("p (a b c) -> p a b c", a=shape[1], b=shape[2])
        elif len(shape) == 5:
            a = a.rearrange("p (a b c d) -> p a b c d", a=shape[1], b=shape[2],
                            c=shape[3])
        else:
            assert len(shape) == 2
        return a


def build_nc(dbg=False):
    nc = bass.Bass("TRN2", target_bir_lowering=False,
                   detect_race_conditions=False)

    def din(name, shape):
        return nc.dram_tensor(name, shape, F32, kind="ExternalInput").ap()

    x_own = din("x_own", [TQ, D])
    x_full = din("x_full", [S, D])
    cqd, sqd = din("cq", [128, 16, 32]), din("sq", [128, 16, 32])
    ckd, skd = din("ck", [128, 32, 32]), din("sk", [128, 32, 32])
    masksd = din("masks", [4, NMSK, 128, 512])
    wqd = din("wq", [128, 8, D])
    wkd = din("wk", [128, 8, VR])
    wvd = din("wv", [128, 8, VR])
    d0d, d1d = din("d0", [128, D]), din("d1", [64, D])
    vd0d, vd1d = din("vd0", [128, D]), din("vd1", [64, D])
    wprojd = din("wproj", [128, 8, D])
    w12d = din("w12", [32, 128, 2, 8, 128])
    h1wd = din("h1w", [32, 128, 32, 128])
    h2wd = din("h2w", [32, 128, 32, 128])
    w3wd = din("w3w", [8, 128, 32, 128])
    s1d, s2d = din("s1s", [128, 32]), din("s2s", [128, 32])
    h1sd, h2sd = din("h1s", [128, 32]), din("h2s", [128, 32])
    w3sd = din("w3s", [128, 8])
    identd = din("ident", [128, 128])
    sel33d = din("sel33", [33, 128])
    onesd = din("onesb", [128, 1])

    out = nc.dram_tensor("out", [TQ, D], F32, kind="ExternalOutput").ap()
    x1sp = nc.dram_tensor("x1sp", [TQ, D], F32, kind="ExternalOutput").ap()
    dbg_o = {}
    if dbg:
        for nm, shp in (("dbg_q", [128, 8, TQ]), ("dbg_k", [128, 2, S]),
                        ("dbg_v", [128, 32, 256]), ("dbg_y", [128, 8, TQ]),
                        ("dbg_x1", [TQ, D]), ("dbg_z", [128, S]),
                        ("dbg_a", [128, 32, 1024]), ("dbg_g", [128, 32, 1024]),
                        ("dbg_af", [128, 32, 1024])):
            dbg_o[nm] = nc.dram_tensor(nm, shp, F32, kind="ExternalOutput").ap()

    P = Prog()

    def G(fn, deps=()):
        return P.add("gpsimd", fn, deps, dma=True)

    def SY(fn, deps=()):
        return P.add("sync", fn, deps, dma=True)

    def T(fn, deps=()):
        return P.add("tensor", fn, deps)

    def V_(fn, deps=()):
        return P.add("vector", fn, deps)

    def SC(fn, deps=()):
        return P.add("scalar", fn, deps)

    def ldg(dst, src, deps=()):
        return G(lambda e, a=dst, b=src: e.dma_start(out=a, in_=b), deps)

    def lds(dst, src, deps=()):
        return SY(lambda e, a=dst, b=src: e.dma_start(out=a, in_=b), deps)

    es = contextlib.ExitStack()
    with es:
        POOLW = 106416
        pool = es.enter_context(nc.sbuf_tensor("pool", [128, POOLW], BF16)).ap()
        psum = es.enter_context(nc.psum_tensor("psum", [128, 8, 512], F32)).ap()

        def bank(b):
            return psum[:, b, :]

        def bank2(b):
            return psum[:, b:b + 2, :].rearrange("p a b -> p (a b)")

        def bankb(b):
            return psum[:, b, :].bitcast(BF16)

        C = Carver(pool, POOLW)
        ident_sb = C.take([128, 128], BF16)
        identf_sb = C.take([128, 128], F32)
        eps_sb = C.take([128, 1], F32)
        s1_sb = C.take([128, 32], F32)
        s2_sb = C.take([128, 32], F32)
        h1s_sb = C.take([128, 32], F32)
        h2s_sb = C.take([128, 32], F32)
        w3s_sb = C.take([128, 8], F32)
        col_sb = C.take([128, 8], F32)
        a_live_end = C.off
        ck_sb = C.take([128, 32, 32], F32)
        sk_sb = C.take([128, 32, 32], F32)
        cq_sb = C.take([128, 16, 32], F32)
        sq_sb = C.take([128, 16, 32], F32)
        mask_sb = C.take([128, NMSK, 512], BF16)
        sel_sb = C.take([33, 128], F32)
        ones_sb = C.take([128, 1], BF16)
        bias_sb = C.take([128, 1], F32)
        pt_sb = C.take([128, 4, 1024], BF16)
        sstage_sb = C.take([33, 512], F32)
        lnr_sb = C.take([128, 512], F32)
        rs_sb = C.take([128, 512], F32)
        wk_sb = C.take([128, 8, VR], BF16)
        wv_sb = C.take([128, 8, VR], BF16)
        d0_sb = C.take([128, D], BF16)
        d1_sb = C.take([64, D], BF16)
        vd0_sb = C.take([128, D], BF16)
        vd1_sb = C.take([64, D], BF16)
        a_end = C.off
        qd_sb = C.take([128, 8, TQ], BF16)
        y_sb = C.take([128, 8, TQ], BF16)
        bc_end = C.off
        z0_sb = C.take([128, S], BF16)
        z1_sb = C.take([64, S], BF16)
        xv0_sb = C.take([128, S], BF16)
        xv1_sb = C.take([64, S], BF16)
        kd_off = C.off
        kd_sb = C.take([128, 2, S], BF16)
        v_sb = C.take([128, 32, 256], BF16)
        d_stream = C.off
        wq_sb = Carver(pool, d_stream, off=kd_off).take([128, 8, D], BF16)
        xtm_sb = C.take([128, 2, D], BF16)
        xfm_sb = C.take([128, 2, 8, 128], BF16)
        qtm_sb = C.take([128, 2, D], BF16)
        ktm_sb = C.take([128, 2, 512], BF16)
        rsc1_sb = C.take([128, 512], F32)
        rsc2_sb = C.take([128, 512], F32)
        attn_peak = C.off
        wproj_sb = Carver(pool, bc_end + 8192, off=bc_end).take([128, 8, D],
                                                               BF16)
        CP = Carver(pool, POOLW, off=kd_off)
        xo_sb = CP.take([128, 2, D], F32)
        x1p_sb = CP.take([128, 2, D], F32)
        x1_sb = CP.take([128, 2, D], F32)
        CF1 = Carver(pool, a_end, off=a_live_end)
        stat_sb = CF1.take([128, D], F32)
        wbuf_off = CF1.off
        wbuf_sb = CF1.take([128, 3, 32, 128], BF16)
        w12b_off = CF1.off
        w12b_sb = CF1.take([128, 2, 2, 8, 128], BF16)
        sw3_off = CF1.off
        silu_sb = CF1.take([128, 4, 512], F32)
        x1b_sb = Carver(pool, a_end, off=wbuf_off).take([128, 2, D], BF16)
        CF2 = Carver(pool, bc_end, off=a_end)
        g_sb = CF2.take([128, 32, 1024], BF16)
        w3b_sb = Carver(pool, a_end, off=sw3_off).take([128, 32, 128], BF16)
        CF3 = Carver(pool, POOLW, off=bc_end)
        af_sb = CF3.take([128, 32, 1024], BF16)
        x1fm_off = CF3.off
        x1fm_sb = CF3.take([128, 8, 1024], BF16)
        x1f_sb = CF3.take([128, 3, D], F32)
        ffn_peak = CF3.off
        fsub_sb = Carver(pool, POOLW, off=x1fm_off).take([128, 8, 512], F32)
        x2_sb = Carver(pool, a_end,
                       off=wbuf_off + 2 * 4096).take([128, 2, D], F32)
        x2c_sb = Carver(pool, a_end, off=wbuf_off + 2 * D).take([128, D], F32)
        w3b2_sb = Carver(pool, a_end, off=w12b_off).take([128, 32, 128], BF16)

        ld_wk = ldg(wk_sb, wkd)
        ld_wv = ldg(wv_sb, wvd)
        ld_d0 = ldg(d0_sb, d0d)
        ld_d1 = ldg(d1_sb, d1d)
        ld_vd0 = ldg(vd0_sb, vd0d)
        ld_vd1 = ldg(vd1_sb, vd1d)
        ld_ident = ldg(ident_sb, identd)
        ld_ones = ldg(ones_sb, onesd)
        ld_wq = ldg(wq_sb, wqd)
        ld_ck = lds(ck_sb, ckd)
        ld_sk = lds(sk_sb, skd)
        ld_cq = lds(cq_sb, cqd)
        ld_sq = lds(sq_sb, sqd)
        ld_identf = lds(identf_sb, identd)
        ld_sel = lds(sel_sb, sel33d)
        ld_s1 = lds(s1_sb, s1d)
        ld_s2 = lds(s2_sb, s2d)
        ld_h1s = lds(h1s_sb, h1sd)
        ld_h2s = lds(h2s_sb, h2sd)
        ld_w3s = lds(w3s_sb, w3sd)
        ms_bias = V_(lambda e: e.memset(bias_sb, -EXPB))
        ms_sst = V_(lambda e: e.memset(sstage_sb, 0.0))
        ms_eps = V_(lambda e: e.memset(eps_sb, LNEPS))

        def reim(ap2, nh):
            return ap2.rearrange("p (h c f) -> p h c f", h=nh, c=2)

        neg0 = V_(lambda e: e.tensor_scalar_mul(
            reim(d0_sb[96:128, :], 16)[:, :, 0, :],
            reim(d0_sb[96:128, :], 16)[:, :, 0, :], -1.0), deps=[ld_d0])
        neg1 = V_(lambda e: e.tensor_scalar_mul(
            reim(d1_sb, 16)[:, :, 0, :],
            reim(d1_sb, 16)[:, :, 0, :], -1.0), deps=[ld_d1])

        def transpose_op(dst, src, idn):
            return lambda e, a=dst, b=src, c=idn: e.transpose(a, b, c)

        ldx = [None, None]
        tr_last = [None, None]
        ev_fm = [None, None]
        zx_ev = [None, None]
        mmz = None
        for i1 in range(33):
            if i1 < 32:
                kt = i1
                sl = kt % 2
                ldx[sl] = ldg(xtm_sb[:, sl, :],
                              x_full[kt * 128:(kt + 1) * 128, :],
                              deps=[tr_last[sl]])
                for ic in range(8):
                    o = T(transpose_op(
                        bankb(6 + sl)[:, ic * 128:(ic + 1) * 128],
                        xtm_sb[:, sl, ic * 128:(ic + 1) * 128], ident_sb),
                        deps=([ldx[sl], ld_ident, ev_fm[sl]] if ic == 0 else ()))
                tr_last[sl] = o
                ev_fm[sl] = SC(lambda e, sl=sl: e.copy(
                    xfm_sb[:, sl, :, :],
                    bankb(6 + sl).rearrange("p (a b) -> p a b", a=8)),
                    deps=[tr_last[sl]])
            if i1 >= 1:
                kt = i1 - 1
                sl = kt % 2
                zx = bank(sl).rearrange("p (a b) -> p a b", a=4)
                for ic in range(8):
                    st_, sp = (ic == 0), (ic == 7)
                    xf = xfm_sb[:, sl, ic, :]
                    dps = [ev_fm[sl], ld_wk, ld_wv, zx_ev[sl]] if ic == 0 else ()
                    T(lambda e, w=wk_sb[:, ic, 0:128], x=xf, s=st_, zx=zx:
                      e.matmul(zx[:, 0, :], w, x, start=s, stop=False), deps=dps)
                    T(lambda e, w=wk_sb[:, ic, 128:192], x=xf, zx=zx:
                      e.matmul(zx[0:64, 1, :], w, x, start=False, stop=False))
                    T(lambda e, w=wv_sb[:, ic, 128:192], x=xf, zx=zx:
                      e.matmul(zx[0:64, 3, :], w, x, start=False, stop=False))
                    mmz = T(lambda e, w=wv_sb[:, ic, 0:128], x=xf, p=sp, zx=zx:
                            e.matmul(zx[:, 2, :], w, x, start=False, stop=p))
                SC(lambda e, kt=kt, zx=zx: e.copy(
                    z0_sb[:, kt * 128:(kt + 1) * 128], zx[:, 0, :]), deps=[mmz])
                SC(lambda e, kt=kt, zx=zx: e.copy(
                    z1_sb[:, kt * 128:(kt + 1) * 128], zx[0:64, 1, :]))
                SC(lambda e, kt=kt, zx=zx: e.copy(
                    xv0_sb[:, kt * 128:(kt + 1) * 128], zx[:, 2, :]))
                zx_ev[sl] = SC(lambda e, kt=kt, zx=zx: e.copy(
                    xv1_sb[:, kt * 128:(kt + 1) * 128], zx[0:64, 3, :]))
        z_done = zx_ev[31 % 2]


        def rope_ops(ps_tm, nh, ctab, stab, out_tm, deps):
            src, dst = reim(ps_tm, nh), reim(out_tm, nh)
            cb = ctab.unsqueeze(1).broadcast_to([128, nh, 32])
            st = stab.unsqueeze(1).broadcast_to([128, nh, 32])
            r1 = rsc1_sb[:, 0:nh * 32].rearrange("p (h f) -> p h f", h=nh)
            r2 = rsc2_sb[:, 0:nh * 32].rearrange("p (h f) -> p h f", h=nh)
            V_(lambda e: e.tensor_tensor(r1, src[:, :, 0, :], cb, ALU.mult),
               deps=deps)
            V_(lambda e: e.tensor_tensor(r2, src[:, :, 1, :], st, ALU.mult))
            V_(lambda e: e.tensor_sub(dst[:, :, 0, :], r1, r2))
            V_(lambda e: e.tensor_tensor(r1, src[:, :, 0, :], st, ALU.mult))
            V_(lambda e: e.tensor_tensor(r2, src[:, :, 1, :], cb, ALU.mult))
            return V_(lambda e: e.tensor_add(dst[:, :, 1, :], r1, r2))

        ldq = [None, None]
        ev1 = [None, None]
        qd_ev = [None, None]
        rope_done = [None, None]
        qtr = [None, None]
        qmm = [None, None]
        for i2 in range(18):
            if i2 < 16:
                tt = i2
                sl = tt % 2
                ldq[sl] = ldg(xtm_sb[:, sl, :],
                              x_own[tt * 128:(tt + 1) * 128, :],
                              deps=[tr_last[sl]])
                for ic in range(8):
                    o = T(transpose_op(
                        bankb(6 + sl)[:, ic * 128:(ic + 1) * 128],
                        xtm_sb[:, sl, ic * 128:(ic + 1) * 128], ident_sb),
                        deps=([ldq[sl], ev_fm[sl] if tt < 2 else ev1[sl]]
                              if ic == 0 else ()))
                tr_last[sl] = o
                ev1[sl] = V_(lambda e, sl=sl: e.tensor_copy(
                    xfm_sb[:, sl, :, :],
                    bankb(6 + sl).rearrange("p (a b) -> p a b", a=8)),
                    deps=[tr_last[sl]])
            if 1 <= i2 <= 16:
                tt = i2 - 1
                sl = tt % 2
                qb = sl * 2
                for og in range(2):
                    for ic in range(8):
                        dps = ([ev1[sl], ld_wq, rope_done[sl], zx_ev[sl]]
                               if (og == 0 and ic == 0) else ())
                        o = T(lambda e, og=og, ic=ic, sl=sl, qb=qb:
                              e.matmul(bank(qb + og), xfm_sb[:, sl, ic, :],
                                       wq_sb[:, ic, og * 512:(og + 1) * 512],
                                       start=(ic == 0), stop=(ic == 7)), deps=dps)
                qmm[sl] = o
                rope_done[sl] = rope_ops(
                    bank2(qb), 16, cq_sb[:, tt, :], sq_sb[:, tt, :],
                    qtm_sb[:, sl, :],
                    deps=[qmm[sl], ld_cq, ld_sq, qtr[sl]])
            if i2 >= 2:
                tt = i2 - 2
                sl = tt % 2
                for p in range(8):
                    o = T(transpose_op(
                        bankb(4 + sl)[:, p * 128:(p + 1) * 128],
                        qtm_sb[:, sl, p * 128:(p + 1) * 128], ident_sb),
                        deps=([rope_done[sl], qd_ev[sl]] if p == 0 else ()))
                qtr[sl] = o
                qd_ev[sl] = SC(lambda e, sl=sl, tt=tt: e.copy(
                    qd_sb[:, :, tt * 128:(tt + 1) * 128],
                    bankb(4 + sl).rearrange("p (a b) -> p a b", a=8)),
                    deps=[qtr[sl]])
        q_done = qd_ev[15 % 2]
        q_done2 = qd_ev[14 % 2]

        fs = dict(it=0, exp=[], avs=[], y_ev=[None, None],
                  sstage_rd=None, bc_rd=[None, None], rs_rd=None, unit=0)
        y_final = None
        prev_flash_tail = [q_done, q_done2, rope_done[0], rope_done[1]]

        mask_rd = None
        for hg in range(NHG):
            dcol = slice(hg * 256, hg * 256 + 256)
            ropek = [None, None]
            trk = [None, None]
            evk = [None, None]
            vev = [None, None]
            kmm = [None, None]
            vmm = [None, None]

            def rope_pair(ps2, pi, out2, deps):
                src4 = ps2.rearrange("p t (h c f) -> p t h c f", h=4, c=2)
                dst4 = out2.rearrange("p t (h c f) -> p t h c f", h=4, c=2)
                cb = ck_sb[:, 2 * pi:2 * pi + 2, :].unsqueeze(2) \
                    .broadcast_to([128, 2, 4, 32])
                st = sk_sb[:, 2 * pi:2 * pi + 2, :].unsqueeze(2) \
                    .broadcast_to([128, 2, 4, 32])
                r1 = rsc1_sb[:, 0:256].rearrange("p (t h f) -> p t h f", t=2, h=4)
                r2 = rsc2_sb[:, 0:256].rearrange("p (t h f) -> p t h f", t=2, h=4)
                V_(lambda e: e.tensor_tensor(r1, src4[:, :, :, 0, :], cb,
                                             ALU.mult), deps=deps)
                V_(lambda e: e.tensor_tensor(r2, src4[:, :, :, 1, :], st,
                                             ALU.mult))
                V_(lambda e: e.tensor_sub(dst4[:, :, :, 0, :], r1, r2))
                V_(lambda e: e.tensor_tensor(r1, src4[:, :, :, 0, :], st,
                                             ALU.mult))
                V_(lambda e: e.tensor_tensor(r2, src4[:, :, :, 1, :], cb,
                                             ALU.mult))
                return V_(lambda e: e.tensor_add(dst4[:, :, :, 1, :], r1, r2))

            for pi in range(17):
                if pi < 16:
                    sl = pi % 2
                    b0 = sl * 2
                    for tl in range(2):
                        kt = 2 * pi + tl
                        dps = ([z_done, neg0, neg1, ld_ck, ld_sk, ropek[sl]] +
                               prev_flash_tail if pi == 0 else
                               ([ropek[sl]] if tl == 0 else []))
                        T(lambda e, kt=kt, b=b0 + tl, dc=dcol:
                          e.matmul(bank(b)[:, 0:256],
                                   z0_sb[:, kt * 128:(kt + 1) * 128],
                                   d0_sb[:, dc], start=True, stop=False),
                          deps=dps)
                        kmm[sl] = T(lambda e, kt=kt, b=b0 + tl, dc=dcol:
                                    e.matmul(bank(b)[:, 0:256],
                                             z1_sb[:, kt * 128:(kt + 1) * 128],
                                             d1_sb[:, dc], start=False,
                                             stop=True))
                        vdps = ([z_done, ld_vd0, ld_vd1, vev[kt % 2]] +
                                (prev_flash_tail if pi == 0 else []))
                        vmm[kt % 2] = T(lambda e, kt=kt, dc=dcol:
                                        e.matmul(bank(4 + kt % 2)[:, 0:256],
                                                 xv0_sb[:, kt * 128:(kt + 1) * 128],
                                                 vd0_sb[:, dc], start=True,
                                                 stop=False), deps=vdps)
                        vmm[kt % 2] = T(lambda e, kt=kt, dc=dcol:
                                        e.matmul(bank(4 + kt % 2)[:, 0:256],
                                                 xv1_sb[:, kt * 128:(kt + 1) * 128],
                                                 vd1_sb[:, dc], start=False,
                                                 stop=True))
                    ropek[sl] = rope_pair(
                        psum[:, b0:b0 + 2, 0:256], pi, ktm_sb[:, sl, :]
                        .rearrange("p (t c) -> p t c", t=2),
                        deps=[kmm[sl], trk[sl]])
                    for tl in range(2):
                        kt = 2 * pi + tl
                        vev[kt % 2] = SC(lambda e, kt=kt: e.copy(
                            v_sb[:, kt, :], bank(4 + kt % 2)[:, 0:256]),
                            deps=[vmm[kt % 2]])
                if pi >= 1:
                    pj = pi - 1
                    sl = pj % 2
                    for p in range(4):
                        o = T(transpose_op(
                            bankb(6 + sl)[:, p * 128:(p + 1) * 128],
                            ktm_sb[:, sl, p * 128:(p + 1) * 128], ident_sb),
                            deps=([ropek[sl], evk[sl]] +
                                  (prev_flash_tail if pj == 0 else [])
                                  if p == 0 else ()))
                    trk[sl] = o
                    evk[sl] = SC(lambda e, sl=sl, pj=pj: e.copy(
                        kd_sb[:, :, pj * 256:(pj + 1) * 256]
                        .rearrange("p r (t c) -> p r t c", t=2),
                        bankb(6 + sl)[:, 0:512]
                        .rearrange("p (t r c) -> p t r c", t=2, r=2)
                        .rearrange("p t r c -> p r t c")),
                        deps=[trk[sl]])
            kv_dep = [evk[0], evk[1], vev[0], vev[1]]

            mask_ld = None
            chunks = []
            for j in range(4):
                for pr in range(2):
                    for ci in range(CNT[j]):
                        chunks.append((j, pr, ci))
            stage = {}
            mask_lds = {}
            for idx in range(len(chunks) + 1):
                if idx < len(chunks):
                    j, pr, ci = chunks[idx]
                    cntj = CNT[j]
                    q0 = j * 512
                    pg = hg * 2 + pr
                    if pr == 0 and ci == 0:
                        mask_lds[j] = ldg(mask_sb,
                                          masksd[j].rearrange("s p q -> p s q"),
                                          deps=[mask_rd])
                    it = fs["it"]
                    scs, pts = it % 2, it % 4
                    sc_ap = bank2(scs * 2)
                    expdep = fs["exp"][it - 2] if it >= 2 else None
                    T(lambda e, sc=sc_ap, pr=pr, pg=pg, q0=q0, ci=ci:
                      e.matmul(sc[:, 0:512],
                               kd_sb[0:64, pr, ci * 128:(ci + 1) * 128],
                               qd_sb[0:64, pg, q0:q0 + 512],
                               start=True, stop=True),
                      deps=[expdep] + (kv_dep + [q_done] if idx == 0 else []))
                    mm_sc = T(lambda e, sc=sc_ap, pr=pr, pg=pg, q0=q0, ci=ci:
                              e.matmul(sc[:, 512:1024],
                                       kd_sb[64:128, pr, ci * 128:(ci + 1) * 128],
                                       qd_sb[64:128, pg, q0:q0 + 512],
                                       start=True, stop=True,
                                       tile_position=(64, 0)))
                    ptdep = fs["avs"][it - 4] if it >= 4 else None
                    exp_o = SC(lambda e, sc=sc_ap, pts=pts:
                               e.activation(pt_sb[:, pts, :], sc, AF.Exp,
                                            scale=0.125, bias=bias_sb),
                               deps=[mm_sc, ms_bias, ptdep])
                    fs["exp"].append(exp_o)
                    av_dep = exp_o
                    if ci >= cntj - NMSK:
                        s_ = ci - (cntj - NMSK)
                        mk = V_(lambda e, pts=pts, s_=s_: e.tensor_tensor(
                            pt_sb[:, pts, :].rearrange("p (a b) -> p a b", a=2),
                            pt_sb[:, pts, :].rearrange("p (a b) -> p a b", a=2),
                            mask_sb[:, s_, :].unsqueeze(1).broadcast_to(
                                [128, 2, 512]), ALU.mult),
                            deps=[exp_o, mask_lds[j]])
                        av_dep = mk
                        mask_rd = mk
                    stage[idx] = dict(av_dep=av_dep, pts=pts)
                    fs["it"] += 1
                if idx >= 1:
                    j, pr, ci = chunks[idx - 1]
                    cntj = CNT[j]
                    q0 = j * 512
                    pg = hg * 2 + pr
                    st_, sp = (ci == 0), (ci == cntj - 1)
                    up = fs["unit"] % 2
                    yb = 4 + 2 * up
                    sb = 5 + 2 * up
                    sg = stage.pop(idx - 1)
                    pts = sg["pts"]
                    ydeps = [sg["av_dep"]] + ([fs["y_ev"][up]] if st_ else [])
                    T(lambda e, pts=pts, pr=pr, yb=yb, s=st_, p=sp, ci=ci:
                      e.matmul(bank(yb)[0:64, :],
                               v_sb[:, ci, pr * 128:pr * 128 + 64],
                               pt_sb[:, pts, 0:512], start=s, stop=p),
                      deps=ydeps)
                    T(lambda e, pts=pts, pr=pr, yb=yb, s=st_, p=sp, ci=ci:
                      e.matmul(bank(yb)[64:128, :],
                               v_sb[:, ci, pr * 128 + 64:pr * 128 + 128],
                               pt_sb[:, pts, 512:1024], start=s, stop=p,
                               tile_position=(0, 64)))
                    T(lambda e, pts=pts, sb=sb, s=st_, p=sp:
                      e.matmul(bank(sb)[0:1, :], ones_sb, pt_sb[:, pts, 0:512],
                               start=s, stop=p),
                      deps=[ld_ones] + ([fs["bc_rd"][up]] if st_ else []))
                    mm_last = T(lambda e, pts=pts, sb=sb, s=st_, p=sp:
                                e.matmul(bank(sb)[32:33, :], ones_sb,
                                         pt_sb[:, pts, 512:1024],
                                         start=s, stop=p,
                                         tile_position=(0, 32)))
                    fs["avs"].append(mm_last)
                    if sp:
                        if fs["unit"] < 2:
                            V_(lambda e, sb=sb: e.tensor_copy(
                                sstage_sb[0:1, :], bank(sb)[0:1, :]),
                               deps=[mm_last, fs["sstage_rd"]])
                            cpB = V_(lambda e, sb=sb: e.tensor_copy(
                                sstage_sb[32:33, :], bank(sb)[32:33, :]))
                        else:
                            cpB = V_(lambda e, sb=sb: e.tensor_copy(
                                sstage_sb[0:33, :], bank(sb)[0:33, :]),
                                deps=[mm_last, fs["sstage_rd"]])
                        bc = T(lambda e, sb=sb: e.matmul(bank(sb), sel_sb,
                                                         sstage_sb,
                                                         start=True, stop=True),
                               deps=[cpB, ld_sel, ms_sst, fs["bc_rd"][up]])
                        fs["sstage_rd"] = bc
                        rc = V_(lambda e, sb=sb: e.reciprocal(rs_sb, bank(sb)),
                                deps=[bc, fs["rs_rd"]])
                        fs["bc_rd"][up] = rc
                        yev = V_(lambda e, pg=pg, q0=q0, yb=yb: e.tensor_tensor(
                            y_sb[:, pg, q0:q0 + 512], bank(yb), rs_sb, ALU.mult),
                            deps=[rc])
                        fs["rs_rd"] = yev
                        fs["y_ev"][up] = yev
                        y_final = yev
                        fs["unit"] += 1
            prev_flash_tail = [fs["exp"][-1], fs["avs"][-1], fs["bc_rd"][0],
                               fs["bc_rd"][1], fs["y_ev"][0], fs["y_ev"][1]]
        attn_done = y_final

        dbg_ops = []
        if dbg:
            dbg_ops.append(G(lambda e: e.dma_start(out=dbg_o["dbg_q"], in_=qd_sb),
               deps=[q_done]))
            dbg_ops.append(G(lambda e: e.dma_start(out=dbg_o["dbg_k"], in_=kd_sb),
               deps=[attn_done]))
            dbg_ops.append(G(lambda e: e.dma_start(out=dbg_o["dbg_v"], in_=v_sb),
               deps=[attn_done]))
            dbg_ops.append(G(lambda e: e.dma_start(out=dbg_o["dbg_y"], in_=y_sb),
               deps=[attn_done]))
            dbg_ops.append(G(lambda e: e.dma_start(out=dbg_o["dbg_z"], in_=z0_sb),
               deps=[z_done]))

        ld_wproj = ldg(wproj_sb, wprojd, deps=list(kv_dep) + dbg_ops)

        ln_prev = [None]

        def ln_fused(bankap, xres, xmid, xout, deps1, deps9):
            add_o = V_(lambda e: e.scalar_tensor_tensor(
                xmid, bankap, 1.0, xres, ALU.mult, ALU.add,
                accum_out=col_sb[:, 0:1]), deps=deps1)
            V_(lambda e: e.scalar_tensor_tensor(stat_sb, xmid, 1.0, xmid,
                                                ALU.mult, ALU.mult,
                                                accum_out=col_sb[:, 1:2]))
            V_(lambda e: e.tensor_scalar_mul(col_sb[:, 2:3], col_sb[:, 0:1],
                                             1.0 / D))
            V_(lambda e: e.scalar_tensor_tensor(col_sb[:, 4:5], col_sb[:, 2:3],
                                                -1.0, col_sb[:, 2:3], ALU.mult,
                                                ALU.mult))
            var = V_(lambda e: e.tensor_scalar(col_sb[:, 5:6], col_sb[:, 1:2],
                                               1.0 / D, col_sb[:, 4:5],
                                               ALU.mult, ALU.add))
            srt = SC(lambda e: e.activation(col_sb[:, 6:7], col_sb[:, 5:6],
                                            AF.Sqrt, bias=eps_sb),
                     deps=[var, ms_eps])
            rst = V_(lambda e: e.reciprocal(col_sb[:, 7:8], col_sb[:, 6:7]),
                     deps=[srt, ln_prev[0]])
            nb = V_(lambda e: e.scalar_tensor_tensor(
                col_sb[:, 3:4], col_sb[:, 2:3], -1.0, col_sb[:, 7:8],
                ALU.mult, ALU.mult), deps=[rst])
            ap = SC(lambda e: e.activation(xout, xmid, AF.Identity,
                                           scale=col_sb[:, 7:8],
                                           bias=col_sb[:, 3:4]),
                    deps=deps9 + [nb])
            ln_prev[0] = ap
            return add_o, ap

        x1ld = [None, None]
        x1cast = [None, None]
        x1tr = [None, None]
        x1ev = [None, None]
        ldw12 = [None, None]
        w12_rd = [None, None]
        ldh = [None, None, None]
        wbuf_rd = [None, None, None]
        ldw3s = [None, None]
        w3_rd = None
        w3_rd2 = [None, None]
        outw = [None, None, None]
        addx_bank = {}
        rf = [None, None, None]
        fev = [None] * 8
        x1w_all = []
        x1ld2_last = [None]

        def fm_tile(tt, q8, fsub_war, pace=None):
            sl = tt % 2
            x1ld[sl] = ldg(x1f_sb[:, sl, :],
                           x1sp[tt * 128:(tt + 1) * 128, :],
                           deps=[x1w_all[tt], rf[sl], pace])
            x1cast[sl] = SC(lambda e, sl=sl: e.copy(
                x1b_sb[:, sl, :], x1f_sb[:, sl, :]), deps=[x1ld[sl]])
            rf[sl] = x1cast[sl]
            for ic in range(8):
                o = T(transpose_op(
                    bankb(6 + sl)[:, ic * 128:(ic + 1) * 128],
                    x1b_sb[:, sl, ic * 128:(ic + 1) * 128], ident_sb),
                    deps=([x1cast[sl], x1ev[sl], w12_rd[0], w12_rd[1],
                           outw[0], outw[1], outw[2], fsub_war] +
                          ([attn_done] if tt < 2 else [])
                          if ic == 0 else ()))
            x1tr[sl] = o
            x1ev[sl] = SC(lambda e, sl=sl, q8=q8: e.copy(
                x1fm_sb[:, :, q8 * 128:(q8 + 1) * 128],
                bankb(6 + sl).rearrange("p (a b) -> p a b", a=8)),
                deps=[x1tr[sl]])

        ldo = [None, None]
        addo = [None, None]
        x1w = [None, None]
        for tt in range(16):
            sl = tt % 2
            ldo[sl] = lds(xo_sb[:, sl, :], x_own[tt * 128:(tt + 1) * 128, :],
                          deps=[addo[sl]] + ([attn_done] + dbg_ops if tt < 2 else []))
            for og in range(2):
                for ch in range(8):
                    dps = ([attn_done, ld_wproj, addo[sl]]
                           if og == 0 and ch == 0 else ())
                    o = T(lambda e, ch=ch, og=og, sl=sl, tt=tt:
                          e.matmul(bank(sl * 2 + og),
                                   y_sb[:, ch, tt * 128:(tt + 1) * 128],
                                   wproj_sb[:, ch, og * 512:(og + 1) * 512],
                                   start=(ch == 0), stop=(ch == 7)), deps=dps)
            pmm = o
            addo[sl], lt = ln_fused(
                bank2(sl * 2), xo_sb[:, sl, :], x1p_sb[:, sl, :],
                x1_sb[:, sl, :],
                deps1=[pmm, ldo[sl], x1w[sl]], deps9=[x1w[sl]])
            x1w[sl] = SY(lambda e, sl=sl, tt=tt: e.dma_start(
                out=x1sp[tt * 128:(tt + 1) * 128, :], in_=x1_sb[:, sl, :]),
                deps=[lt])
            x1w_all.append(x1w[sl])
            if dbg:
                SY(lambda e, sl=sl, tt=tt: e.dma_start(
                    out=dbg_o["dbg_x1"][tt * 128:(tt + 1) * 128, :],
                    in_=x1_sb[:, sl, :]), deps=[lt])
            if tt >= 2 and tt < 10:
                fm_tile(tt - 2, tt - 2, None)
        x1fm_done_tg0 = [x1ev[0], x1ev[1]]

        af_war = None
        fsub_rd = None
        dbg_done = False
        for tg in range(2):
            t0 = tg * 1024
            sil = [None, None, None, None]
            aev = [None, None, None, None]
            if tg == 0:
                x1fm_done = x1fm_done_tg0
            else:
                for q8 in range(8):
                    fm_tile(8 + q8, q8, fsub_rd,
                            pace=(x1ld2_last[0] if q8 < 2 else None))
                x1fm_done = [x1ev[0], x1ev[1]]
            for oc in range(32):
                wsl = oc % 2
                ldw12[wsl] = ldg(w12b_sb[:, wsl, :, :, :], w12d[oc],
                                 deps=[w12_rd[wsl], w3_rd] +
                                 ([attn_done, x1w_all[7]]
                                  if tg == 0 and oc < 2 else []) +
                                 ([x1ld2_last[0]]
                                  if tg == 1 and oc < 2 else []))
                for half in range(2):
                    pb = half * 4 + wsl * 2
                    hs = slice(half * 512, (half + 1) * 512)
                    ss = half * 2 + wsl
                    for w_ in range(2):
                        for ic in range(8):
                            dps = ([ldw12[wsl]] + x1fm_done +
                                   [sil[ss], aev[ss]] +
                                   ([addo[0], addo[1]]
                                    if tg == 0 and oc < 2 else []))                                 if (w_ == 0 and ic == 0) else ()
                            o = T(lambda e, w_=w_, ic=ic, pb=pb, wsl=wsl, hs=hs:
                                  e.matmul(bank(pb + w_),
                                           w12b_sb[:, wsl, w_, ic, :],
                                           x1fm_sb[:, ic, hs],
                                           start=(ic == 0), stop=(ic == 7)),
                                  deps=dps)
                    w12_rd[wsl] = o
                    sil[ss] = SC(lambda e, pb=pb, oc=oc, ss=ss:
                                 e.activation(silu_sb[:, ss, :], bank(pb),
                                              AF.Silu,
                                              scale=s1_sb[:, oc:oc + 1]),
                                 deps=[w12_rd[wsl], ld_s1, aev[ss]])
                    aev[ss] = V_(lambda e, pb=pb, oc=oc, ss=ss, hs=hs:
                                 e.scalar_tensor_tensor(
                                     af_sb[:, oc, hs], bank(pb + 1),
                                     s2_sb[:, oc:oc + 1], silu_sb[:, ss, :],
                                     ALU.mult, ALU.mult),
                                 deps=[sil[ss], ld_s2, af_war] +
                                 ([x1w_all[15]]
                                  if tg == 0 and oc == 16 else []))
            a_done = [aev[0], aev[1], aev[2], aev[3]]
            if dbg and tg == 0:
                G(lambda e: e.dma_start(out=dbg_o["dbg_af"], in_=af_sb),
                  deps=a_done)

            def h_layer(src_sb, dst_sb, wsrc, scale_sb, scale_ld, dep_start,
                        ld_gate=()):
                gact = [None, None, None, None]
                for oc in range(32):
                    wsl = oc % 3
                    ldh[wsl] = ldg(wbuf_sb[:, wsl, :, :], wsrc[oc],
                                   deps=[wbuf_rd[wsl]] + list(ld_gate) +
                                   ([outw[0], outw[1], outw[2]]
                                    if wsl != 1 else []))
                    for half in range(2):
                        pb = (oc * 2 + half) % 4
                        hs = slice(half * 512, (half + 1) * 512)
                        for rc in range(32):
                            dps = ([ldh[wsl], gact[pb]] + dep_start)                                 if rc == 0 else ()
                            o = T(lambda e, rc=rc, pb=pb, wsl=wsl, hs=hs:
                                  e.matmul(bank(pb), wbuf_sb[:, wsl, rc, :],
                                           src_sb[:, rc, hs],
                                           start=(rc == 0), stop=(rc == 31)),
                                  deps=dps)
                        wbuf_rd[wsl] = o
                        gact[pb] = SC(lambda e, oc=oc, pb=pb, hs=hs:
                                      e.activation(dst_sb[:, oc, hs], bank(pb),
                                                   AF.Gelu,
                                                   scale=scale_sb[:, oc:oc + 1]),
                                      deps=[wbuf_rd[wsl], scale_ld])
                return [gact[0], gact[1], gact[2], gact[3]]

            g_done = h_layer(af_sb, g_sb, h1wd, h1s_sb, ld_h1s,
                             a_done[2:],
                             ld_gate=([x1w[0], x1w[1]] if tg == 0 else ()))
            g2_done = h_layer(g_sb, af_sb, h2wd, h2s_sb, ld_h2s, g_done)

            if dbg and not dbg_done:
                G(lambda e: e.dma_start(out=dbg_o["dbg_a"], in_=g_sb),
                  deps=g_done)
                G(lambda e: e.dma_start(out=dbg_o["dbg_g"], in_=af_sb),
                  deps=g2_done)
                dbg_done = True

            for half in range(2):
                th0 = t0 + half * 512
                hs = slice(half * 512, (half + 1) * 512)
                for oc in range(8):
                    pb = oc % 2
                    wb3 = w3b_sb if oc % 2 == 0 else w3b2_sb
                    ldw3s[oc % 2] = ldg(wb3, w3wd[oc],
                                        deps=[w3_rd2[oc % 2], a_done[3],
                                              w12_rd[0], w12_rd[1]] +
                                        ([attn_done, x1w[0], x1w[1]]
                                         if tg == 0 and half == 0 and oc == 0
                                         else []))
                    for rc in range(32):
                        dps = ([ldw3s[oc % 2],
                                fev[oc - 2] if oc >= 2 else None,
                                fev[oc - 2 + 8] if oc < 2 else None] + g2_done) \
                            if rc == 0 else ()
                        o = T(lambda e, rc=rc, pb=pb, hs=hs, wb3=wb3:
                              e.matmul(bank(pb), wb3[:, rc, :],
                                       af_sb[:, rc, hs],
                                       start=(rc == 0), stop=(rc == 31)),
                              deps=dps)
                    w3_rd2[oc % 2] = o
                    w3_rd = o
                    fev[oc] = V_(lambda e, oc=oc, pb=pb: e.tensor_scalar_mul(
                        fsub_sb[:, oc, :], bank(pb), w3s_sb[:, oc:oc + 1]),
                        deps=[w3_rd, ld_w3s, fsub_rd])
                for q in range(4):
                    tt = th0 // 128 + q
                    sl = tt % 2
                    s3 = tt % 3
                    x1ld2 = ldg(x1f_sb[:, s3, :],
                                x1sp[tt * 128:(tt + 1) * 128, :],
                                deps=[rf[s3], fev[4 + q]])
                    x1ld2_last[0] = x1ld2
                    xb = 4 if q % 2 == 0 else 6
                    for oc in range(8):
                        o = T(transpose_op(
                            bank2(xb)[:, oc * 128:(oc + 1) * 128],
                            fsub_sb[:, oc, q * 128:(q + 1) * 128], identf_sb),
                            deps=([fev[7], ld_identf, addx_bank.get(xb)]
                                  if oc == 0 else ()))
                    ftr = o
                    if q == 3:
                        fsub_rd = ftr
                    x2ap = x2_sb[:, s3, :] if s3 < 2 else x2c_sb
                    addx, lt2 = ln_fused(
                        bank2(xb), x1f_sb[:, s3, :], x1f_sb[:, s3, :],
                        x2ap,
                        deps1=[ftr, x1ld2], deps9=[outw[s3]])
                    addx_bank[xb] = addx
                    rf[s3] = lt2
                    outw[s3] = SY(lambda e, ap=x2ap, tt=tt: e.dma_start(
                        out=out[tt * 128:(tt + 1) * 128, :], in_=ap),
                        deps=[lt2])
            af_war = w3_rd

        P.add("sync", lambda e: e.nop(), deps=[outw[0], outw[1], outw[2]])

        block = es.enter_context(nc.Block())
        P.emit(nc, block, es)
    return nc


def _host_prep(inputs):
    f32 = np.float32
    Wq = np.asarray(inputs["L0_attn_Wq"], f32)
    Wk = np.asarray(inputs["L0_attn_Wk"], f32)
    Wv = np.asarray(inputs["L0_attn_Wv"], f32)
    kdec = np.asarray(inputs["L0_attn_key_decoder"], f32)
    vdec = np.asarray(inputs["L0_attn_value_decoder"], f32)
    Wproj = np.asarray(inputs["L0_attn_Wproj"], f32)
    w1w = np.asarray(inputs["L0_ffn_w1_w"], f32)
    w2w = np.asarray(inputs["L0_ffn_w2_w"], f32)
    h1w = np.asarray(inputs["L0_ffn_h1_w"], f32)
    h2w = np.asarray(inputs["L0_ffn_h2_w"], f32)
    w3w = np.asarray(inputs["L0_ffn_w3_w"], f32)

    com = {}
    hh, cc, ff = np.meshgrid(np.arange(H), np.arange(2), np.arange(NF),
                             indexing="ij")
    qperm = (64 * hh + 2 * ff + cc).reshape(-1)
    com["wq"] = np.ascontiguousarray(Wq[:, qperm].reshape(8, 128, D)
                                     .transpose(1, 0, 2))
    ffk, rrk = np.meshgrid(np.arange(NF), np.arange(KR), indexing="ij")
    kperm = np.concatenate([(6 * ffk + 2 * rrk + 0).reshape(-1),
                            (6 * ffk + 2 * rrk + 1).reshape(-1)])
    com["wk"] = np.ascontiguousarray(Wk[:, kperm].reshape(8, 128, VR)
                                     .transpose(1, 0, 2))
    com["wv"] = np.ascontiguousarray(Wv.reshape(8, 128, VR).transpose(1, 0, 2))
    Dm = np.zeros((VR, D), f32)
    br, bi = kdec[..., 0], kdec[..., 1]
    fi, hi, ri = np.meshgrid(np.arange(NF), np.arange(H), np.arange(KR),
                             indexing="ij")
    for c, top, bot in ((0, br, bi), (1, bi, br)):
        Dm[3 * fi + ri, 64 * hi + 32 * c + fi] = top
        Dm[96 + 3 * fi + ri, 64 * hi + 32 * c + fi] = bot
    com["d0"] = np.ascontiguousarray(Dm[:128])
    com["d1"] = np.ascontiguousarray(Dm[128:])
    VD = np.ascontiguousarray(vdec.transpose(1, 0, 2).reshape(VR, D))
    com["vd0"] = np.ascontiguousarray(VD[:128])
    com["vd1"] = np.ascontiguousarray(VD[128:])
    com["wproj"] = np.ascontiguousarray(Wproj.reshape(8, 128, D)
                                        .transpose(1, 0, 2))
    w1s_ = w1w.reshape(32, 128, 8, 128).transpose(0, 3, 2, 1)
    w2s_ = w2w.reshape(32, 128, 8, 128).transpose(0, 3, 2, 1)
    com["w12"] = np.ascontiguousarray(np.stack([w1s_, w2s_], 2))
    com["h1w"] = np.ascontiguousarray(h1w.reshape(32, 128, 32, 128)
                                      .transpose(0, 3, 2, 1))
    com["h2w"] = np.ascontiguousarray(h2w.reshape(32, 128, 32, 128)
                                      .transpose(0, 3, 2, 1))
    com["w3w"] = np.ascontiguousarray(w3w.reshape(8, 128, 32, 128)
                                      .transpose(0, 3, 2, 1))
    com["s1s"] = np.ascontiguousarray(
        np.asarray(inputs["L0_ffn_w1_s"], f32).reshape(32, 128).T)
    com["s2s"] = np.ascontiguousarray(
        np.asarray(inputs["L0_ffn_w2_s"], f32).reshape(32, 128).T)
    com["h1s"] = np.ascontiguousarray(
        np.asarray(inputs["L0_ffn_h1_s"], f32).reshape(32, 128).T)
    com["h2s"] = np.ascontiguousarray(
        np.asarray(inputs["L0_ffn_h2_s"], f32).reshape(32, 128).T)
    com["w3s"] = np.ascontiguousarray(
        np.asarray(inputs["L0_ffn_w3_s"], f32).reshape(8, 128).T)
    com["ident"] = np.eye(128, dtype=f32)
    sel = np.zeros((33, 128), f32)
    sel[0, 0:64] = 1.0
    sel[32, 64:128] = 1.0
    com["sel33"] = sel
    com["onesb"] = np.ones((128, 1), f32)
    half = HD // 2
    freq = (1.0 / ROPE_BASE ** (np.arange(half, dtype=f32) / half)).astype(f32)
    freqs = np.outer(np.arange(S, dtype=f32), freq)
    COS = np.cos(freqs).astype(f32)
    SIN = np.sin(freqs).astype(f32)
    com["ck"] = np.ascontiguousarray(COS.reshape(32, 128, 32).transpose(1, 0, 2))
    com["sk"] = np.ascontiguousarray(SIN.reshape(32, 128, 32).transpose(1, 0, 2))
    return com, COS, SIN


def _core_inputs(inputs, com, COS, SIN, core):
    b, hlf = core // 2, core % 2
    x = np.asarray(inputs["L0_attn_x"], np.float32)
    groups = [2 * j + 1 - hlf for j in range(4)]
    rows = np.concatenate([np.arange(g * 512, (g + 1) * 512) for g in groups])
    d = dict(com)
    d["x_own"] = np.ascontiguousarray(x[b][rows])
    d["x_full"] = np.ascontiguousarray(x[b])
    d["cq"] = np.ascontiguousarray(COS[rows].reshape(16, 128, 32)
                                   .transpose(1, 0, 2))
    d["sq"] = np.ascontiguousarray(SIN[rows].reshape(16, 128, 32)
                                   .transpose(1, 0, 2))
    m = np.zeros((4, NMSK, 128, 512), np.float32)
    for j in range(4):
        q0 = groups[j] * 512
        for s_ in range(NMSK):
            k0 = (CNT[j] - NMSK + s_) * 128
            kk = k0 + np.arange(128)[:, None]
            qq = q0 + np.arange(512)[None, :]
            m[j, s_] = (kk <= qq).astype(np.float32)
    d["masks"] = m
    return d, groups


def kernel(**inputs):
    com, COS, SIN = _host_prep(inputs)
    in_maps, group_list = [], []
    for c in range(8):
        d, groups = _core_inputs(inputs, com, COS, SIN, c)
        in_maps.append(d)
        group_list.append(groups)
    nc = build_nc()
    res = run_bass_kernel_spmd(nc, in_maps, core_ids=list(range(8)))
    outa = np.empty((BSZ, S, D), np.float32)
    for c in range(8):
        b = c // 2
        shard = np.asarray(res.results[c]["out"], np.float32).reshape(TQ, D)
        for j, g in enumerate(group_list[c]):
            outa[b, g * 512:(g + 1) * 512] = shard[j * 512:(j + 1) * 512]
    return outa
